# version 07
# baseline (speedup 1.0000x reference)
import numpy as np
import ml_dtypes
import concourse.bass as bass
import concourse.mybir as mybir
import concourse.bass_utils as _bass_utils
from concourse.bass_utils import run_bass_kernel_spmd

_orig_run_command = _bass_utils.run_command


def _run_command_ldwopt(cmd, *a, **k):
    cmd = ["--enable-ldw-opt=true" if c == "--enable-ldw-opt=false" else c for c in cmd]
    return _orig_run_command(cmd, *a, **k)


_bass_utils.run_command = _run_command_ldwopt

F32 = mybir.dt.float32
BF16 = mybir.dt.bfloat16
I8 = mybir.dt.int8
F8 = mybir.dt.float8e4
ALU = mybir.AluOpType
ACT = mybir.ActivationFunctionType
AX = mybir.AxisListType
bf16 = ml_dtypes.bfloat16

D = 1024
H = 16
HD = 64
TSEQ = 2048
T = 4096
DH = 4096
NCH = 8
HCH = 32
TT = 256
NTILE = 16
TPS = 8
TTH = TT + 2
EPS = 1e-5
RMAGIC = 12582912.0


class Ctr:
    def __init__(self, nc, sems):
        self.c = {}
        self.lastw = {}
        self.eng = {"sync": nc.sync, "gpsimd": nc.gpsimd, "tensor": nc.tensor,
                    "vector": nc.vector, "scalar": nc.scalar}
        self.sems = sems

    def inc(self, inst, sem, n=1):
        inst.then_inc(self.sems[sem], n)
        self.c[sem] = self.c.get(sem, 0) + n
        return self.c[sem]

    def wait(self, ename, sem, val):
        if val is None or val <= 0:
            return
        k = (ename, sem)
        if self.lastw.get(k, 0) >= val:
            return
        self.lastw[k] = val
        self.eng[ename].wait_ge(self.sems[sem], val)


def _ap_span(ap):
    try:
        t = ap.tensor
        name = t.name
        esz = mybir.dt.size(ap.dtype)
        pairs = list(ap.ap)
        p0 = int(getattr(t, "base_partition", 0) or 0)
        np_ = int(pairs[0][1]) if pairs else 128
        off = int(ap.offset)
        ext = 0
        for (st, ct) in pairs[1:]:
            ext += abs(int(st)) * (int(ct) - 1)
        lo = off * esz
        hi = (off + ext + 1) * esz
        return (name, p0, p0 + np_, lo, hi)
    except Exception:
        return ("?", 0, 128, 0, 1 << 40)


def _overlap(a, b):
    return (a[0] == b[0] and a[1] < b[2] and b[1] < a[2]
            and a[3] < b[4] and b[3] < a[4])


class DrainWrap:
    _OPS = ("tensor_tensor", "tensor_scalar", "tensor_add", "tensor_sub", "tensor_mul",
            "tensor_max", "tensor_copy", "tensor_single_scalar", "tensor_scalar_add",
            "tensor_scalar_sub", "tensor_scalar_mul", "tensor_scalar_max", "tensor_scalar_min",
            "tensor_reduce", "reduce_max", "reduce_sum", "scalar_tensor_tensor",
            "reciprocal", "memset", "memzero", "activation", "copy", "add", "mul",
            "square", "sqrt", "sign", "select", "transpose", "bn_stats", "bn_aggr")

    def __init__(self, eng):
        self._e = eng
        self._w = []
        self._r = []

    def __getattr__(self, attr):
        f = getattr(self._e, attr)
        if attr in DrainWrap._OPS:
            def wrapped(*a, _f=f, **k):
                aps = [x for x in list(a) + list(k.values())
                       if hasattr(x, "tensor") and hasattr(x, "ap")]
                if aps:
                    wr = [_ap_span(aps[0])]
                    ao = k.get("accum_out")
                    if ao is not None and hasattr(ao, "tensor"):
                        wr.append(_ap_span(ao))
                    rd = [_ap_span(x) for x in aps[1:]]
                    haz = (any(_overlap(x, y) for x in rd + wr for y in self._w)
                           or any(_overlap(x, y) for x in wr for y in self._r))
                    if haz:
                        self._e.drain()
                        self._w = []
                        self._r = []
                    self._w.extend(wr)
                    self._r.extend(rd)
                return _f(*a, **k)
            return wrapped
        return f


class Res:
    def __init__(self, name=""):
        self.name = name
        self.ready = []
        self.users = []

    def acquire(self, C, ename):
        if isinstance(ename, str):
            ename = (ename,)
        for e in ename:
            for (s, v) in self.users:
                C.wait(e, s, v)
        self.users = []
        self.ready = []

    def use(self, C, ename):
        for (s, v) in self.ready:
            C.wait(ename, s, v)


def build_program(nc, do_c=True, nb=NTILE, do_b=True, bstop="full"):
    DI = lambda n, sh, dt=F32: nc.dram_tensor(n, sh, dt, kind="ExternalInput").ap()
    d_xf = DI("xf", [D, T])
    d_wqkvT = DI("wqkvT", [D, 3 * D])
    d_woutT = DI("woutT", [D, D])
    d_boutl = DI("d_boutl", [128, NCH])
    d_w1T = DI("w1T", [D, DH])
    d_w2T = DI("w2T", [DH, D])
    d_convw = DI("d_convw", [128, 3 * HCH])
    d_convb = DI("d_convb", [128, HCH])
    d_cos = DI("d_cosT", [128, T], BF16)
    d_sin = DI("d_sinT", [128, T], BF16)
    d_pswap = DI("d_pswap", [128, 128], BF16)
    d_ident = DI("d_ident", [128, 128], BF16)
    d_maskint = DI("d_maskint", [128, 384], BF16)
    d_maskfirst = DI("d_maskfirst", [128, 256], BF16)
    d_masklast = DI("d_masklast", [128, 256], BF16)
    d_ones1 = DI("d_ones1", [1, 128])
    d_mones1 = DI("d_mones1", [1, 128])
    d_ones1b = DI("d_ones1b", [1, 128], BF16)
    d_mones1b = DI("d_mones1b", [1, 128], BF16)
    d_onescol = DI("d_onescol", [128, 1])
    d_onescolb = DI("d_onescolb", [128, 1], BF16)
    d_identf = DI("d_identf", [128, 128])
    d_out = nc.dram_tensor("outf", [D, T], F32, kind="ExternalOutput").ap()

    import contextlib
    with contextlib.ExitStack() as stk:
        SB = lambda n, sh, dt: stk.enter_context(nc.sbuf_tensor(n, sh, dt))
        PS = lambda n, sh, dt: stk.enter_context(nc.psum_tensor(n, sh, dt))

        xtile = [SB(f"xtile{i}", [128, NCH * TTH], F32) for i in range(2)]
        outsb = [SB(f"outsb{i}", [128, NCH * TT], F32) for i in range(2)]
        mr_rows = SB("mr_rows", [1, T], BF16)
        rstd_rows = SB("rstd_rows", [1, T], BF16)
        ones1b = SB("ones1b", [1, 128], BF16)
        mones1b = SB("mones1b", [1, 128], BF16)
        ones1 = SB("ones1", [1, 128], F32)
        mones1 = SB("mones1", [1, 128], F32)
        onescol = SB("onescol", [128, 1], F32)
        onescolb = SB("onescolb", [128, 1], BF16)
        identf = SB("identf", [128, 128], F32)
        boutl = SB("boutl", [128, NCH], F32)
        convw = SB("convw", [128, 3 * HCH], F32)
        convb = SB("convb", [128, HCH], F32)
        m127 = SB("m127", [1, 2], F32)
        mw2col = SB("mw2col", [128, 1], F32)
        scw1col = SB("scw1col", [128, 1], F32)
        scw2col = SB("scw2col", [128, 1], F32)
        ident = SB("ident", [128, 128], BF16)
        pswap = SB("pswap", [128, 128], BF16)
        maskint = SB("maskint", [128, 384], BF16)
        maskfirst = maskint[:, 128:384]
        masklast = maskint[:, 0:256]
        s2f2 = SB("s2f2", [128, 2], F32)
        fmx32 = SB("fmx32", [128, TTH], F32)
        fmn32 = SB("fmn32", [128, TTH], F32)
        gsc = SB("gsc", [128, 2 * TT], BF16)
        reccol2 = SB("reccol2", [128, 1], F32)
        f2row = SB("f2row", [1, 128], F32)
        s2row = SB("s2row", [1, 128], F32)
        sxrow = SB("sxrow", [1, TTH], F32)
        reccol = SB("reccol", [128, 1], F32)
        rowbuf = SB("rowbuf", [1, 8 * TTH], F32)
        s2b = SB("s2b", [128, TT], F32)
        f2b = SB("f2b", [128, TT], BF16)

        WA = SB("warena", [128, 81920], I8)
        w1s = WA[:, 0:NCH * DH].bitcast(F8)
        w2s = WA[:, NCH * DH:NCH * DH + HCH * D].bitcast(F8)
        wqkv = WA[:, 0:2 * NCH * 3 * D].bitcast(BF16)
        woutw = WA[:, 49152:49152 + 2 * NCH * D].bitcast(BF16)
        cosT = WA[:, 65536:65536 + 2 * T].bitcast(BF16)
        sinT = WA[:, 73728:73728 + 2 * T].bitcast(BF16)
        xs2 = WA[:, 65536:65536 + NCH * TTH * 2].bitcast(BF16)
        xtile3 = WA[:, 65536 + NCH * TTH * 2:65536 + NCH * TTH * 2 + NCH * TTH * 4].bitcast(F32)

        ARENA_SZ = 61824
        AR = SB("arena", [128, ARENA_SZ], I8)

        def carve(off, n, dt):
            size = {F32: 4, BF16: 2, I8: 1, F8: 1}[dt]
            off = (off + 3) // 4 * 4
            ap = AR[:, off:off + n * size]
            return (ap.bitcast(dt) if dt != I8 else ap), off + n * size

        o = 0
        xs, o = carve(o, NCH * TTH, BF16)
        xc_ctr, o = carve(o, NCH * TTH, F32)
        h1_l = []
        for i in range(2):
            a, o = carve(o, HCH * TTH, BF16)
            h1_l.append(a)
        h2, o = carve(o, HCH * TT, BF16)
        assert o <= ARENA_SZ, o
        wtmp = [AR[:, 0:16384].bitcast(F32), AR[:, 16384:32768].bitcast(F32)]
        wq8 = AR[:, 32768:36864]
        wabs_l = [AR[:, 36864:45056].bitcast(BF16), AR[:, 45056 - 8192 + 8192:53248].bitcast(BF16)]
        o = 0
        qsb_l = []
        for i in range(2):
            a, o = carve(o, NCH * TT, BF16)
            qsb_l.append(a)
        kblk_l = []
        for i in range(4):
            a, o = carve(o, NCH * 128, BF16)
            kblk_l.append(a)
        vblk_l = []
        for i in range(4):
            a, o = carve(o, H * (HD + 1), BF16)
            vblk_l.append(a)
        attnT_l = []
        for i in range(2):
            a, o = carve(o, NCH * TT, BF16)
            attnT_l.append(a)
        qraw_l = []
        for i in range(2):
            a, o = carve(o, TT, BF16)
            qraw_l.append(a)
        tr1, o = carve(o, TT, BF16)
        tr2, o = carve(o, TT, BF16)
        xnsb, o = carve(o, NCH * TT, BF16)
        expsb, o = carve(o, 384, BF16)
        atok, o = carve(o, H * HD, BF16)
        assert o <= ARENA_SZ, o

        pb = [PS(f"pb{i}", [128, 512], F32) for i in range(8)]
        pbT = [pb[i][:, :].bitcast(BF16) for i in range(8)]
        rpb = [Res(f"pb{i}") for i in range(8)]

        sems = {}
        for s in ("L", "LX0", "LX1", "LX2", "LW0", "LW1", "T", "V", "S", "G", "ST0", "ST1"):
            sems[s] = stk.enter_context(nc.semaphore(s))

        with nc.Block() as _block:
            C = Ctr(nc, sems)
            Tn, Sy = nc.tensor, nc.sync
            G = DrainWrap(nc.gpsimd)
            V = DrainWrap(nc.vector)
            S = DrainWrap(nc.scalar)

            r_xtile = [Res("xt0"), Res("xt1")]
            r_wabs = [Res("wa0"), Res("wa1")]
            r_outsb = [Res("ou0"), Res("ou1")]
            r_wtmp = [Res("wt0"), Res("wt1")]
            r_qsb = [Res("q0"), Res("q1")]
            r_kblk = [Res(f"k{i}") for i in range(4)]
            r_vblk = [Res(f"v{i}") for i in range(4)]
            r_attnT = [Res("aT0"), Res("aT1")]
            r_qraw = [Res("qr0"), Res("qr1")]
            r_atok = Res("atok")
            r_expsb = Res("expsb")
            r_xs = Res("xs")
            r_xc = Res("xc")
            r_s2b = Res("s2b")
            r_gsc = Res("gsc")
            r_h1b = [Res("h1a"), Res("h1b")]
            r_h2 = Res("h2")
            r_fold = Res("fold")
            r_xnsb = Res("xnsb")

            def dma_in(out_ap, in_ap, sem="L"):
                return C.inc(Sy.dma_start(out=out_ap, in_=in_ap), sem, 16)

            for (dst, src) in [(ones1, d_ones1), (mones1, d_mones1), (onescol, d_onescol),
                               (boutl, d_boutl), (convw, d_convw), (convb, d_convb),
                               (ident, d_ident), (identf, d_identf), (onescolb, d_onescolb),
                               (ones1b, d_ones1b), (mones1b, d_mones1b),
                               (pswap, d_pswap), (maskint, d_maskint)]:
                l_consts = dma_in(dst[:, :], src[:, :])

            C.wait("tensor", "L", l_consts)
            first = True
            tsum = 0
            for wi, (d_w, nch) in enumerate(((d_w1T, NCH), (d_w2T, HCH))):
                wd = d_w.shape[1]
                for i in range(nch):
                    bi = i % 2
                    r_wtmp[bi].acquire(C, "sync")
                    lv = dma_in(wtmp[bi][:, 0:wd], d_w[i * 128:(i + 1) * 128, :], f"LW{bi}")
                    C.wait("scalar", f"LW{bi}", lv)
                    r_wabs[bi].acquire(C, "scalar")
                    sv = C.inc(S.activation(wabs_l[bi][:, 0:wd], wtmp[bi][:, 0:wd], ACT.Abs), "S")
                    r_wtmp[bi].users.append(("S", sv))
                    C.wait("tensor", "S", sv)
                    for ss in range(wd // 256):
                        last = (i == nch - 1) and (ss == wd // 256 - 1)
                        tsum = C.inc(Tn.matmul(pb[6][0:1, wi * 256:wi * 256 + 256],
                                               onescolb[:, :], wabs_l[bi][:, ss * 256:(ss + 1) * 256],
                                               start=first, stop=last), "T")
                        first = False
                    r_wabs[bi].users.append(("T", tsum))
                first = True
            C.wait("vector", "T", tsum)
            V.reduce_sum(rowbuf[0:1, 0:1], pb[6][0:1, 0:256], AX.X)
            V.reduce_sum(rowbuf[0:1, 1:2], pb[6][0:1, 256:512], AX.X)
            V.tensor_scalar(rowbuf[0:1, 2:4], rowbuf[0:1, 0:2], 1.0 / (D * DH), 1e-5,
                            op0=ALU.mult, op1=ALU.max)
            V.tensor_scalar(rowbuf[0:1, 4:6], rowbuf[0:1, 2:4], 1.0 / 127.0, None, op0=ALU.mult)
            vv = C.inc(V.tensor_copy(m127[:, :], rowbuf[0:1, 4:6]), "V")
            C.wait("tensor", "V", vv)
            tv = C.inc(Tn.matmul(pb[7][:, 0:4], ones1[:, :], rowbuf[0:1, 2:6], start=True, stop=True), "T")
            C.wait("vector", "T", tv)
            V.reciprocal(scw1col[:, :], pb[7][:, 0:1])
            V.reciprocal(scw2col[:, :], pb[7][:, 1:2])
            v_scales = C.inc(V.tensor_copy(mw2col[:, :], pb[7][:, 3:4]), "V")
            rpb[6].users.append(("V", v_scales))
            rpb[7].users.append(("V", v_scales))

            vq = 0
            for (d_w, nch, dst, sc) in ((d_w1T, NCH, w1s, scw1col), (d_w2T, HCH, w2s, scw2col)):
                wd = d_w.shape[1]
                for i in range(nch):
                    bi = i % 2
                    r_wtmp[bi].acquire(C, "sync")
                    lv = dma_in(wtmp[bi][:, 0:wd], d_w[i * 128:(i + 1) * 128, :], f"LW{bi}")
                    C.wait("vector", f"LW{bi}", lv)
                    C.wait("vector", "V", v_scales)
                    V.tensor_scalar(wtmp[bi][:, 0:wd], wtmp[bi][:, 0:wd], sc[:, :], 1.49,
                                    op0=ALU.mult, op1=ALU.min)
                    V.tensor_scalar(wtmp[bi][:, 0:wd], wtmp[bi][:, 0:wd], -1.49, RMAGIC,
                                    op0=ALU.max, op1=ALU.add)
                    vq = C.inc(V.tensor_scalar(dst[:, i * wd:(i + 1) * wd], wtmp[bi][:, 0:wd],
                                               -RMAGIC, None, op0=ALU.add), "V")
                    r_wtmp[bi].users.append(("V", vq))
            v_signs = vq

            x3 = [xtile[i][:, :].rearrange("p (c w) -> p c w", c=NCH) for i in range(2)]
            xc3 = xc_ctr.rearrange("p (c w) -> p c w", c=NCH)
            xs3 = xs.rearrange("p (c w) -> p c w", c=NCH)
            RS = lambda k: rowbuf[0:1, k * TTH:(k + 1) * TTH]

            v_rows = 0
            xtile_b = [xtile[0][:, :], xtile[1][:, :], xtile3]
            r_xtb = [r_xtile[0], r_xtile[1], Res("xt3")]
            x3b = [a.rearrange("p (c w) -> p c w", c=NCH) for a in xtile_b]
            xs_l = [xs, xs2]
            xsl3 = [a.rearrange("p (c w) -> p c w", c=NCH) for a in xs_l]
            r_xsl = [r_xs, Res("xs2")]
            for i in range(3):
                vz0 = C.inc(V.memset(xtile_b[i], 0.0), "V")
                r_xtb[i].users.append(("V", vz0))
            bst = {}

            def b_p1(it):
                t0 = it * TT
                s0 = (it // TPS) * TSEQ
                sL = (t0 - 1) >= s0
                sR = (t0 + TT + 1) <= s0 + TSEQ
                g0 = t0 - 1 if sL else t0
                g1 = t0 + TT + 1 if sR else t0 + TT
                W = g1 - g0
                a0 = 0 if sL else 1
                xb = xtile_b[it % 3]
                xc = x3b[it % 3]
                rx = r_xtb[it % 3]
                xs_ = xs_l[it % 2]
                xs3_ = xsl3[it % 2]
                rxs_ = r_xsl[it % 2]

                rx.acquire(C, "sync")
                lxs = f"LX{it % 3}"
                lx = dma_in(
                    xb.rearrange("p (c w) -> p c w", c=NCH)[:, :, a0:a0 + W],
                    d_xf[:, g0:g1].rearrange("(c p) w -> p c w", p=128), lxs)

                rpb[6].acquire(C, "tensor")
                rpb[7].acquire(C, "tensor")
                C.wait("tensor", lxs, lx)
                for c in range(NCH):
                    C.inc(Tn.matmul(pb[6][0:1, 0:TTH], onescol[:, :], xc[:, c, 0:TTH],
                                    start=(c == 0), stop=(c == NCH - 1)), "T")
                C.wait("scalar", lxs, lx)
                r_xc.acquire(C, "scalar")
                tq_slot = [0, 0]
                for c in range(NCH):
                    C.wait("scalar", "T", tq_slot[c % 2])
                    sv = C.inc(S.activation(xc3[:, c % 2, 0:TTH],
                                            xc[:, c, 0:TTH], ACT.Square), "S")
                    C.wait("tensor", "S", sv)
                    tq = C.inc(Tn.matmul(pb[7][0:1, 0:TTH], onescol[:, :],
                                         xc3[:, c % 2, 0:TTH],
                                         start=(c == 0), stop=(c == NCH - 1)), "T")
                    tq_slot[c % 2] = tq

                C.wait("scalar", "T", tq)
                sv_mean = C.inc(S.activation(RS(0), pb[6][0:1, 0:TTH], ACT.Copy, scale=1.0 / D), "S")
                rpb[6].users.append(("S", sv_mean))
                sv = C.inc(S.activation(RS(1), pb[7][0:1, 0:TTH], ACT.Copy, scale=1.0 / D), "S")
                rpb[7].users.append(("S", sv))
                sv2 = C.inc(S.activation(RS(2), RS(0), ACT.Square), "S")
                C.wait("vector", "S", sv2)
                vv = C.inc(V.tensor_sub(RS(1), RS(1), RS(2)), "V")
                vv = C.inc(V.tensor_scalar(RS(1), RS(1), EPS, None, op0=ALU.add), "V")
                C.wait("scalar", "V", vv)
                sv = C.inc(S.activation(RS(2), RS(1), ACT.Ln), "S")
                sv_rstd = C.inc(S.activation(RS(3), RS(2), ACT.Exp, scale=-0.5), "S")
                C.wait("vector", "S", sv_rstd)
                v_mr = C.inc(V.tensor_mul(RS(2), RS(0), RS(3)), "V")

                rpb[6].acquire(C, "tensor")
                C.wait("tensor", "S", sv_mean)
                t_meanB = C.inc(Tn.matmul(pb[6][:, 0:TTH], ones1[:, :], RS(0), start=True, stop=True), "T")
                C.wait("vector", "T", t_meanB)
                for c in range(NCH):
                    C.inc(V.tensor_tensor(xc3[:, c, 0:TTH], xc[:, c, 0:TTH], pb[6][:, 0:TTH],
                                          op=ALU.subtract), "V")
                    if c == 0:
                        V.tensor_copy(fmx32[:, 0:TTH], xc3[:, c, 0:TTH])
                        vv = C.inc(V.tensor_copy(fmn32[:, 0:TTH], xc3[:, c, 0:TTH]), "V")
                    else:
                        V.tensor_tensor(fmx32[:, 0:TTH], fmx32[:, 0:TTH], xc3[:, c, 0:TTH],
                                        op=ALU.max)
                        vv = C.inc(V.tensor_tensor(fmn32[:, 0:TTH], fmn32[:, 0:TTH], xc3[:, c, 0:TTH],
                                                   op=ALU.min), "V")
                V.tensor_scalar(fmn32[:, 0:TTH], fmn32[:, 0:TTH], -1.0, None, op0=ALU.mult)
                v_fold = C.inc(V.tensor_tensor(fmx32[:, 0:TTH], fmx32[:, 0:TTH], fmn32[:, 0:TTH],
                                               op=ALU.max), "V")
                rpb[6].users.append(("V", v_fold))

                rpb[7].acquire(C, "tensor")
                C.wait("tensor", "V", v_fold)
                cuts = [0, 128, TTH - 128]
                ttr = [C.inc(Tn.transpose(pb[7][:, j * 128:(j + 1) * 128], fmx32[:, cu:cu + 128],
                                          identf[:, :]), "T") for j, cu in enumerate(cuts)]
                for j, cu in enumerate(cuts):
                    C.wait("vector", "T", ttr[-1])
                    vv = C.inc(V.reduce_max(reccol[:, :], pb[7][:, j * 128:(j + 1) * 128], AX.X), "V")
                    C.wait("tensor", "V", vv)
                    tv = C.inc(Tn.matmul(pb[6][0:1, 300:428], reccol[:, :], identf[:, :],
                                         start=True, stop=True), "T")
                    C.wait("vector", "T", tv)
                    vv = C.inc(V.tensor_copy(sxrow[0:1, cu:cu + 128], pb[6][0:1, 300:428]), "V")
                rpb[6].users.append(("V", vv))
                rpb[7].users.append(("V", vv))

                vv = C.inc(V.tensor_mul(RS(4), sxrow[0:1, 0:TTH], RS(3)), "V")
                vv = C.inc(V.tensor_scalar(RS(4), RS(4), 1e-5, None, op0=ALU.max), "V")
                vv = C.inc(V.reciprocal(RS(5), RS(4)), "V")
                vv = C.inc(V.tensor_scalar(RS(5), RS(5), 127.0, None, op0=ALU.mult), "V")
                vv_a = C.inc(V.tensor_mul(RS(6), RS(3), RS(5)), "V")
                vv_f1 = C.inc(V.tensor_scalar(RS(7), RS(4), m127[0:1, 0:1], None, op0=ALU.mult), "V")
                C.inc(V.tensor_copy(rstd_rows[0:1, t0:t0 + TT], RS(3)[0:1, 1:257]), "V")
                vr = C.inc(V.tensor_copy(mr_rows[0:1, t0:t0 + TT], RS(2)[0:1, 1:257]), "V")

                rpb[7].acquire(C, "tensor")
                C.wait("tensor", "V", vv_a)
                t_aB = C.inc(Tn.matmul(pb[7][:, 0:TTH], ones1[:, :], RS(6), start=True, stop=True), "T")
                rpb[6].acquire(C, "tensor")
                C.wait("tensor", "V", vv_f1)
                t_f1B = C.inc(Tn.matmul(pb[6][:, 0:TTH], ones1[:, :], RS(7), start=True, stop=True), "T")
                C.wait("vector", "T", t_aB)
                rxs_.acquire(C, "vector")
                V.tensor_tensor(
                    xc3[:, :, :], xc3[:, :, :],
                    pb[7][:, 0:TTH].unsqueeze(1).broadcast_to((128, NCH, TTH)), op=ALU.mult)
                vv = C.inc(V.tensor_scalar(xs3_[:, :, :], xc3[:, :, :], RMAGIC, -RMAGIC,
                                           op0=ALU.add, op1=ALU.add), "V")
                rpb[7].users.append(("V", vv))
                r_xc.users.append(("V", vv))
                C.wait("vector", "T", t_f1B)
                v_xs = C.inc(V.tensor_tensor(
                    xs3_[:, :, :], xs3_[:, :, :],
                    pb[6][:, 0:TTH].unsqueeze(1).broadcast_to((128, NCH, TTH)), op=ALU.mult), "V")
                rpb[6].users.append(("V", v_xs))
                if not sL:
                    v_xs = C.inc(V.memset(xs3_[:, :, 0:1], 0.0), "V")
                if not sR:
                    v_xs = C.inc(V.memset(xs3_[:, :, TTH - 1:TTH], 0.0), "V")
                bst[it] = {"v_xs": v_xs, "v_rows": vr}

            def b_p2(it):
                st_ = bst[it]
                xs_ = xs_l[it % 2]
                rxs_ = r_xsl[it % 2]
                C.wait("tensor", "V", st_["v_xs"])
                C.wait("tensor", "V", v_signs)
                h1b = h1_l[it % 2]
                rh = r_h1b[it % 2]
                rh.acquire(C, ("vector", "scalar"))
                h1ev = []
                for cc in range(HCH):
                    bk = 2 + cc % 4
                    rpb[bk].acquire(C, "tensor")
                    for dc in range(NCH):
                        tm = C.inc(Tn.matmul(pb[bk][:, 0:TTH],
                                             w1s[:, dc * DH + cc * 128: dc * DH + cc * 128 + 128],
                                             xs_[:, dc * TTH:(dc + 1) * TTH],
                                             start=(dc == 0), stop=(dc == NCH - 1)), "T")
                    if cc % 2 == 0:
                        C.wait("vector", "T", tm)
                        se = C.inc(V.tensor_copy(h1b[:, cc * TTH:(cc + 1) * TTH], pb[bk][:, 0:TTH]), "V")
                        rpb[bk].users.append(("V", se))
                        h1ev.append(("V", se))
                    else:
                        C.wait("scalar", "T", tm)
                        se = C.inc(S.copy(h1b[:, cc * TTH:(cc + 1) * TTH], pb[bk][:, 0:TTH]), "S")
                        rpb[bk].users.append(("S", se))
                        h1ev.append(("S", se))
                rxs_.users.append(("T", tm))
                st_["h1ev"] = h1ev

            def b_p3(it):
                st_ = bst[it]
                h1ev = st_["h1ev"]
                h1b = h1_l[it % 2]
                rh = r_h1b[it % 2]
                r_h2.acquire(C, ("vector", "scalar", "gpsimd"))
                r_gsc.acquire(C, "gpsimd")
                sI = {}
                vA = {}
                gB = {}
                silu_done = [None] * HCH
                for it2 in range(HCH + 7):
                    cc = it2
                    if cc < HCH:
                        C.wait("scalar", h1ev[cc][0], h1ev[cc][1])
                        z = h2[:, cc * TT:(cc + 1) * TT]
                        sI[cc] = C.inc(S.activation(z, h1b[:, cc * TTH + 0: cc * TTH + TT],
                                                    ACT.Identity, scale=convw[:, cc:cc + 1],
                                                    bias=convb[:, cc:cc + 1]), "S")
                    c1 = it2 - 4
                    if 0 <= c1 < HCH:
                        C.wait("vector", "S", sI[c1])
                        C.wait("vector", "G", gB[c1])
                        C.wait("vector", h1ev[c1][0], h1ev[c1][1])
                        z = h2[:, c1 * TT:(c1 + 1) * TT]
                        V.scalar_tensor_tensor(z, h1b[:, c1 * TTH + 1: c1 * TTH + 1 + TT],
                                               convw[:, 32 + c1:32 + c1 + 1], z,
                                               op0=ALU.mult, op1=ALU.add)
                        g = gsc[:, (c1 % 2) * TT:(c1 % 2) * TT + TT]
                        vA[c1] = C.inc(V.tensor_tensor(z, z, g, op=ALU.add), "V")
                    c2 = it2 - 2
                    if 0 <= c2 < HCH:
                        g = gsc[:, (c2 % 2) * TT:(c2 % 2) * TT + TT]
                        C.wait("gpsimd", h1ev[c2][0], h1ev[c2][1])
                        if c2 - 2 in vA:
                            C.wait("gpsimd", "V", vA[c2 - 2])
                        gB[c2] = C.inc(G.tensor_tensor(
                            g, h1b[:, c2 * TTH + 2: c2 * TTH + 2 + TT],
                            convw[:, 64 + c2:64 + c2 + 1].broadcast_to((128, TT)),
                            op=ALU.mult), "G")
                    c3 = it2 - 7
                    if 0 <= c3 < HCH and c3 % 4 == 3:
                        C.wait("scalar", "V", vA[c3])
                        zb = h2[:, (c3 - 3) * TT:(c3 + 1) * TT]
                        sz = C.inc(S.activation(zb, zb, ACT.Silu), "S")
                        for c4 in range(c3 - 3, c3 + 1):
                            silu_done[c4] = sz
                rh.users.append(("V", vA[HCH - 1]))
                rh.users.append(("S", sI[HCH - 1]))
                r_gsc.users.append(("V", vA[HCH - 2]))
                r_gsc.users.append(("V", vA[HCH - 1]))

                fx3 = h1b[:, 0:16 * TT].rearrange("p (c t) -> p c t", c=16)
                fn3 = h1b[:, 16 * TT:32 * TT].rearrange("p (c t) -> p c t", c=16)
                h23v = h2.rearrange("p (c t) -> p c t", c=HCH)
                C.wait("vector", "S", silu_done[HCH - 1])
                V.tensor_tensor(fx3[:, :, :], h23v[:, 0:16, :], h23v[:, 16:32, :], op=ALU.max)
                V.tensor_tensor(fn3[:, :, :], h23v[:, 0:16, :], h23v[:, 16:32, :], op=ALU.min)
                for lv in (8, 4, 2, 1):
                    V.tensor_tensor(fx3[:, 0:lv, :], fx3[:, 0:lv, :], fx3[:, lv:2 * lv, :], op=ALU.max)
                    V.tensor_tensor(fn3[:, 0:lv, :], fn3[:, 0:lv, :], fn3[:, lv:2 * lv, :], op=ALU.min)
                V.tensor_scalar(fn3[:, 0, :], fn3[:, 0, :], -1.0, None, op0=ALU.mult)
                vf = C.inc(V.tensor_tensor(fx3[:, 0, :], fx3[:, 0, :], fn3[:, 0, :],
                                           op=ALU.max), "V")

                rpb[0].acquire(C, "tensor")
                C.wait("tensor", "V", vf)
                tt1 = C.inc(Tn.transpose(pbT[0][:, 0:128], h1b[:, 0:128], ident[:, :]), "T")
                tt2 = C.inc(Tn.transpose(pbT[0][:, 128:256], h1b[:, 128:256], ident[:, :]), "T")
                rh.users.append(("T", tt2))
                rh.users.append(("V", vf))
                svals = []
                for j in range(2):
                    C.wait("vector", "T", tt2)
                    vv = C.inc(V.reduce_max(reccol2[:, :], pbT[0][:, j * 128:(j + 1) * 128], AX.X), "V")
                    vv = C.inc(V.tensor_scalar(s2f2[:, 0:1], reccol2[:, :], 1e-5, None, op0=ALU.max), "V")
                    vv = C.inc(V.reciprocal(s2f2[:, 1:2], s2f2[:, 0:1]), "V")
                    V.tensor_scalar(s2f2[:, 1:2], s2f2[:, 1:2], 127.0, None, op0=ALU.mult)
                    vv = C.inc(V.tensor_scalar(s2f2[:, 0:1], s2f2[:, 0:1], mw2col[:, :], None,
                                               op0=ALU.mult), "V")
                    C.wait("tensor", "V", vv)
                    tv = C.inc(Tn.matmul(pb[0][0:1, 300:428], s2f2[:, 0:1], identf[:, :],
                                         start=True, stop=True), "T")
                    tv = C.inc(Tn.matmul(pb[0][0:1, 172:300], s2f2[:, 1:2], identf[:, :],
                                         start=True, stop=True), "T")
                    C.wait("vector", "T", tv)
                    vv = C.inc(V.tensor_copy(f2row[0:1, 0:128], pb[0][0:1, 300:428]), "V")
                    vv = C.inc(V.tensor_copy(s2row[0:1, 0:128], pb[0][0:1, 172:300]), "V")
                    C.wait("tensor", "V", vv)
                    if j == 0:
                        rpb[1].acquire(C, "tensor")
                    else:
                        C.wait("tensor", "S", svals[-1])
                    tb = C.inc(Tn.matmul(pb[1][:, j * 128:(j + 1) * 128],
                                         ones1[:, :], s2row[0:1, 0:128], start=True, stop=True), "T")
                    tb2 = C.inc(Tn.matmul(pb[1][:, 256 + j * 128:256 + (j + 1) * 128],
                                          ones1[:, :], f2row[0:1, 0:128], start=True, stop=True), "T")
                    C.wait("scalar", "T", tb2)
                    if j == 0:
                        r_s2b.acquire(C, "scalar")
                    svals.append(C.inc(S.copy(s2b[:, j * 128:(j + 1) * 128],
                                              pb[1][:, j * 128:(j + 1) * 128]), "S"))
                    svals.append(C.inc(S.copy(f2b[:, j * 128:(j + 1) * 128],
                                              pb[1][:, 256 + j * 128:256 + (j + 1) * 128]), "S"))
                rpb[0].users.append(("V", vv))
                rpb[1].users.append(("S", svals[-1]))

                C.wait("vector", "S", svals[-1])
                h23 = h2.rearrange("p (c t) -> p c t", c=HCH)
                V.tensor_tensor(h23[:, :, :], h23[:, :, :],
                                s2b[:, :].unsqueeze(1).broadcast_to((128, HCH, TT)),
                                op=ALU.mult)
                v_hq = C.inc(V.tensor_scalar(h23[:, :, :], h23[:, :, :], RMAGIC, -RMAGIC,
                                             op0=ALU.add, op1=ALU.add), "V")
                st_["v_hq"] = v_hq
                r_h2.users.append(("V", v_hq))
                rh.users.append(("T", None))
                rh.users.pop()

            def b_p45(it):
                st_ = bst.pop(it)
                t0 = it * TT
                xc = x3b[it % 3]
                rx = r_xtb[it % 3]
                C.wait("tensor", "V", st_["v_hq"])
                ob = outsb[it % 2]
                ro = r_outsb[it % 2]
                ro.acquire(C, "vector")
                for oc in (0, 2, 4, 6, 1, 3, 5, 7):
                    bk = oc // 2
                    half = (oc % 2) * 256
                    rpb[bk].acquire(C, "tensor")
                    for cc in range(HCH):
                        tm = C.inc(Tn.matmul(pb[bk][:, half:half + TT],
                                             w2s[:, cc * D + oc * 128: cc * D + oc * 128 + 128],
                                             h2[:, cc * TT:(cc + 1) * TT],
                                             start=(cc == 0), stop=(cc == HCH - 1)), "T")
                    C.wait("vector", "T", tm)
                    dst = ob[:, oc * TT:(oc + 1) * TT]
                    V.tensor_tensor(dst, pb[bk][:, half:half + TT], f2b[:, :], op=ALU.mult)
                    ve = C.inc(V.scalar_tensor_tensor(dst, xc[:, oc, 1:257], 1.0, dst,
                                                      op0=ALU.mult, op1=ALU.add), "V")
                    rpb[bk].users.append(("V", ve))
                    C.wait("scalar", "V", ve)
                    ve2 = C.inc(S.activation(dst, dst, ACT.Identity, scale=1.0,
                                             bias=boutl[:, oc:oc + 1]), "S")
                    b_p45.sevs.append(("S", ve2))
                rx.users.append(("V", ve))
                r_s2b.users.append(("V", ve))
                r_h2.users.append(("T", tm))
                for u in b_p45.sevs:
                    C.wait("gpsimd", u[0], u[1])
                b_p45.sevs = []
                stv = C.inc(G.dma_start(out=d_out[:, t0:t0 + TT].rearrange("(c p) w -> p c w", p=128),
                                        in_=ob[:, :].rearrange("p (c w) -> p c w", c=NCH)), f"ST{it % 2}", 16)
                ro.users.append((f"ST{it % 2}", stv))
                return vr if False else st_["v_rows"]

            b_p45.sevs = []
            nb_ = nb if do_b else 0
            if nb_ > 0:
                b_p1(0)
                b_p2(0)
                b_p1(1)
                for it in range(nb_):
                    if it + 2 < nb_:
                        b_p1(it + 2)
                    if it + 1 < nb_:
                        b_p2(it + 1)
                    b_p3(it)
                    v_rows = b_p45(it)
            b_rows_done = v_rows
            if bstop != "full":
                C.inc(G.dma_start(out=d_out[:, 0:TT].rearrange("(c p) w -> p c w", p=128),
                                  in_=outsb[0][:, :].rearrange("p (c w) -> p c w", c=NCH)), "ST0", 16)
                C.wait("gpsimd", "ST0", C.c.get("ST0", 0))
                return
            if not do_c:
                if not do_b:
                    C.wait("gpsimd", "V", v_signs)
                    C.inc(G.dma_start(out=d_out[:, 0:TT].rearrange("(c p) w -> p c w", p=128),
                                      in_=outsb[0][:, :].rearrange("p (c w) -> p c w", c=NCH)), "ST0", 16)
                C.wait("gpsimd", "ST0", C.c.get("ST0", 0))
                C.wait("gpsimd", "ST1", C.c.get("ST1", 0))
                return

            last_ffn_T = C.c["T"]
            C.wait("sync", "T", last_ffn_T)
            C.wait("sync", "ST0", C.c.get("ST0", 0))
            C.wait("sync", "ST1", C.c.get("ST1", 0))
            l_tab = dma_in(cosT, d_cos[:, :])
            l_tab = dma_in(sinT, d_sin[:, :])
            for (d_w, nch, dst, wd) in ((d_wqkvT, NCH, wqkv, 3 * D), (d_woutT, NCH, woutw, D)):
                for i in range(nch):
                    bi = i % 2
                    r_wtmp[bi].acquire(C, "sync")
                    lv = dma_in(wtmp[bi][:, 0:wd], d_w[i * 128:(i + 1) * 128, :], f"LW{bi}")
                    C.wait("scalar", f"LW{bi}", lv)
                    wl = C.inc(S.copy(dst[:, i * wd:(i + 1) * wd], wtmp[bi][:, 0:wd]), "S")
                    r_wtmp[bi].users.append(("S", wl))
            s_wmm = wl

            C.wait("vector", "S", s_wmm)
            C.wait("vector", "L", l_tab)
            for b in range(4):
                vb3 = vblk_l[b].rearrange("p (h d) -> p h d", h=H)
                vinit = C.inc(V.memset(vb3[:, :, HD:HD + 1], 1.0), "V")

            qsb3 = [qsb_l[i].rearrange("p (c t) -> p c t", c=NCH) for i in range(2)]
            xn3 = xnsb.rearrange("p (c t) -> p c t", c=NCH)
            state = {"rot": 0, "trot": 0}

            def attn_block(gb):
                bl = gb % 16
                jq = gb // 2
                u = gb % 2
                if bl == 0:
                    deltas, mask = [0, 1], maskfirst
                elif bl == 15:
                    deltas, mask = [-1, 0], masklast
                else:
                    deltas, mask = [-1, 0, 1], maskint
                nck = len(deltas)
                va = None
                for h in range(H):
                    ch = h // 2
                    ro_ = (h % 2) * 64
                    sb_ = 0 + (state["rot"] % 2)
                    pv_ = 2 + (state["rot"] % 2)
                    state["rot"] += 1
                    rpb[sb_].acquire(C, "tensor")
                    for ci, dl in enumerate(deltas):
                        kb = (gb + dl) % 4
                        r_kblk[kb].use(C, "tensor")
                        r_qsb[jq % 2].use(C, "tensor")
                        tsc = C.inc(Tn.matmul(
                            pb[sb_][:, ci * 128:(ci + 1) * 128],
                            kblk_l[kb][ro_:ro_ + 64, ch * 128:(ch + 1) * 128],
                            qsb3[jq % 2][ro_:ro_ + 64, ch, u * 128:(u + 1) * 128],
                            start=True, stop=True), "T")
                    r_expsb.acquire(C, "scalar")
                    C.wait("scalar", "T", tsc)
                    se = C.inc(S.activation(expsb[:, 0:nck * 128], pb[sb_][:, 0:nck * 128],
                                            ACT.Exp, scale=0.125), "S")
                    rpb[sb_].users.append(("S", se))
                    C.wait("vector", "S", se)
                    vm = C.inc(V.tensor_tensor(expsb[:, 0:nck * 128], expsb[:, 0:nck * 128],
                                               mask[:, 0:nck * 128], op=ALU.mult), "V")
                    rpb[pv_].acquire(C, "tensor")
                    C.wait("tensor", "V", vm)
                    for ci, dl in enumerate(deltas):
                        vb = (gb + dl) % 4
                        r_vblk[vb].use(C, "tensor")
                        tpv = C.inc(Tn.matmul(pb[pv_][:, 0:HD + 1],
                                              expsb[:, ci * 128:(ci + 1) * 128],
                                              vblk_l[vb][:, h * (HD + 1):(h + 1) * (HD + 1)],
                                              start=(ci == 0), stop=(ci == nck - 1)), "T")
                        r_vblk[vb].users.append(("T", tpv))
                    r_expsb.users.append(("T", tpv))
                    C.wait("vector", "T", tpv)
                    if h == 0:
                        r_atok.acquire(C, "vector")
                    V.reciprocal(reccol[:, :], pb[pv_][:, HD:HD + 1])
                    va = C.inc(V.tensor_scalar(atok[:, h * 64:(h + 1) * 64], pb[pv_][:, 0:HD],
                                               reccol[:, :], None, op0=ALU.mult), "V")
                    rpb[pv_].users.append(("V", va))
                for ch in range(NCH):
                    tb_ = 4 + (state["trot"] % 2)
                    state["trot"] += 1
                    rpb[tb_].acquire(C, "tensor")
                    C.wait("tensor", "V", va)
                    ttv = C.inc(Tn.transpose(pbT[tb_][:, 0:128], atok[:, ch * 128:(ch + 1) * 128],
                                             ident[:, :]), "T")
                    r_atok.users.append(("T", ttv))
                    C.wait("scalar", "T", ttv)
                    sev = C.inc(S.copy(attnT_l[jq % 2][:, ch * TT + u * 128: ch * TT + u * 128 + 128],
                                       pbT[tb_][:, 0:128]), "S")
                    rpb[tb_].users.append(("S", sev))
                    r_attnT[jq % 2].ready.append(("S", sev))

            def wout_tile(tj):
                t0 = tj * TT
                ob = outsb[tj % 2]
                ro = r_outsb[tj % 2]
                r_attnT[tj % 2].use(C, "tensor")
                evs = []
                tm_bank = {}
                for oc in (0, 2, 4, 6, 1, 3, 5, 7):
                    bk = oc // 2
                    half = (oc % 2) * 256
                    rpb[bk].acquire(C, "tensor")
                    for dc in range(NCH):
                        tm = C.inc(Tn.matmul(pb[bk][:, half:half + TT],
                                             woutw[:, dc * D + oc * 128: dc * D + oc * 128 + 128],
                                             attnT_l[tj % 2][:, dc * TT:(dc + 1) * TT],
                                             start=(dc == 0), stop=(dc == NCH - 1)), "T")
                    tm_bank[bk] = tm
                    if oc % 2 == 1:
                        C.wait("vector", "T", tm)
                        C.wait("vector", f"ST{tj % 2}", wout_tile.rb[tj])
                        ve = C.inc(V.tensor_tensor(ob[:, bk * 512:(bk + 1) * 512],
                                                   ob[:, bk * 512:(bk + 1) * 512],
                                                   pb[bk][:, 0:512], op=ALU.add), "V")
                        rpb[bk].users.append(("V", ve))
                        evs.append(ve)
                C.wait("gpsimd", "V", evs[-1])
                stv = C.inc(G.dma_start(out=d_out[:, t0:t0 + TT].rearrange("(c p) w -> p c w", p=128),
                                        in_=ob[:, :].rearrange("p (c w) -> p c w", c=NCH)), f"ST{tj % 2}", 16)
                ro.users.append((f"ST{tj % 2}", stv))
                r_attnT[tj % 2].users.append(("T", tm))
                return stv

            wout_tile.rb = {}

            def readback(tj):
                ob = outsb[tj % 2]
                ro = r_outsb[tj % 2]
                ro.acquire(C, "gpsimd")
                wout_tile.rb[tj] = C.inc(
                    G.dma_start(out=ob[:, :].rearrange("p (c w) -> p c w", c=NCH),
                                in_=d_out[:, tj * TT:tj * TT + TT].rearrange("(c p) w -> p c w", p=128)),
                    f"ST{tj % 2}", 16)

            st_final = 0
            for j in range(NTILE):
                jl = j % TPS
                t0 = j * TT
                xb = xtile[j % 2]
                rx = r_xtile[j % 2]

                rx.acquire(C, "sync")
                lxs = f"LX{j % 2}"
                lx = dma_in(
                    xb[:, :].rearrange("p (c w) -> p c w", c=NCH)[:, :, 1:257],
                    d_xf[:, t0:t0 + TT].rearrange("(c p) w -> p c w", p=128), lxs)
                if jl >= 1:
                    readback(j - 1)
                rpb[4].acquire(C, "tensor")
                C.wait("tensor", "V", b_rows_done)
                C.inc(Tn.matmul(pb[4][:, 0:128], ones1b[:, :], rstd_rows[0:1, t0:t0 + 128],
                                start=True, stop=True), "T")
                C.inc(Tn.matmul(pb[4][:, 128:256], ones1b[:, :], rstd_rows[0:1, t0 + 128:t0 + 256],
                                start=True, stop=True), "T")
                C.inc(Tn.matmul(pb[4][:, 256:384], mones1b[:, :], mr_rows[0:1, t0:t0 + 128],
                                start=True, stop=True), "T")
                tb2 = C.inc(Tn.matmul(pb[4][:, 384:512], mones1b[:, :], mr_rows[0:1, t0 + 128:t0 + 256],
                                      start=True, stop=True), "T")
                C.wait("vector", lxs, lx)
                C.wait("vector", "T", tb2)
                r_xnsb.acquire(C, "vector")
                V.tensor_tensor(xn3[:, :, :], x3[j % 2][:, :, 1:257],
                                pb[4][:, 0:256].unsqueeze(1).broadcast_to((128, NCH, 256)),
                                op=ALU.mult)
                v_xn = C.inc(V.tensor_tensor(xn3[:, :, :], xn3[:, :, :],
                                             pb[4][:, 256:512].unsqueeze(1).broadcast_to((128, NCH, 256)),
                                             op=ALU.add), "V")
                rpb[4].users.append(("V", v_xn))
                rx.users.append(("V", v_xn))

                C.wait("tensor", "V", v_xn)
                C.wait("tensor", "S", s_wmm)
                for kind in ("q", "k"):
                    for ch in range(NCH):
                        qb = 6 + (ch % 2)
                        sw = 4 + (ch % 2)
                        woff = (0 if kind == "q" else D) + ch * 128
                        rpb[qb].acquire(C, "tensor")
                        for dc in range(NCH):
                            tm = C.inc(Tn.matmul(pb[qb][:, 0:TT],
                                                 wqkv[:, dc * 3 * D + woff: dc * 3 * D + woff + 128],
                                                 xnsb[:, dc * TT:(dc + 1) * TT],
                                                 start=(dc == 0), stop=(dc == NCH - 1)), "T")
                        r_xnsb.users.append(("T", tm))
                        qr = ch % 2
                        r_qraw[qr].acquire(C, "scalar")
                        C.wait("scalar", "T", tm)
                        se = C.inc(S.copy(qraw_l[qr][:, 0:TT], pb[qb][:, 0:TT]), "S")
                        rpb[qb].users.append(("S", se))
                        rpb[sw].acquire(C, "tensor")
                        C.wait("tensor", "S", se)
                        tsw = C.inc(Tn.matmul(pb[sw][:, 0:TT], pswap[:, :], qraw_l[qr][:, 0:TT],
                                              start=True, stop=True), "T")
                        C.wait("vector", "T", tsw)
                        V.tensor_tensor(tr1, pb[sw][:, 0:TT], sinT[:, t0:t0 + TT], op=ALU.mult)
                        V.tensor_tensor(tr2, qraw_l[qr][:, 0:TT], cosT[:, t0:t0 + TT], op=ALU.mult)
                        if kind == "q":
                            if ch == 0:
                                r_qsb[j % 2].acquire(C, "vector")
                            vr = C.inc(V.tensor_tensor(qsb3[j % 2][:, ch, :], tr1, tr2, op=ALU.add), "V")
                            r_qsb[j % 2].ready.append(("V", vr))
                        else:
                            for half in range(2):
                                kb = (2 * j + half) % 4
                                if ch == 0:
                                    r_kblk[kb].acquire(C, "vector")
                                vr = C.inc(V.tensor_tensor(
                                    kblk_l[kb][:, ch * 128:(ch + 1) * 128],
                                    tr1[:, half * 128:(half + 1) * 128],
                                    tr2[:, half * 128:(half + 1) * 128], op=ALU.add), "V")
                                r_kblk[kb].ready.append(("V", vr))
                        rpb[sw].users.append(("V", vr))
                        r_qraw[qr].users.append(("V", vr))
                for u in range(2):
                    vbi = (2 * j + u) % 4
                    r_vblk[vbi].acquire(C, "vector")
                    for hf in range(2):
                        qb = 6 + hf
                        rpb[qb].acquire(C, "tensor")
                        for dc in range(NCH):
                            tm = C.inc(Tn.matmul(
                                pb[qb][:, 0:512],
                                xn3[:, dc, u * 128:(u + 1) * 128],
                                wqkv[:, dc * 3 * D + 2 * D + hf * 512: dc * 3 * D + 2 * D + (hf + 1) * 512],
                                start=(dc == 0), stop=(dc == NCH - 1)), "T")
                        r_xnsb.users.append(("T", tm))
                        C.wait("vector", "T", tm)
                        vb3 = vblk_l[vbi].rearrange("p (h d) -> p h d", h=H)
                        vev = C.inc(V.tensor_copy(
                            vb3[:, hf * 8:(hf + 1) * 8, 0:HD],
                            pb[qb][:, 0:512].rearrange("p (h d) -> p h d", h=8)), "V")
                        rpb[qb].users.append(("V", vev))
                        r_vblk[vbi].ready.append(("V", vev))

                s_ = j // TPS
                blocks = [0] if jl == 0 else [2 * jl - 1, 2 * jl]
                if jl == TPS - 1:
                    blocks.append(15)
                for bl in blocks:
                    attn_block(s_ * 16 + bl)

                if jl >= 1:
                    st_final = wout_tile(j - 1)
                if jl == TPS - 1:
                    readback(j)
                    st_final = wout_tile(j)

            C.wait("gpsimd", "ST0", C.c.get("ST0", 0))
            C.wait("gpsimd", "ST1", C.c.get("ST1", 0))
    return nc


_BUILT = {}


def _get_nc():
    if "nc" not in _BUILT:
        nc = bass.Bass("TRN2", target_bir_lowering=False, debug=False)
        build_program(nc)
        _BUILT["nc"] = nc
    return _BUILT["nc"]


def _consts():
    if "c" in _BUILT:
        return _BUILT["c"]
    pos = np.arange(TSEQ, dtype=np.float64)
    inv = 1.0 / (10000.0 ** (np.arange(0, HD, 2, dtype=np.float64) / HD))
    fr = pos[:, None] * inv[None, :]
    cosv = np.cos(fr)
    sinv = np.sin(fr)
    cosT = np.zeros((128, TSEQ), np.float32)
    sinT = np.zeros((128, TSEQ), np.float32)
    for r in range(128):
        d = r % 64
        jj = d if d < 32 else d - 32
        cosT[r] = cosv[:, jj]
        sinT[r] = (-sinv[:, jj]) if d < 32 else sinv[:, jj]
    cosT = np.tile(cosT, (1, 2)).astype(bf16)
    sinT = np.tile(sinT, (1, 2)).astype(bf16)
    psw = np.zeros((128, 128), np.float32)
    for i in range(128):
        d = i % 64
        k = i + 32 if d < 32 else i - 32
        psw[k, i] = 1.0
    kk = np.arange(128)[:, None]
    qq = np.arange(128)[None, :]
    mneg = (kk > qq).astype(np.float32)
    mpos = (kk <= qq).astype(np.float32)
    one = np.ones((128, 128), np.float32)
    c = dict(
        d_cosT=cosT, d_sinT=sinT, d_pswap=psw.astype(bf16),
        d_ident=np.eye(128, dtype=np.float32).astype(bf16),
        d_maskint=np.concatenate([mneg, one, mpos], 1).astype(bf16),
        d_maskfirst=np.concatenate([one, mpos], 1).astype(bf16),
        d_masklast=np.concatenate([mneg, one], 1).astype(bf16),
        d_ones1=np.ones((1, 128), np.float32),
        d_mones1=np.full((1, 128), -1.0, np.float32),
        d_ones1b=np.ones((1, 128), bf16),
        d_mones1b=np.full((1, 128), -1.0, np.float32).astype(bf16),
        d_onescol=np.ones((128, 1), np.float32),
        d_onescolb=np.ones((128, 1), bf16),
        d_identf=np.eye(128, dtype=np.float32),
    )
    _BUILT["c"] = c
    return c


def kernel(L0_attn_x, L0_attn_w_qkv, L0_attn_w_out, L0_attn_b_out,
           L0_ffn_w1, L0_ffn_conv_w, L0_ffn_conv_b, L0_ffn_w2):
    x = np.ascontiguousarray(np.asarray(L0_attn_x, np.float32))
    wqkv = np.asarray(L0_attn_w_qkv, np.float32)
    wout = np.asarray(L0_attn_w_out, np.float32)
    bout = np.asarray(L0_attn_b_out, np.float32)
    w1 = np.asarray(L0_ffn_w1, np.float32)
    cw = np.asarray(L0_ffn_conv_w, np.float32)
    cb = np.asarray(L0_ffn_conv_b, np.float32)
    w2 = np.asarray(L0_ffn_w2, np.float32)

    shared = dict(
        wqkvT=np.ascontiguousarray(wqkv.T),
        woutT=np.ascontiguousarray(wout.T),
        d_boutl=np.ascontiguousarray(bout.reshape(NCH, 128).T),
        w1T=np.ascontiguousarray(w1.T),
        w2T=np.ascontiguousarray(w2.T),
        d_convw=np.ascontiguousarray(cw[:, 0, :].reshape(HCH, 128, 3).transpose(1, 2, 0).reshape(128, 96)),
        d_convb=np.ascontiguousarray(cb.reshape(HCH, 128).T),
        **_consts(),
    )
    nc = _get_nc()
    in_maps = []
    for ci in range(8):
        xc = x[2 * ci:2 * ci + 2]
        xf = np.concatenate([xc[0].T, xc[1].T], axis=1)
        m = dict(shared)
        m["xf"] = np.ascontiguousarray(xf)
        in_maps.append(m)
    res = run_bass_kernel_spmd(nc, in_maps, core_ids=list(range(8)))
    out = np.empty((16, TSEQ, D), np.float32)
    for ci in range(8):
        of = res.results[ci]["outf"]
        out[2 * ci] = of[:, :TSEQ].T
        out[2 * ci + 1] = of[:, TSEQ:].T
    return out


# version 08
# speedup vs baseline: 1.0165x; 1.0165x over previous
import numpy as np
import ml_dtypes
import concourse.bass as bass
import concourse.mybir as mybir
import concourse.bass_utils as _bass_utils
from concourse.bass_utils import run_bass_kernel_spmd

_orig_run_command = _bass_utils.run_command


def _run_command_ldwopt(cmd, *a, **k):
    cmd = ["--enable-ldw-opt=true" if c == "--enable-ldw-opt=false" else c for c in cmd]
    return _orig_run_command(cmd, *a, **k)


_bass_utils.run_command = _run_command_ldwopt

F32 = mybir.dt.float32
BF16 = mybir.dt.bfloat16
I8 = mybir.dt.int8
F8 = mybir.dt.float8e4
ALU = mybir.AluOpType
ACT = mybir.ActivationFunctionType
AX = mybir.AxisListType
bf16 = ml_dtypes.bfloat16

D = 1024
H = 16
HD = 64
TSEQ = 2048
T = 4096
DH = 4096
NCH = 8
HCH = 32
TT = 256
NTILE = 16
TPS = 8
TTH = TT + 2
EPS = 1e-5
RMAGIC = 12582912.0


class Ctr:
    def __init__(self, nc, sems):
        self.c = {}
        self.lastw = {}
        self.eng = {"sync": nc.sync, "gpsimd": nc.gpsimd, "tensor": nc.tensor,
                    "vector": nc.vector, "scalar": nc.scalar}
        self.sems = sems

    def inc(self, inst, sem, n=1):
        inst.then_inc(self.sems[sem], n)
        self.c[sem] = self.c.get(sem, 0) + n
        return self.c[sem]

    def wait(self, ename, sem, val):
        if val is None or val <= 0:
            return
        k = (ename, sem)
        if self.lastw.get(k, 0) >= val:
            return
        self.lastw[k] = val
        self.eng[ename].wait_ge(self.sems[sem], val)


def _ap_span(ap):
    try:
        t = ap.tensor
        name = t.name
        esz = mybir.dt.size(ap.dtype)
        pairs = list(ap.ap)
        p0 = int(getattr(t, "base_partition", 0) or 0)
        np_ = int(pairs[0][1]) if pairs else 128
        off = int(ap.offset)
        ext = 0
        for (st, ct) in pairs[1:]:
            ext += abs(int(st)) * (int(ct) - 1)
        lo = off * esz
        hi = (off + ext + 1) * esz
        return (name, p0, p0 + np_, lo, hi)
    except Exception:
        return ("?", 0, 128, 0, 1 << 40)


def _overlap(a, b):
    return (a[0] == b[0] and a[1] < b[2] and b[1] < a[2]
            and a[3] < b[4] and b[3] < a[4])


class DrainWrap:
    _OPS = ("tensor_tensor", "tensor_scalar", "tensor_add", "tensor_sub", "tensor_mul",
            "tensor_max", "tensor_copy", "tensor_single_scalar", "tensor_scalar_add",
            "tensor_scalar_sub", "tensor_scalar_mul", "tensor_scalar_max", "tensor_scalar_min",
            "tensor_reduce", "reduce_max", "reduce_sum", "scalar_tensor_tensor",
            "reciprocal", "memset", "memzero", "activation", "copy", "add", "mul",
            "square", "sqrt", "sign", "select", "transpose", "bn_stats", "bn_aggr")

    def __init__(self, eng):
        self._e = eng
        self._w = []
        self._r = []

    def __getattr__(self, attr):
        f = getattr(self._e, attr)
        if attr in DrainWrap._OPS:
            def wrapped(*a, _f=f, **k):
                aps = [x for x in list(a) + list(k.values())
                       if hasattr(x, "tensor") and hasattr(x, "ap")]
                if aps:
                    wr = [_ap_span(aps[0])]
                    ao = k.get("accum_out")
                    if ao is not None and hasattr(ao, "tensor"):
                        wr.append(_ap_span(ao))
                    rd = [_ap_span(x) for x in aps[1:]]
                    haz = (any(_overlap(x, y) for x in rd + wr for y in self._w)
                           or any(_overlap(x, y) for x in wr for y in self._r))
                    if haz:
                        self._e.drain()
                        self._w = []
                        self._r = []
                    self._w.extend(wr)
                    self._r.extend(rd)
                return _f(*a, **k)
            return wrapped
        return f


class Res:
    def __init__(self, name=""):
        self.name = name
        self.ready = []
        self.users = []

    def acquire(self, C, ename):
        if isinstance(ename, str):
            ename = (ename,)
        for e in ename:
            for (s, v) in self.users:
                C.wait(e, s, v)
        self.users = []
        self.ready = []

    def use(self, C, ename):
        for (s, v) in self.ready:
            C.wait(ename, s, v)


def build_program(nc, do_c=True, nb=NTILE, do_b=True, bstop="full"):
    DI = lambda n, sh, dt=F32: nc.dram_tensor(n, sh, dt, kind="ExternalInput").ap()
    d_xf = DI("xf", [D, T])
    d_wqkvT = DI("wqkvT", [D, 3 * D])
    d_woutT = DI("woutT", [D, D])
    d_boutl = DI("d_boutl", [128, NCH])
    d_w1T = DI("w1T", [D, DH])
    d_w2T = DI("w2T", [DH, D])
    d_convw = DI("d_convw", [128, 3 * HCH])
    d_convb = DI("d_convb", [128, HCH])
    d_cos = DI("d_cosT", [128, T], BF16)
    d_sin = DI("d_sinT", [128, T], BF16)
    d_pswap = DI("d_pswap", [128, 128], BF16)
    d_ident = DI("d_ident", [128, 128], BF16)
    d_maskint = DI("d_maskint", [128, 384], BF16)
    d_maskfirst = DI("d_maskfirst", [128, 256], BF16)
    d_masklast = DI("d_masklast", [128, 256], BF16)
    d_ones1 = DI("d_ones1", [1, 128])
    d_mones1 = DI("d_mones1", [1, 128])
    d_ones1b = DI("d_ones1b", [1, 128], BF16)
    d_mones1b = DI("d_mones1b", [1, 128], BF16)
    d_onescol = DI("d_onescol", [128, 1])
    d_onescolb = DI("d_onescolb", [128, 1], BF16)
    d_identf = DI("d_identf", [128, 128])
    d_out = nc.dram_tensor("outf", [D, T], F32, kind="ExternalOutput").ap()

    import contextlib
    with contextlib.ExitStack() as stk:
        SB = lambda n, sh, dt: stk.enter_context(nc.sbuf_tensor(n, sh, dt))
        PS = lambda n, sh, dt: stk.enter_context(nc.psum_tensor(n, sh, dt))

        xtile = [SB(f"xtile{i}", [128, NCH * TTH], F32) for i in range(2)]
        outsb = [SB(f"outsb{i}", [128, NCH * TT], F32) for i in range(2)]
        mr_rows = SB("mr_rows", [1, T], BF16)
        rstd_rows = SB("rstd_rows", [1, T], BF16)
        ones1b = SB("ones1b", [1, 128], BF16)
        mones1b = SB("mones1b", [1, 128], BF16)
        ones1 = SB("ones1", [1, 128], F32)
        mones1 = SB("mones1", [1, 128], F32)
        onescol = SB("onescol", [128, 1], F32)
        onescolb = SB("onescolb", [128, 1], BF16)
        identf = SB("identf", [128, 128], F32)
        boutl = SB("boutl", [128, NCH], F32)
        convw = SB("convw", [128, 3 * HCH], F32)
        convb = SB("convb", [128, HCH], F32)
        m127 = SB("m127", [1, 2], F32)
        mw2col = SB("mw2col", [128, 1], F32)
        scw1col = SB("scw1col", [128, 1], F32)
        scw2col = SB("scw2col", [128, 1], F32)
        ident = SB("ident", [128, 128], BF16)
        pswap = SB("pswap", [128, 128], BF16)
        maskint = SB("maskint", [128, 384], BF16)
        maskfirst = maskint[:, 128:384]
        masklast = maskint[:, 0:256]
        s2f2 = SB("s2f2", [128, 2], F32)
        fmx32 = SB("fmx32", [128, TTH], F32)
        fmn32 = SB("fmn32", [128, TTH], F32)
        gsc = SB("gsc", [128, 2 * TT], BF16)
        reccol2 = SB("reccol2", [128, 1], F32)
        f2row = SB("f2row", [1, 128], F32)
        s2row = SB("s2row", [1, 128], F32)
        sxrow = SB("sxrow", [1, TTH], F32)
        reccol = SB("reccol", [128, 1], F32)
        rowbuf = SB("rowbuf", [1, 8 * TTH], F32)
        s2b = SB("s2b", [128, TT], F32)
        f2b = SB("f2b", [128, TT], BF16)

        WA = SB("warena", [128, 81920], I8)
        w1s = WA[:, 0:NCH * DH].bitcast(F8)
        w2s = WA[:, NCH * DH:NCH * DH + HCH * D].bitcast(F8)
        wqkv = WA[:, 0:2 * NCH * 3 * D].bitcast(BF16)
        woutw = WA[:, 49152:49152 + 2 * NCH * D].bitcast(BF16)
        cosT = WA[:, 65536:65536 + 2 * T].bitcast(BF16)
        sinT = WA[:, 73728:73728 + 2 * T].bitcast(BF16)
        xs2 = WA[:, 65536:65536 + NCH * TTH * 2].bitcast(BF16)
        xtile3 = WA[:, 65536 + NCH * TTH * 2:65536 + NCH * TTH * 2 + NCH * TTH * 4].bitcast(F32)

        ARENA_SZ = 61824
        AR = SB("arena", [128, ARENA_SZ], I8)

        def carve(off, n, dt):
            size = {F32: 4, BF16: 2, I8: 1, F8: 1}[dt]
            off = (off + 3) // 4 * 4
            ap = AR[:, off:off + n * size]
            return (ap.bitcast(dt) if dt != I8 else ap), off + n * size

        o = 0
        xs, o = carve(o, NCH * TTH, BF16)
        xc_ctr, o = carve(o, NCH * TTH, F32)
        h1_l = []
        for i in range(2):
            a, o = carve(o, HCH * TTH, BF16)
            h1_l.append(a)
        h2, o = carve(o, HCH * TT, BF16)
        assert o <= ARENA_SZ, o
        wtmp = [AR[:, 0:16384].bitcast(F32), AR[:, 16384:32768].bitcast(F32)]
        wq8 = AR[:, 32768:36864]
        wabs_l = [AR[:, 36864:45056].bitcast(BF16), AR[:, 45056 - 8192 + 8192:53248].bitcast(BF16)]
        o = 0
        qsb_l = []
        for i in range(2):
            a, o = carve(o, NCH * TT, BF16)
            qsb_l.append(a)
        kblk_l = []
        for i in range(4):
            a, o = carve(o, NCH * 128, BF16)
            kblk_l.append(a)
        vblk_l = []
        for i in range(4):
            a, o = carve(o, H * (HD + 1), BF16)
            vblk_l.append(a)
        attnT_l = []
        for i in range(2):
            a, o = carve(o, NCH * TT, BF16)
            attnT_l.append(a)
        qraw_l = []
        for i in range(2):
            a, o = carve(o, TT, BF16)
            qraw_l.append(a)
        tr1, o = carve(o, TT, BF16)
        tr2, o = carve(o, TT, BF16)
        xnsb, o = carve(o, NCH * TT, BF16)
        expsb, o = carve(o, 384, BF16)
        atok, o = carve(o, H * HD, BF16)
        assert o <= ARENA_SZ, o

        pb = [PS(f"pb{i}", [128, 512], F32) for i in range(8)]
        pbT = [pb[i][:, :].bitcast(BF16) for i in range(8)]
        rpb = [Res(f"pb{i}") for i in range(8)]

        sems = {}
        for s in ("L", "LX0", "LX1", "LX2", "LW0", "LW1", "T", "V", "S", "G", "ST0", "ST1"):
            sems[s] = stk.enter_context(nc.semaphore(s))

        with nc.Block() as _block:
            C = Ctr(nc, sems)
            Tn, Sy = nc.tensor, nc.sync
            G = DrainWrap(nc.gpsimd)
            V = DrainWrap(nc.vector)
            S = DrainWrap(nc.scalar)

            r_xtile = [Res("xt0"), Res("xt1")]
            r_wabs = [Res("wa0"), Res("wa1")]
            r_outsb = [Res("ou0"), Res("ou1")]
            r_wtmp = [Res("wt0"), Res("wt1")]
            r_qsb = [Res("q0"), Res("q1")]
            r_kblk = [Res(f"k{i}") for i in range(4)]
            r_vblk = [Res(f"v{i}") for i in range(4)]
            r_attnT = [Res("aT0"), Res("aT1")]
            r_qraw = [Res("qr0"), Res("qr1")]
            r_atok = Res("atok")
            r_expsb = Res("expsb")
            r_xs = Res("xs")
            r_xc = Res("xc")
            r_s2b = Res("s2b")
            r_gsc = Res("gsc")
            r_h1b = [Res("h1a"), Res("h1b")]
            r_h2 = Res("h2")
            r_fold = Res("fold")
            r_xnsb = Res("xnsb")

            def dma_in(out_ap, in_ap, sem="L"):
                return C.inc(Sy.dma_start(out=out_ap, in_=in_ap), sem, 16)

            for (dst, src) in [(ones1, d_ones1), (mones1, d_mones1), (onescol, d_onescol),
                               (boutl, d_boutl), (convw, d_convw), (convb, d_convb),
                               (ident, d_ident), (identf, d_identf), (onescolb, d_onescolb),
                               (ones1b, d_ones1b), (mones1b, d_mones1b),
                               (pswap, d_pswap), (maskint, d_maskint)]:
                l_consts = dma_in(dst[:, :], src[:, :])

            C.wait("tensor", "L", l_consts)
            first = True
            tsum = 0
            for wi, (d_w, nch) in enumerate(((d_w1T, NCH), (d_w2T, HCH))):
                wd = d_w.shape[1]
                for i in range(nch):
                    bi = i % 2
                    r_wtmp[bi].acquire(C, "sync")
                    lv = dma_in(wtmp[bi][:, 0:wd], d_w[i * 128:(i + 1) * 128, :], f"LW{bi}")
                    C.wait("scalar", f"LW{bi}", lv)
                    r_wabs[bi].acquire(C, "scalar")
                    sv = C.inc(S.activation(wabs_l[bi][:, 0:wd], wtmp[bi][:, 0:wd], ACT.Abs), "S")
                    r_wtmp[bi].users.append(("S", sv))
                    C.wait("tensor", "S", sv)
                    for ss in range(wd // 256):
                        last = (i == nch - 1) and (ss == wd // 256 - 1)
                        tsum = C.inc(Tn.matmul(pb[6][0:1, wi * 256:wi * 256 + 256],
                                               onescolb[:, :], wabs_l[bi][:, ss * 256:(ss + 1) * 256],
                                               start=first, stop=last), "T")
                        first = False
                    r_wabs[bi].users.append(("T", tsum))
                first = True
            C.wait("vector", "T", tsum)
            V.reduce_sum(rowbuf[0:1, 0:1], pb[6][0:1, 0:256], AX.X)
            V.reduce_sum(rowbuf[0:1, 1:2], pb[6][0:1, 256:512], AX.X)
            V.tensor_scalar(rowbuf[0:1, 2:4], rowbuf[0:1, 0:2], 1.0 / (D * DH), 1e-5,
                            op0=ALU.mult, op1=ALU.max)
            V.tensor_scalar(rowbuf[0:1, 4:6], rowbuf[0:1, 2:4], 1.0 / 127.0, None, op0=ALU.mult)
            vv = C.inc(V.tensor_copy(m127[:, :], rowbuf[0:1, 4:6]), "V")
            C.wait("tensor", "V", vv)
            tv = C.inc(Tn.matmul(pb[7][:, 0:4], ones1[:, :], rowbuf[0:1, 2:6], start=True, stop=True), "T")
            C.wait("vector", "T", tv)
            V.reciprocal(scw1col[:, :], pb[7][:, 0:1])
            V.reciprocal(scw2col[:, :], pb[7][:, 1:2])
            v_scales = C.inc(V.tensor_copy(mw2col[:, :], pb[7][:, 3:4]), "V")
            rpb[6].users.append(("V", v_scales))
            rpb[7].users.append(("V", v_scales))

            vq = 0
            for (d_w, nch, dst, sc) in ((d_w1T, NCH, w1s, scw1col), (d_w2T, HCH, w2s, scw2col)):
                wd = d_w.shape[1]
                for i in range(nch):
                    bi = i % 2
                    r_wtmp[bi].acquire(C, "sync")
                    lv = dma_in(wtmp[bi][:, 0:wd], d_w[i * 128:(i + 1) * 128, :], f"LW{bi}")
                    C.wait("vector", f"LW{bi}", lv)
                    C.wait("vector", "V", v_scales)
                    V.tensor_scalar(wtmp[bi][:, 0:wd], wtmp[bi][:, 0:wd], sc[:, :], 1.49,
                                    op0=ALU.mult, op1=ALU.min)
                    V.tensor_scalar(wtmp[bi][:, 0:wd], wtmp[bi][:, 0:wd], -1.49, RMAGIC,
                                    op0=ALU.max, op1=ALU.add)
                    vq = C.inc(V.tensor_scalar(dst[:, i * wd:(i + 1) * wd], wtmp[bi][:, 0:wd],
                                               -RMAGIC, None, op0=ALU.add), "V")
                    r_wtmp[bi].users.append(("V", vq))
            v_signs = vq

            x3 = [xtile[i][:, :].rearrange("p (c w) -> p c w", c=NCH) for i in range(2)]
            xc3 = xc_ctr.rearrange("p (c w) -> p c w", c=NCH)
            xs3 = xs.rearrange("p (c w) -> p c w", c=NCH)
            RS = lambda k: rowbuf[0:1, k * TTH:(k + 1) * TTH]

            v_rows = 0
            xtile_b = [xtile[0][:, :], xtile[1][:, :], xtile3]
            r_xtb = [r_xtile[0], r_xtile[1], Res("xt3")]
            x3b = [a.rearrange("p (c w) -> p c w", c=NCH) for a in xtile_b]
            xs_l = [xs, xs2]
            xsl3 = [a.rearrange("p (c w) -> p c w", c=NCH) for a in xs_l]
            r_xsl = [r_xs, Res("xs2")]
            for i in range(3):
                vz0 = C.inc(V.memset(xtile_b[i], 0.0), "V")
                r_xtb[i].users.append(("V", vz0))
            bst = {}

            def b_p1(it):
                t0 = it * TT
                s0 = (it // TPS) * TSEQ
                sL = (t0 - 1) >= s0
                sR = (t0 + TT + 1) <= s0 + TSEQ
                g0 = t0 - 1 if sL else t0
                g1 = t0 + TT + 1 if sR else t0 + TT
                W = g1 - g0
                a0 = 0 if sL else 1
                xb = xtile_b[it % 3]
                xc = x3b[it % 3]
                rx = r_xtb[it % 3]
                xs_ = xs_l[it % 2]
                xs3_ = xsl3[it % 2]
                rxs_ = r_xsl[it % 2]

                rx.acquire(C, "sync")
                lxs = f"LX{it % 3}"
                lx = dma_in(
                    xb.rearrange("p (c w) -> p c w", c=NCH)[:, :, a0:a0 + W],
                    d_xf[:, g0:g1].rearrange("(c p) w -> p c w", p=128), lxs)

                rpb[6].acquire(C, "tensor")
                rpb[7].acquire(C, "tensor")
                C.wait("tensor", lxs, lx)
                for c in range(NCH):
                    C.inc(Tn.matmul(pb[6][0:1, 0:TTH], onescol[:, :], xc[:, c, 0:TTH],
                                    start=(c == 0), stop=(c == NCH - 1)), "T")
                C.wait("scalar", lxs, lx)
                r_xc.acquire(C, "scalar")
                tq_slot = [0, 0]
                for c in range(NCH):
                    C.wait("scalar", "T", tq_slot[c % 2])
                    sv = C.inc(S.activation(xc3[:, c % 2, 0:TTH],
                                            xc[:, c, 0:TTH], ACT.Square), "S")
                    C.wait("tensor", "S", sv)
                    tq = C.inc(Tn.matmul(pb[7][0:1, 0:TTH], onescol[:, :],
                                         xc3[:, c % 2, 0:TTH],
                                         start=(c == 0), stop=(c == NCH - 1)), "T")
                    tq_slot[c % 2] = tq

                C.wait("scalar", "T", tq)
                sv_mean = C.inc(S.activation(RS(0), pb[6][0:1, 0:TTH], ACT.Copy, scale=1.0 / D), "S")
                rpb[6].users.append(("S", sv_mean))
                sv = C.inc(S.activation(RS(1), pb[7][0:1, 0:TTH], ACT.Copy, scale=1.0 / D), "S")
                rpb[7].users.append(("S", sv))
                sv2 = C.inc(S.activation(RS(2), RS(0), ACT.Square), "S")
                C.wait("vector", "S", sv2)
                vv = C.inc(V.tensor_sub(RS(1), RS(1), RS(2)), "V")
                vv = C.inc(V.tensor_scalar(RS(1), RS(1), EPS, None, op0=ALU.add), "V")
                C.wait("scalar", "V", vv)
                sv = C.inc(S.activation(RS(2), RS(1), ACT.Ln), "S")
                sv_rstd = C.inc(S.activation(RS(3), RS(2), ACT.Exp, scale=-0.5), "S")
                C.wait("vector", "S", sv_rstd)
                v_mr = C.inc(V.tensor_mul(RS(2), RS(0), RS(3)), "V")

                rpb[6].acquire(C, "tensor")
                C.wait("tensor", "S", sv_mean)
                t_meanB = C.inc(Tn.matmul(pb[6][:, 0:TTH], ones1[:, :], RS(0), start=True, stop=True), "T")
                C.wait("vector", "T", t_meanB)
                for c in range(NCH):
                    C.inc(V.tensor_tensor(xc3[:, c, 0:TTH], xc[:, c, 0:TTH], pb[6][:, 0:TTH],
                                          op=ALU.subtract), "V")
                    if c == 0:
                        V.tensor_copy(fmx32[:, 0:TTH], xc3[:, c, 0:TTH])
                        vv = C.inc(V.tensor_copy(fmn32[:, 0:TTH], xc3[:, c, 0:TTH]), "V")
                    else:
                        V.tensor_tensor(fmx32[:, 0:TTH], fmx32[:, 0:TTH], xc3[:, c, 0:TTH],
                                        op=ALU.max)
                        vv = C.inc(V.tensor_tensor(fmn32[:, 0:TTH], fmn32[:, 0:TTH], xc3[:, c, 0:TTH],
                                                   op=ALU.min), "V")
                V.tensor_scalar(fmn32[:, 0:TTH], fmn32[:, 0:TTH], -1.0, None, op0=ALU.mult)
                v_fold = C.inc(V.tensor_tensor(fmx32[:, 0:TTH], fmx32[:, 0:TTH], fmn32[:, 0:TTH],
                                               op=ALU.max), "V")
                rpb[6].users.append(("V", v_fold))

                rpb[7].acquire(C, "tensor")
                C.wait("tensor", "V", v_fold)
                cuts = [0, 128, TTH - 128]
                ttr = [C.inc(Tn.transpose(pb[7][:, j * 128:(j + 1) * 128], fmx32[:, cu:cu + 128],
                                          identf[:, :]), "T") for j, cu in enumerate(cuts)]
                for j, cu in enumerate(cuts):
                    C.wait("vector", "T", ttr[-1])
                    vv = C.inc(V.reduce_max(reccol[:, :], pb[7][:, j * 128:(j + 1) * 128], AX.X), "V")
                    C.wait("tensor", "V", vv)
                    tv = C.inc(Tn.matmul(pb[6][0:1, 300:428], reccol[:, :], identf[:, :],
                                         start=True, stop=True), "T")
                    C.wait("vector", "T", tv)
                    vv = C.inc(V.tensor_copy(sxrow[0:1, cu:cu + 128], pb[6][0:1, 300:428]), "V")
                rpb[6].users.append(("V", vv))
                rpb[7].users.append(("V", vv))

                vv = C.inc(V.tensor_mul(RS(4), sxrow[0:1, 0:TTH], RS(3)), "V")
                vv = C.inc(V.tensor_scalar(RS(4), RS(4), 1e-5, None, op0=ALU.max), "V")
                vv = C.inc(V.reciprocal(RS(5), RS(4)), "V")
                vv = C.inc(V.tensor_scalar(RS(5), RS(5), 127.0, None, op0=ALU.mult), "V")
                vv_a = C.inc(V.tensor_mul(RS(6), RS(3), RS(5)), "V")
                vv_f1 = C.inc(V.tensor_scalar(RS(7), RS(4), m127[0:1, 0:1], None, op0=ALU.mult), "V")
                C.inc(V.tensor_copy(rstd_rows[0:1, t0:t0 + TT], RS(3)[0:1, 1:257]), "V")
                vr = C.inc(V.tensor_copy(mr_rows[0:1, t0:t0 + TT], RS(2)[0:1, 1:257]), "V")

                rpb[7].acquire(C, "tensor")
                C.wait("tensor", "V", vv_a)
                t_aB = C.inc(Tn.matmul(pb[7][:, 0:TTH], ones1[:, :], RS(6), start=True, stop=True), "T")
                rpb[6].acquire(C, "tensor")
                C.wait("tensor", "V", vv_f1)
                t_f1B = C.inc(Tn.matmul(pb[6][:, 0:TTH], ones1[:, :], RS(7), start=True, stop=True), "T")
                C.wait("vector", "T", t_aB)
                rxs_.acquire(C, "vector")
                V.tensor_tensor(
                    xc3[:, :, :], xc3[:, :, :],
                    pb[7][:, 0:TTH].unsqueeze(1).broadcast_to((128, NCH, TTH)), op=ALU.mult)
                vv = C.inc(V.tensor_scalar(xs3_[:, :, :], xc3[:, :, :], RMAGIC, -RMAGIC,
                                           op0=ALU.add, op1=ALU.add), "V")
                rpb[7].users.append(("V", vv))
                r_xc.users.append(("V", vv))
                C.wait("vector", "T", t_f1B)
                v_xs = C.inc(V.tensor_tensor(
                    xs3_[:, :, :], xs3_[:, :, :],
                    pb[6][:, 0:TTH].unsqueeze(1).broadcast_to((128, NCH, TTH)), op=ALU.mult), "V")
                rpb[6].users.append(("V", v_xs))
                if not sL:
                    v_xs = C.inc(V.memset(xs3_[:, :, 0:1], 0.0), "V")
                if not sR:
                    v_xs = C.inc(V.memset(xs3_[:, :, TTH - 1:TTH], 0.0), "V")
                bst[it] = {"v_xs": v_xs, "v_rows": vr}

            def b_p2pair(ta, tb):
                for it in (ta, tb):
                    C.wait("tensor", "V", bst[it]["v_xs"])
                C.wait("tensor", "V", v_signs)
                r_h1b[0].acquire(C, ("vector", "scalar"))
                r_h1b[1].acquire(C, ("vector", "scalar"))
                eva = []
                evb = []
                for cc in range(HCH):
                    bka = 2 + 2 * (cc % 2)
                    bkb = bka + 1
                    rpb[bka].acquire(C, "tensor")
                    rpb[bkb].acquire(C, "tensor")
                    for dc in range(NCH):
                        lhs = w1s[:, dc * DH + cc * 128: dc * DH + cc * 128 + 128]
                        C.inc(Tn.matmul(pb[bka][:, 0:TTH], lhs,
                                        xs_l[ta % 2][:, dc * TTH:(dc + 1) * TTH],
                                        start=(dc == 0), stop=(dc == NCH - 1)), "T")
                        tm = C.inc(Tn.matmul(pb[bkb][:, 0:TTH], lhs,
                                             xs_l[tb % 2][:, dc * TTH:(dc + 1) * TTH],
                                             start=(dc == 0), stop=(dc == NCH - 1)), "T")
                    C.wait("vector", "T", tm)
                    se = C.inc(V.tensor_copy(h1_l[ta % 2][:, cc * TTH:(cc + 1) * TTH],
                                             pb[bka][:, 0:TTH]), "V")
                    rpb[bka].users.append(("V", se))
                    eva.append(("V", se))
                    C.wait("scalar", "T", tm)
                    se2 = C.inc(S.copy(h1_l[tb % 2][:, cc * TTH:(cc + 1) * TTH],
                                       pb[bkb][:, 0:TTH]), "S")
                    rpb[bkb].users.append(("S", se2))
                    evb.append(("S", se2))
                r_xsl[ta % 2].users.append(("T", tm))
                r_xsl[tb % 2].users.append(("T", tm))
                bst[ta]["h1ev"] = eva
                bst[tb]["h1ev"] = evb

            def b_p3(it):
                st_ = bst[it]
                h1ev = st_["h1ev"]
                h1b = h1_l[it % 2]
                rh = r_h1b[it % 2]
                r_h2.acquire(C, ("vector", "scalar", "gpsimd"))
                r_gsc.acquire(C, "gpsimd")
                sI = {}
                vA = {}
                gB = {}
                silu_done = [None] * HCH
                for it2 in range(HCH + 7):
                    cc = it2
                    if cc < HCH:
                        C.wait("scalar", h1ev[cc][0], h1ev[cc][1])
                        z = h2[:, cc * TT:(cc + 1) * TT]
                        sI[cc] = C.inc(S.activation(z, h1b[:, cc * TTH + 0: cc * TTH + TT],
                                                    ACT.Identity, scale=convw[:, cc:cc + 1],
                                                    bias=convb[:, cc:cc + 1]), "S")
                    c1 = it2 - 4
                    if 0 <= c1 < HCH:
                        C.wait("vector", "S", sI[c1])
                        C.wait("vector", "G", gB[c1])
                        C.wait("vector", h1ev[c1][0], h1ev[c1][1])
                        z = h2[:, c1 * TT:(c1 + 1) * TT]
                        V.scalar_tensor_tensor(z, h1b[:, c1 * TTH + 1: c1 * TTH + 1 + TT],
                                               convw[:, 32 + c1:32 + c1 + 1], z,
                                               op0=ALU.mult, op1=ALU.add)
                        g = gsc[:, (c1 % 2) * TT:(c1 % 2) * TT + TT]
                        vA[c1] = C.inc(V.tensor_tensor(z, z, g, op=ALU.add), "V")
                    c2 = it2 - 2
                    if 0 <= c2 < HCH:
                        g = gsc[:, (c2 % 2) * TT:(c2 % 2) * TT + TT]
                        C.wait("gpsimd", h1ev[c2][0], h1ev[c2][1])
                        if c2 - 2 in vA:
                            C.wait("gpsimd", "V", vA[c2 - 2])
                        gB[c2] = C.inc(G.tensor_tensor(
                            g, h1b[:, c2 * TTH + 2: c2 * TTH + 2 + TT],
                            convw[:, 64 + c2:64 + c2 + 1].broadcast_to((128, TT)),
                            op=ALU.mult), "G")
                    c3 = it2 - 7
                    if 0 <= c3 < HCH and c3 % 4 == 3:
                        C.wait("scalar", "V", vA[c3])
                        zb = h2[:, (c3 - 3) * TT:(c3 + 1) * TT]
                        sz = C.inc(S.activation(zb, zb, ACT.Silu), "S")
                        for c4 in range(c3 - 3, c3 + 1):
                            silu_done[c4] = sz
                rh.users.append(("V", vA[HCH - 1]))
                rh.users.append(("S", sI[HCH - 1]))
                r_gsc.users.append(("V", vA[HCH - 2]))
                r_gsc.users.append(("V", vA[HCH - 1]))

                fx3 = h1b[:, 0:16 * TT].rearrange("p (c t) -> p c t", c=16)
                fn3 = h1b[:, 16 * TT:32 * TT].rearrange("p (c t) -> p c t", c=16)
                h23v = h2.rearrange("p (c t) -> p c t", c=HCH)
                C.wait("vector", "S", silu_done[HCH - 1])
                V.tensor_tensor(fx3[:, :, :], h23v[:, 0:16, :], h23v[:, 16:32, :], op=ALU.max)
                V.tensor_tensor(fn3[:, :, :], h23v[:, 0:16, :], h23v[:, 16:32, :], op=ALU.min)
                for lv in (8, 4, 2, 1):
                    V.tensor_tensor(fx3[:, 0:lv, :], fx3[:, 0:lv, :], fx3[:, lv:2 * lv, :], op=ALU.max)
                    V.tensor_tensor(fn3[:, 0:lv, :], fn3[:, 0:lv, :], fn3[:, lv:2 * lv, :], op=ALU.min)
                V.tensor_scalar(fn3[:, 0, :], fn3[:, 0, :], -1.0, None, op0=ALU.mult)
                vf = C.inc(V.tensor_tensor(fx3[:, 0, :], fx3[:, 0, :], fn3[:, 0, :],
                                           op=ALU.max), "V")

                rpb[0].acquire(C, "tensor")
                C.wait("tensor", "V", vf)
                tt1 = C.inc(Tn.transpose(pbT[0][:, 0:128], h1b[:, 0:128], ident[:, :]), "T")
                tt2 = C.inc(Tn.transpose(pbT[0][:, 128:256], h1b[:, 128:256], ident[:, :]), "T")
                rh.users.append(("T", tt2))
                rh.users.append(("V", vf))
                svals = []
                for j in range(2):
                    C.wait("vector", "T", tt2)
                    vv = C.inc(V.reduce_max(reccol2[:, :], pbT[0][:, j * 128:(j + 1) * 128], AX.X), "V")
                    vv = C.inc(V.tensor_scalar(s2f2[:, 0:1], reccol2[:, :], 1e-5, None, op0=ALU.max), "V")
                    vv = C.inc(V.reciprocal(s2f2[:, 1:2], s2f2[:, 0:1]), "V")
                    V.tensor_scalar(s2f2[:, 1:2], s2f2[:, 1:2], 127.0, None, op0=ALU.mult)
                    vv = C.inc(V.tensor_scalar(s2f2[:, 0:1], s2f2[:, 0:1], mw2col[:, :], None,
                                               op0=ALU.mult), "V")
                    C.wait("tensor", "V", vv)
                    tv = C.inc(Tn.matmul(pb[0][0:1, 300:428], s2f2[:, 0:1], identf[:, :],
                                         start=True, stop=True), "T")
                    tv = C.inc(Tn.matmul(pb[0][0:1, 172:300], s2f2[:, 1:2], identf[:, :],
                                         start=True, stop=True), "T")
                    C.wait("vector", "T", tv)
                    vv = C.inc(V.tensor_copy(f2row[0:1, 0:128], pb[0][0:1, 300:428]), "V")
                    vv = C.inc(V.tensor_copy(s2row[0:1, 0:128], pb[0][0:1, 172:300]), "V")
                    C.wait("tensor", "V", vv)
                    if j == 0:
                        rpb[1].acquire(C, "tensor")
                    else:
                        C.wait("tensor", "S", svals[-1])
                    tb = C.inc(Tn.matmul(pb[1][:, j * 128:(j + 1) * 128],
                                         ones1[:, :], s2row[0:1, 0:128], start=True, stop=True), "T")
                    tb2 = C.inc(Tn.matmul(pb[1][:, 256 + j * 128:256 + (j + 1) * 128],
                                          ones1[:, :], f2row[0:1, 0:128], start=True, stop=True), "T")
                    C.wait("scalar", "T", tb2)
                    if j == 0:
                        r_s2b.acquire(C, "scalar")
                    svals.append(C.inc(S.copy(s2b[:, j * 128:(j + 1) * 128],
                                              pb[1][:, j * 128:(j + 1) * 128]), "S"))
                    svals.append(C.inc(S.copy(f2b[:, j * 128:(j + 1) * 128],
                                              pb[1][:, 256 + j * 128:256 + (j + 1) * 128]), "S"))
                rpb[0].users.append(("V", vv))
                rpb[1].users.append(("S", svals[-1]))

                C.wait("vector", "S", svals[-1])
                h23 = h2.rearrange("p (c t) -> p c t", c=HCH)
                V.tensor_tensor(h23[:, :, :], h23[:, :, :],
                                s2b[:, :].unsqueeze(1).broadcast_to((128, HCH, TT)),
                                op=ALU.mult)
                v_hq = C.inc(V.tensor_scalar(h23[:, :, :], h23[:, :, :], RMAGIC, -RMAGIC,
                                             op0=ALU.add, op1=ALU.add), "V")
                st_["v_hq"] = v_hq
                r_h2.users.append(("V", v_hq))
                rh.users.append(("T", None))
                rh.users.pop()

            def b_p45(it):
                st_ = bst.pop(it)
                t0 = it * TT
                xc = x3b[it % 3]
                rx = r_xtb[it % 3]
                C.wait("tensor", "V", st_["v_hq"])
                ob = outsb[it % 2]
                ro = r_outsb[it % 2]
                ro.acquire(C, "vector")
                for oc in (0, 2, 4, 6, 1, 3, 5, 7):
                    bk = oc // 2
                    half = (oc % 2) * 256
                    rpb[bk].acquire(C, "tensor")
                    for cc in range(HCH):
                        tm = C.inc(Tn.matmul(pb[bk][:, half:half + TT],
                                             w2s[:, cc * D + oc * 128: cc * D + oc * 128 + 128],
                                             h2[:, cc * TT:(cc + 1) * TT],
                                             start=(cc == 0), stop=(cc == HCH - 1)), "T")
                    C.wait("vector", "T", tm)
                    dst = ob[:, oc * TT:(oc + 1) * TT]
                    V.tensor_tensor(dst, pb[bk][:, half:half + TT], f2b[:, :], op=ALU.mult)
                    ve = C.inc(V.scalar_tensor_tensor(dst, xc[:, oc, 1:257], 1.0, dst,
                                                      op0=ALU.mult, op1=ALU.add), "V")
                    rpb[bk].users.append(("V", ve))
                    C.wait("scalar", "V", ve)
                    ve2 = C.inc(S.activation(dst, dst, ACT.Identity, scale=1.0,
                                             bias=boutl[:, oc:oc + 1]), "S")
                    b_p45.sevs.append(("S", ve2))
                rx.users.append(("V", ve))
                r_s2b.users.append(("V", ve))
                r_h2.users.append(("T", tm))
                for u in b_p45.sevs:
                    C.wait("gpsimd", u[0], u[1])
                b_p45.sevs = []
                stv = C.inc(G.dma_start(out=d_out[:, t0:t0 + TT].rearrange("(c p) w -> p c w", p=128),
                                        in_=ob[:, :].rearrange("p (c w) -> p c w", c=NCH)), f"ST{it % 2}", 16)
                ro.users.append((f"ST{it % 2}", stv))
                return vr if False else st_["v_rows"]

            b_p45.sevs = []
            nb_ = nb if do_b else 0
            if nb_ > 0:
                assert nb_ % 2 == 0
                b_p1(0)
                b_p1(1)
                for p in range(nb_ // 2):
                    a, b = 2 * p, 2 * p + 1
                    b_p2pair(a, b)
                    b_p3(a)
                    v_rows = b_p45(a)
                    if a + 2 < nb_:
                        b_p1(a + 2)
                    b_p3(b)
                    v_rows = b_p45(b)
                    if b + 2 < nb_:
                        b_p1(b + 2)
            b_rows_done = v_rows
            if bstop != "full":
                C.inc(G.dma_start(out=d_out[:, 0:TT].rearrange("(c p) w -> p c w", p=128),
                                  in_=outsb[0][:, :].rearrange("p (c w) -> p c w", c=NCH)), "ST0", 16)
                C.wait("gpsimd", "ST0", C.c.get("ST0", 0))
                return
            if not do_c:
                if not do_b:
                    C.wait("gpsimd", "V", v_signs)
                    C.inc(G.dma_start(out=d_out[:, 0:TT].rearrange("(c p) w -> p c w", p=128),
                                      in_=outsb[0][:, :].rearrange("p (c w) -> p c w", c=NCH)), "ST0", 16)
                C.wait("gpsimd", "ST0", C.c.get("ST0", 0))
                C.wait("gpsimd", "ST1", C.c.get("ST1", 0))
                return

            last_ffn_T = C.c["T"]
            C.wait("sync", "T", last_ffn_T)
            C.wait("sync", "ST0", C.c.get("ST0", 0))
            C.wait("sync", "ST1", C.c.get("ST1", 0))
            l_tab = dma_in(cosT, d_cos[:, :])
            l_tab = dma_in(sinT, d_sin[:, :])
            for (d_w, nch, dst, wd) in ((d_wqkvT, NCH, wqkv, 3 * D), (d_woutT, NCH, woutw, D)):
                for i in range(nch):
                    bi = i % 2
                    r_wtmp[bi].acquire(C, "sync")
                    lv = dma_in(wtmp[bi][:, 0:wd], d_w[i * 128:(i + 1) * 128, :], f"LW{bi}")
                    C.wait("scalar", f"LW{bi}", lv)
                    wl = C.inc(S.copy(dst[:, i * wd:(i + 1) * wd], wtmp[bi][:, 0:wd]), "S")
                    r_wtmp[bi].users.append(("S", wl))
            s_wmm = wl

            C.wait("vector", "S", s_wmm)
            C.wait("vector", "L", l_tab)
            for b in range(4):
                vb3 = vblk_l[b].rearrange("p (h d) -> p h d", h=H)
                vinit = C.inc(V.memset(vb3[:, :, HD:HD + 1], 1.0), "V")

            qsb3 = [qsb_l[i].rearrange("p (c t) -> p c t", c=NCH) for i in range(2)]
            xn3 = xnsb.rearrange("p (c t) -> p c t", c=NCH)
            state = {"rot": 0, "trot": 0}

            def attn_block(gb):
                bl = gb % 16
                jq = gb // 2
                u = gb % 2
                if bl == 0:
                    deltas, mask = [0, 1], maskfirst
                elif bl == 15:
                    deltas, mask = [-1, 0], masklast
                else:
                    deltas, mask = [-1, 0, 1], maskint
                nck = len(deltas)
                va = None
                for h in range(H):
                    ch = h // 2
                    ro_ = (h % 2) * 64
                    sb_ = 0 + (state["rot"] % 2)
                    pv_ = 2 + (state["rot"] % 2)
                    state["rot"] += 1
                    rpb[sb_].acquire(C, "tensor")
                    for ci, dl in enumerate(deltas):
                        kb = (gb + dl) % 4
                        r_kblk[kb].use(C, "tensor")
                        r_qsb[jq % 2].use(C, "tensor")
                        tsc = C.inc(Tn.matmul(
                            pb[sb_][:, ci * 128:(ci + 1) * 128],
                            kblk_l[kb][ro_:ro_ + 64, ch * 128:(ch + 1) * 128],
                            qsb3[jq % 2][ro_:ro_ + 64, ch, u * 128:(u + 1) * 128],
                            start=True, stop=True), "T")
                    r_expsb.acquire(C, "scalar")
                    C.wait("scalar", "T", tsc)
                    se = C.inc(S.activation(expsb[:, 0:nck * 128], pb[sb_][:, 0:nck * 128],
                                            ACT.Exp, scale=0.125), "S")
                    rpb[sb_].users.append(("S", se))
                    C.wait("vector", "S", se)
                    vm = C.inc(V.tensor_tensor(expsb[:, 0:nck * 128], expsb[:, 0:nck * 128],
                                               mask[:, 0:nck * 128], op=ALU.mult), "V")
                    rpb[pv_].acquire(C, "tensor")
                    C.wait("tensor", "V", vm)
                    for ci, dl in enumerate(deltas):
                        vb = (gb + dl) % 4
                        r_vblk[vb].use(C, "tensor")
                        tpv = C.inc(Tn.matmul(pb[pv_][:, 0:HD + 1],
                                              expsb[:, ci * 128:(ci + 1) * 128],
                                              vblk_l[vb][:, h * (HD + 1):(h + 1) * (HD + 1)],
                                              start=(ci == 0), stop=(ci == nck - 1)), "T")
                        r_vblk[vb].users.append(("T", tpv))
                    r_expsb.users.append(("T", tpv))
                    C.wait("vector", "T", tpv)
                    if h == 0:
                        r_atok.acquire(C, "vector")
                    V.reciprocal(reccol[:, :], pb[pv_][:, HD:HD + 1])
                    va = C.inc(V.tensor_scalar(atok[:, h * 64:(h + 1) * 64], pb[pv_][:, 0:HD],
                                               reccol[:, :], None, op0=ALU.mult), "V")
                    rpb[pv_].users.append(("V", va))
                for ch in range(NCH):
                    tb_ = 4 + (state["trot"] % 2)
                    state["trot"] += 1
                    rpb[tb_].acquire(C, "tensor")
                    C.wait("tensor", "V", va)
                    ttv = C.inc(Tn.transpose(pbT[tb_][:, 0:128], atok[:, ch * 128:(ch + 1) * 128],
                                             ident[:, :]), "T")
                    r_atok.users.append(("T", ttv))
                    C.wait("scalar", "T", ttv)
                    sev = C.inc(S.copy(attnT_l[jq % 2][:, ch * TT + u * 128: ch * TT + u * 128 + 128],
                                       pbT[tb_][:, 0:128]), "S")
                    rpb[tb_].users.append(("S", sev))
                    r_attnT[jq % 2].ready.append(("S", sev))

            def wout_tile(tj):
                t0 = tj * TT
                ob = outsb[tj % 2]
                ro = r_outsb[tj % 2]
                r_attnT[tj % 2].use(C, "tensor")
                evs = []
                tm_bank = {}
                for oc in (0, 2, 4, 6, 1, 3, 5, 7):
                    bk = oc // 2
                    half = (oc % 2) * 256
                    rpb[bk].acquire(C, "tensor")
                    for dc in range(NCH):
                        tm = C.inc(Tn.matmul(pb[bk][:, half:half + TT],
                                             woutw[:, dc * D + oc * 128: dc * D + oc * 128 + 128],
                                             attnT_l[tj % 2][:, dc * TT:(dc + 1) * TT],
                                             start=(dc == 0), stop=(dc == NCH - 1)), "T")
                    tm_bank[bk] = tm
                    if oc % 2 == 1:
                        C.wait("vector", "T", tm)
                        C.wait("vector", f"ST{tj % 2}", wout_tile.rb[tj])
                        ve = C.inc(V.tensor_tensor(ob[:, bk * 512:(bk + 1) * 512],
                                                   ob[:, bk * 512:(bk + 1) * 512],
                                                   pb[bk][:, 0:512], op=ALU.add), "V")
                        rpb[bk].users.append(("V", ve))
                        evs.append(ve)
                C.wait("gpsimd", "V", evs[-1])
                stv = C.inc(G.dma_start(out=d_out[:, t0:t0 + TT].rearrange("(c p) w -> p c w", p=128),
                                        in_=ob[:, :].rearrange("p (c w) -> p c w", c=NCH)), f"ST{tj % 2}", 16)
                ro.users.append((f"ST{tj % 2}", stv))
                r_attnT[tj % 2].users.append(("T", tm))
                return stv

            wout_tile.rb = {}

            def readback(tj):
                ob = outsb[tj % 2]
                ro = r_outsb[tj % 2]
                ro.acquire(C, "gpsimd")
                wout_tile.rb[tj] = C.inc(
                    G.dma_start(out=ob[:, :].rearrange("p (c w) -> p c w", c=NCH),
                                in_=d_out[:, tj * TT:tj * TT + TT].rearrange("(c p) w -> p c w", p=128)),
                    f"ST{tj % 2}", 16)

            st_final = 0
            for j in range(NTILE):
                jl = j % TPS
                t0 = j * TT
                xb = xtile[j % 2]
                rx = r_xtile[j % 2]

                rx.acquire(C, "sync")
                lxs = f"LX{j % 2}"
                lx = dma_in(
                    xb[:, :].rearrange("p (c w) -> p c w", c=NCH)[:, :, 1:257],
                    d_xf[:, t0:t0 + TT].rearrange("(c p) w -> p c w", p=128), lxs)
                if jl >= 1:
                    readback(j - 1)
                rpb[4].acquire(C, "tensor")
                C.wait("tensor", "V", b_rows_done)
                C.inc(Tn.matmul(pb[4][:, 0:128], ones1b[:, :], rstd_rows[0:1, t0:t0 + 128],
                                start=True, stop=True), "T")
                C.inc(Tn.matmul(pb[4][:, 128:256], ones1b[:, :], rstd_rows[0:1, t0 + 128:t0 + 256],
                                start=True, stop=True), "T")
                C.inc(Tn.matmul(pb[4][:, 256:384], mones1b[:, :], mr_rows[0:1, t0:t0 + 128],
                                start=True, stop=True), "T")
                tb2 = C.inc(Tn.matmul(pb[4][:, 384:512], mones1b[:, :], mr_rows[0:1, t0 + 128:t0 + 256],
                                      start=True, stop=True), "T")
                C.wait("vector", lxs, lx)
                C.wait("vector", "T", tb2)
                r_xnsb.acquire(C, "vector")
                V.tensor_tensor(xn3[:, :, :], x3[j % 2][:, :, 1:257],
                                pb[4][:, 0:256].unsqueeze(1).broadcast_to((128, NCH, 256)),
                                op=ALU.mult)
                v_xn = C.inc(V.tensor_tensor(xn3[:, :, :], xn3[:, :, :],
                                             pb[4][:, 256:512].unsqueeze(1).broadcast_to((128, NCH, 256)),
                                             op=ALU.add), "V")
                rpb[4].users.append(("V", v_xn))
                rx.users.append(("V", v_xn))

                C.wait("tensor", "V", v_xn)
                C.wait("tensor", "S", s_wmm)
                for kind in ("q", "k"):
                    for ch in range(NCH):
                        qb = 6 + (ch % 2)
                        sw = 4 + (ch % 2)
                        woff = (0 if kind == "q" else D) + ch * 128
                        rpb[qb].acquire(C, "tensor")
                        for dc in range(NCH):
                            tm = C.inc(Tn.matmul(pb[qb][:, 0:TT],
                                                 wqkv[:, dc * 3 * D + woff: dc * 3 * D + woff + 128],
                                                 xnsb[:, dc * TT:(dc + 1) * TT],
                                                 start=(dc == 0), stop=(dc == NCH - 1)), "T")
                        r_xnsb.users.append(("T", tm))
                        qr = ch % 2
                        r_qraw[qr].acquire(C, "scalar")
                        C.wait("scalar", "T", tm)
                        se = C.inc(S.copy(qraw_l[qr][:, 0:TT], pb[qb][:, 0:TT]), "S")
                        rpb[qb].users.append(("S", se))
                        rpb[sw].acquire(C, "tensor")
                        C.wait("tensor", "S", se)
                        tsw = C.inc(Tn.matmul(pb[sw][:, 0:TT], pswap[:, :], qraw_l[qr][:, 0:TT],
                                              start=True, stop=True), "T")
                        C.wait("vector", "T", tsw)
                        V.tensor_tensor(tr1, pb[sw][:, 0:TT], sinT[:, t0:t0 + TT], op=ALU.mult)
                        V.tensor_tensor(tr2, qraw_l[qr][:, 0:TT], cosT[:, t0:t0 + TT], op=ALU.mult)
                        if kind == "q":
                            if ch == 0:
                                r_qsb[j % 2].acquire(C, "vector")
                            vr = C.inc(V.tensor_tensor(qsb3[j % 2][:, ch, :], tr1, tr2, op=ALU.add), "V")
                            r_qsb[j % 2].ready.append(("V", vr))
                        else:
                            for half in range(2):
                                kb = (2 * j + half) % 4
                                if ch == 0:
                                    r_kblk[kb].acquire(C, "vector")
                                vr = C.inc(V.tensor_tensor(
                                    kblk_l[kb][:, ch * 128:(ch + 1) * 128],
                                    tr1[:, half * 128:(half + 1) * 128],
                                    tr2[:, half * 128:(half + 1) * 128], op=ALU.add), "V")
                                r_kblk[kb].ready.append(("V", vr))
                        rpb[sw].users.append(("V", vr))
                        r_qraw[qr].users.append(("V", vr))
                for u in range(2):
                    vbi = (2 * j + u) % 4
                    r_vblk[vbi].acquire(C, "vector")
                    for hf in range(2):
                        qb = 6 + hf
                        rpb[qb].acquire(C, "tensor")
                        for dc in range(NCH):
                            tm = C.inc(Tn.matmul(
                                pb[qb][:, 0:512],
                                xn3[:, dc, u * 128:(u + 1) * 128],
                                wqkv[:, dc * 3 * D + 2 * D + hf * 512: dc * 3 * D + 2 * D + (hf + 1) * 512],
                                start=(dc == 0), stop=(dc == NCH - 1)), "T")
                        r_xnsb.users.append(("T", tm))
                        C.wait("vector", "T", tm)
                        vb3 = vblk_l[vbi].rearrange("p (h d) -> p h d", h=H)
                        vev = C.inc(V.tensor_copy(
                            vb3[:, hf * 8:(hf + 1) * 8, 0:HD],
                            pb[qb][:, 0:512].rearrange("p (h d) -> p h d", h=8)), "V")
                        rpb[qb].users.append(("V", vev))
                        r_vblk[vbi].ready.append(("V", vev))

                s_ = j // TPS
                blocks = [0] if jl == 0 else [2 * jl - 1, 2 * jl]
                if jl == TPS - 1:
                    blocks.append(15)
                for bl in blocks:
                    attn_block(s_ * 16 + bl)

                if jl >= 1:
                    st_final = wout_tile(j - 1)
                if jl == TPS - 1:
                    readback(j)
                    st_final = wout_tile(j)

            C.wait("gpsimd", "ST0", C.c.get("ST0", 0))
            C.wait("gpsimd", "ST1", C.c.get("ST1", 0))
    return nc


_BUILT = {}


def _get_nc():
    if "nc" not in _BUILT:
        nc = bass.Bass("TRN2", target_bir_lowering=False, debug=False)
        build_program(nc)
        _BUILT["nc"] = nc
    return _BUILT["nc"]


def _consts():
    if "c" in _BUILT:
        return _BUILT["c"]
    pos = np.arange(TSEQ, dtype=np.float64)
    inv = 1.0 / (10000.0 ** (np.arange(0, HD, 2, dtype=np.float64) / HD))
    fr = pos[:, None] * inv[None, :]
    cosv = np.cos(fr)
    sinv = np.sin(fr)
    cosT = np.zeros((128, TSEQ), np.float32)
    sinT = np.zeros((128, TSEQ), np.float32)
    for r in range(128):
        d = r % 64
        jj = d if d < 32 else d - 32
        cosT[r] = cosv[:, jj]
        sinT[r] = (-sinv[:, jj]) if d < 32 else sinv[:, jj]
    cosT = np.tile(cosT, (1, 2)).astype(bf16)
    sinT = np.tile(sinT, (1, 2)).astype(bf16)
    psw = np.zeros((128, 128), np.float32)
    for i in range(128):
        d = i % 64
        k = i + 32 if d < 32 else i - 32
        psw[k, i] = 1.0
    kk = np.arange(128)[:, None]
    qq = np.arange(128)[None, :]
    mneg = (kk > qq).astype(np.float32)
    mpos = (kk <= qq).astype(np.float32)
    one = np.ones((128, 128), np.float32)
    c = dict(
        d_cosT=cosT, d_sinT=sinT, d_pswap=psw.astype(bf16),
        d_ident=np.eye(128, dtype=np.float32).astype(bf16),
        d_maskint=np.concatenate([mneg, one, mpos], 1).astype(bf16),
        d_maskfirst=np.concatenate([one, mpos], 1).astype(bf16),
        d_masklast=np.concatenate([mneg, one], 1).astype(bf16),
        d_ones1=np.ones((1, 128), np.float32),
        d_mones1=np.full((1, 128), -1.0, np.float32),
        d_ones1b=np.ones((1, 128), bf16),
        d_mones1b=np.full((1, 128), -1.0, np.float32).astype(bf16),
        d_onescol=np.ones((128, 1), np.float32),
        d_onescolb=np.ones((128, 1), bf16),
        d_identf=np.eye(128, dtype=np.float32),
    )
    _BUILT["c"] = c
    return c


def kernel(L0_attn_x, L0_attn_w_qkv, L0_attn_w_out, L0_attn_b_out,
           L0_ffn_w1, L0_ffn_conv_w, L0_ffn_conv_b, L0_ffn_w2):
    x = np.ascontiguousarray(np.asarray(L0_attn_x, np.float32))
    wqkv = np.asarray(L0_attn_w_qkv, np.float32)
    wout = np.asarray(L0_attn_w_out, np.float32)
    bout = np.asarray(L0_attn_b_out, np.float32)
    w1 = np.asarray(L0_ffn_w1, np.float32)
    cw = np.asarray(L0_ffn_conv_w, np.float32)
    cb = np.asarray(L0_ffn_conv_b, np.float32)
    w2 = np.asarray(L0_ffn_w2, np.float32)

    shared = dict(
        wqkvT=np.ascontiguousarray(wqkv.T),
        woutT=np.ascontiguousarray(wout.T),
        d_boutl=np.ascontiguousarray(bout.reshape(NCH, 128).T),
        w1T=np.ascontiguousarray(w1.T),
        w2T=np.ascontiguousarray(w2.T),
        d_convw=np.ascontiguousarray(cw[:, 0, :].reshape(HCH, 128, 3).transpose(1, 2, 0).reshape(128, 96)),
        d_convb=np.ascontiguousarray(cb.reshape(HCH, 128).T),
        **_consts(),
    )
    nc = _get_nc()
    in_maps = []
    for ci in range(8):
        xc = x[2 * ci:2 * ci + 2]
        xf = np.concatenate([xc[0].T, xc[1].T], axis=1)
        m = dict(shared)
        m["xf"] = np.ascontiguousarray(xf)
        in_maps.append(m)
    res = run_bass_kernel_spmd(nc, in_maps, core_ids=list(range(8)))
    out = np.empty((16, TSEQ, D), np.float32)
    for ci in range(8):
        of = res.results[ci]["outf"]
        out[2 * ci] = of[:, :TSEQ].T
        out[2 * ci + 1] = of[:, TSEQ:].T
    return out


# version 09
# speedup vs baseline: 1.0577x; 1.0405x over previous
import numpy as np
import ml_dtypes
import concourse.bass as bass
import concourse.mybir as mybir
import concourse.bass_utils as _bass_utils
from concourse.bass_utils import run_bass_kernel_spmd

_orig_run_command = _bass_utils.run_command


def _run_command_ldwopt(cmd, *a, **k):
    cmd = ["--enable-ldw-opt=true" if c == "--enable-ldw-opt=false" else c for c in cmd]
    return _orig_run_command(cmd, *a, **k)


_bass_utils.run_command = _run_command_ldwopt

F32 = mybir.dt.float32
BF16 = mybir.dt.bfloat16
I8 = mybir.dt.int8
F8 = mybir.dt.float8e4
ALU = mybir.AluOpType
ACT = mybir.ActivationFunctionType
AX = mybir.AxisListType
bf16 = ml_dtypes.bfloat16

D = 1024
H = 16
HD = 64
TSEQ = 2048
T = 4096
DH = 4096
NCH = 8
HCH = 32
TT = 256
NTILE = 16
TPS = 8
TTH = TT + 2
EPS = 1e-5
RMAGIC = 12582912.0


class Ctr:
    def __init__(self, nc, sems):
        self.c = {}
        self.lastw = {}
        self.eng = {"sync": nc.sync, "gpsimd": nc.gpsimd, "tensor": nc.tensor,
                    "vector": nc.vector, "scalar": nc.scalar}
        self.sems = sems

    def inc(self, inst, sem, n=1):
        inst.then_inc(self.sems[sem], n)
        self.c[sem] = self.c.get(sem, 0) + n
        return self.c[sem]

    def wait(self, ename, sem, val):
        if val is None or val <= 0:
            return
        k = (ename, sem)
        if self.lastw.get(k, 0) >= val:
            return
        self.lastw[k] = val
        self.eng[ename].wait_ge(self.sems[sem], val)


def _ap_span(ap):
    try:
        t = ap.tensor
        name = t.name
        esz = mybir.dt.size(ap.dtype)
        pairs = list(ap.ap)
        p0 = int(getattr(t, "base_partition", 0) or 0)
        np_ = int(pairs[0][1]) if pairs else 128
        off = int(ap.offset)
        ext = 0
        for (st, ct) in pairs[1:]:
            ext += abs(int(st)) * (int(ct) - 1)
        lo = off * esz
        hi = (off + ext + 1) * esz
        return (name, p0, p0 + np_, lo, hi)
    except Exception:
        return ("?", 0, 128, 0, 1 << 40)


def _overlap(a, b):
    return (a[0] == b[0] and a[1] < b[2] and b[1] < a[2]
            and a[3] < b[4] and b[3] < a[4])


class DrainWrap:
    _OPS = ("tensor_tensor", "tensor_scalar", "tensor_add", "tensor_sub", "tensor_mul",
            "tensor_max", "tensor_copy", "tensor_single_scalar", "tensor_scalar_add",
            "tensor_scalar_sub", "tensor_scalar_mul", "tensor_scalar_max", "tensor_scalar_min",
            "tensor_reduce", "reduce_max", "reduce_sum", "scalar_tensor_tensor",
            "reciprocal", "memset", "memzero", "activation", "copy", "add", "mul",
            "square", "sqrt", "sign", "select", "transpose", "bn_stats", "bn_aggr")

    def __init__(self, eng):
        self._e = eng
        self._w = []
        self._r = []

    def __getattr__(self, attr):
        f = getattr(self._e, attr)
        if attr in DrainWrap._OPS:
            def wrapped(*a, _f=f, **k):
                aps = [x for x in list(a) + list(k.values())
                       if hasattr(x, "tensor") and hasattr(x, "ap")]
                if aps:
                    wr = [_ap_span(aps[0])]
                    ao = k.get("accum_out")
                    if ao is not None and hasattr(ao, "tensor"):
                        wr.append(_ap_span(ao))
                    rd = [_ap_span(x) for x in aps[1:]]
                    haz = (any(_overlap(x, y) for x in rd + wr for y in self._w)
                           or any(_overlap(x, y) for x in wr for y in self._r))
                    if haz:
                        self._e.drain()
                        self._w = []
                        self._r = []
                    self._w.extend(wr)
                    self._r.extend(rd)
                return _f(*a, **k)
            return wrapped
        return f


class Res:
    def __init__(self, name=""):
        self.name = name
        self.ready = []
        self.users = []

    def acquire(self, C, ename):
        if isinstance(ename, str):
            ename = (ename,)
        for e in ename:
            for (s, v) in self.users:
                C.wait(e, s, v)
        self.users = []
        self.ready = []

    def use(self, C, ename):
        for (s, v) in self.ready:
            C.wait(ename, s, v)


def build_program(nc, do_c=True, nb=NTILE, do_b=True, bstop="full"):
    DI = lambda n, sh, dt=F32: nc.dram_tensor(n, sh, dt, kind="ExternalInput").ap()
    d_xf = DI("xf", [D, T])
    d_wqkvT = DI("wqkvT", [D, 3 * D])
    d_woutT = DI("woutT", [D, D])
    d_boutl = DI("d_boutl", [128, NCH])
    d_w1T = DI("w1T", [D, DH])
    d_w2T = DI("w2T", [DH, D])
    d_convw = DI("d_convw", [128, 3 * HCH])
    d_convb = DI("d_convb", [128, HCH])
    d_cos = DI("d_cosT", [128, T], BF16)
    d_sin = DI("d_sinT", [128, T], BF16)
    d_pswap = DI("d_pswap", [128, 128], BF16)
    d_ident = DI("d_ident", [128, 128], BF16)
    d_maskint = DI("d_maskint", [128, 384], BF16)
    d_maskfirst = DI("d_maskfirst", [128, 256], BF16)
    d_masklast = DI("d_masklast", [128, 256], BF16)
    d_ones1 = DI("d_ones1", [1, 128])
    d_mones1 = DI("d_mones1", [1, 128])
    d_ones1b = DI("d_ones1b", [1, 128], BF16)
    d_mones1b = DI("d_mones1b", [1, 128], BF16)
    d_onescol = DI("d_onescol", [128, 1])
    d_onescolb = DI("d_onescolb", [128, 1], BF16)
    d_identf = DI("d_identf", [128, 128])
    d_out = nc.dram_tensor("outf", [D, T], F32, kind="ExternalOutput").ap()

    import contextlib
    with contextlib.ExitStack() as stk:
        SB = lambda n, sh, dt: stk.enter_context(nc.sbuf_tensor(n, sh, dt))
        PS = lambda n, sh, dt: stk.enter_context(nc.psum_tensor(n, sh, dt))

        xtile = [SB(f"xtile{i}", [128, NCH * TTH], F32) for i in range(2)]
        outsb = [SB(f"outsb{i}", [128, NCH * TT], F32) for i in range(2)]
        mr_rows = SB("mr_rows", [1, T], BF16)
        rstd_rows = SB("rstd_rows", [1, T], BF16)
        ones1b = SB("ones1b", [1, 128], BF16)
        mones1b = SB("mones1b", [1, 128], BF16)
        ones1 = SB("ones1", [1, 128], F32)
        mones1 = SB("mones1", [1, 128], F32)
        onescol = SB("onescol", [128, 1], F32)
        onescolb = SB("onescolb", [128, 1], BF16)
        identf = SB("identf", [128, 128], F32)
        boutl = SB("boutl", [128, NCH], F32)
        convw = SB("convw", [128, 3 * HCH], F32)
        convb = SB("convb", [128, HCH], F32)
        m127 = SB("m127", [1, 2], F32)
        mw2col = SB("mw2col", [128, 1], F32)
        scw1col = SB("scw1col", [128, 1], F32)
        scw2col = SB("scw2col", [128, 1], F32)
        ident = SB("ident", [128, 128], BF16)
        pswap = SB("pswap", [128, 128], BF16)
        maskint = SB("maskint", [128, 384], BF16)
        maskfirst = maskint[:, 128:384]
        masklast = maskint[:, 0:256]
        s2f2 = SB("s2f2", [128, 2], F32)
        fmx32 = SB("fmx32", [128, TTH], F32)
        fmn32 = SB("fmn32", [128, TTH], F32)
        gsc = SB("gsc", [128, 2 * TT], BF16)
        reccol2 = SB("reccol2", [128, 1], F32)
        f2row = SB("f2row", [1, 128], F32)
        s2row = SB("s2row", [1, 128], F32)
        sxrow = SB("sxrow", [1, TTH], F32)
        reccol = SB("reccol", [128, 1], F32)
        rowbuf = SB("rowbuf", [1, 8 * TTH], F32)
        s2b = SB("s2b", [128, TT], F32)
        f2b = SB("f2b", [128, TT], BF16)

        WA = SB("warena", [128, 81920], I8)
        w1s = WA[:, 0:NCH * DH].bitcast(F8)
        w2s = WA[:, NCH * DH:NCH * DH + HCH * D].bitcast(F8)
        wqkv = WA[:, 0:2 * NCH * 3 * D].bitcast(BF16)
        woutw = WA[:, 49152:49152 + 2 * NCH * D].bitcast(BF16)
        cosT = WA[:, 65536:65536 + 2 * T].bitcast(BF16)
        sinT = WA[:, 73728:73728 + 2 * T].bitcast(BF16)
        xs2 = WA[:, 65536:65536 + NCH * TTH * 2].bitcast(BF16)
        xtile3 = WA[:, 65536 + NCH * TTH * 2:65536 + NCH * TTH * 2 + NCH * TTH * 4].bitcast(F32)

        ARENA_SZ = 61824
        AR = SB("arena", [128, ARENA_SZ], I8)

        def carve(off, n, dt):
            size = {F32: 4, BF16: 2, I8: 1, F8: 1}[dt]
            off = (off + 3) // 4 * 4
            ap = AR[:, off:off + n * size]
            return (ap.bitcast(dt) if dt != I8 else ap), off + n * size

        o = 0
        xs, o = carve(o, NCH * TTH, BF16)
        xc_ctr, o = carve(o, NCH * TTH, F32)
        h1_l = []
        for i in range(2):
            a, o = carve(o, HCH * TTH, BF16)
            h1_l.append(a)
        h2, o = carve(o, HCH * TT, BF16)
        assert o <= ARENA_SZ, o
        wtmp = [AR[:, 0:16384].bitcast(F32), AR[:, 16384:32768].bitcast(F32)]
        wq8 = AR[:, 32768:36864]
        wabs_l = [AR[:, 36864:45056].bitcast(BF16), AR[:, 45056 - 8192 + 8192:53248].bitcast(BF16)]
        o = 0
        qsb_l = []
        for i in range(2):
            a, o = carve(o, NCH * TT, BF16)
            qsb_l.append(a)
        kblk_l = []
        for i in range(4):
            a, o = carve(o, NCH * 128, BF16)
            kblk_l.append(a)
        vblk_l = []
        for i in range(4):
            a, o = carve(o, H * (HD + 1), BF16)
            vblk_l.append(a)
        attnT_l = []
        for i in range(2):
            a, o = carve(o, NCH * TT, BF16)
            attnT_l.append(a)
        qraw_l = []
        for i in range(2):
            a, o = carve(o, TT, BF16)
            qraw_l.append(a)
        tr1, o = carve(o, TT, BF16)
        tr2, o = carve(o, TT, BF16)
        xnsb, o = carve(o, NCH * TT, BF16)
        expsb_l = []
        for i in range(2):
            a, o = carve(o, 384, BF16)
            expsb_l.append(a)
        atok_l = []
        for i in range(2):
            a, o = carve(o, H * HD, BF16)
            atok_l.append(a)
        assert o <= ARENA_SZ, o

        pb = [PS(f"pb{i}", [128, 512], F32) for i in range(8)]
        pbT = [pb[i][:, :].bitcast(BF16) for i in range(8)]
        rpb = [Res(f"pb{i}") for i in range(8)]

        sems = {}
        for s in ("L", "LX0", "LX1", "LX2", "LW0", "LW1", "T", "V", "S", "G", "ST0", "ST1"):
            sems[s] = stk.enter_context(nc.semaphore(s))

        with nc.Block() as _block:
            C = Ctr(nc, sems)
            Tn, Sy = nc.tensor, nc.sync
            G = DrainWrap(nc.gpsimd)
            V = DrainWrap(nc.vector)
            S = DrainWrap(nc.scalar)

            r_xtile = [Res("xt0"), Res("xt1")]
            r_wabs = [Res("wa0"), Res("wa1")]
            r_outsb = [Res("ou0"), Res("ou1")]
            r_wtmp = [Res("wt0"), Res("wt1")]
            r_qsb = [Res("q0"), Res("q1")]
            r_kblk = [Res(f"k{i}") for i in range(4)]
            r_vblk = [Res(f"v{i}") for i in range(4)]
            r_attnT = [Res("aT0"), Res("aT1")]
            r_qraw = [Res("qr0"), Res("qr1")]
            r_atok = [Res("atok0"), Res("atok1")]
            r_expsb = [Res("expsb0"), Res("expsb1")]
            r_xs = Res("xs")
            r_xc = Res("xc")
            r_s2b = Res("s2b")
            r_gsc = Res("gsc")
            r_h1b = [Res("h1a"), Res("h1b")]
            r_h2 = Res("h2")
            r_fold = Res("fold")
            r_xnsb = Res("xnsb")

            def dma_in(out_ap, in_ap, sem="L"):
                return C.inc(Sy.dma_start(out=out_ap, in_=in_ap), sem, 16)

            for (dst, src) in [(ones1, d_ones1), (mones1, d_mones1), (onescol, d_onescol),
                               (boutl, d_boutl), (convw, d_convw), (convb, d_convb),
                               (ident, d_ident), (identf, d_identf), (onescolb, d_onescolb),
                               (ones1b, d_ones1b), (mones1b, d_mones1b),
                               (pswap, d_pswap), (maskint, d_maskint)]:
                l_consts = dma_in(dst[:, :], src[:, :])

            C.wait("tensor", "L", l_consts)
            first = True
            tsum = 0
            for wi, (d_w, nch) in enumerate(((d_w1T, NCH), (d_w2T, HCH))):
                wd = d_w.shape[1]
                for i in range(nch):
                    bi = i % 2
                    r_wtmp[bi].acquire(C, "sync")
                    lv = dma_in(wtmp[bi][:, 0:wd], d_w[i * 128:(i + 1) * 128, :], f"LW{bi}")
                    C.wait("scalar", f"LW{bi}", lv)
                    r_wabs[bi].acquire(C, "scalar")
                    sv = C.inc(S.activation(wabs_l[bi][:, 0:wd], wtmp[bi][:, 0:wd], ACT.Abs), "S")
                    r_wtmp[bi].users.append(("S", sv))
                    C.wait("tensor", "S", sv)
                    for ss in range(wd // 256):
                        last = (i == nch - 1) and (ss == wd // 256 - 1)
                        tsum = C.inc(Tn.matmul(pb[6][0:1, wi * 256:wi * 256 + 256],
                                               onescolb[:, :], wabs_l[bi][:, ss * 256:(ss + 1) * 256],
                                               start=first, stop=last), "T")
                        first = False
                    r_wabs[bi].users.append(("T", tsum))
                first = True
            C.wait("vector", "T", tsum)
            V.reduce_sum(rowbuf[0:1, 0:1], pb[6][0:1, 0:256], AX.X)
            V.reduce_sum(rowbuf[0:1, 1:2], pb[6][0:1, 256:512], AX.X)
            V.tensor_scalar(rowbuf[0:1, 2:4], rowbuf[0:1, 0:2], 1.0 / (D * DH), 1e-5,
                            op0=ALU.mult, op1=ALU.max)
            V.tensor_scalar(rowbuf[0:1, 4:6], rowbuf[0:1, 2:4], 1.0 / 127.0, None, op0=ALU.mult)
            vv = C.inc(V.tensor_copy(m127[:, :], rowbuf[0:1, 4:6]), "V")
            C.wait("tensor", "V", vv)
            tv = C.inc(Tn.matmul(pb[7][:, 0:4], ones1[:, :], rowbuf[0:1, 2:6], start=True, stop=True), "T")
            C.wait("vector", "T", tv)
            V.reciprocal(scw1col[:, :], pb[7][:, 0:1])
            V.reciprocal(scw2col[:, :], pb[7][:, 1:2])
            v_scales = C.inc(V.tensor_copy(mw2col[:, :], pb[7][:, 3:4]), "V")
            rpb[6].users.append(("V", v_scales))
            rpb[7].users.append(("V", v_scales))

            vq = 0
            for (d_w, nch, dst, sc) in ((d_w1T, NCH, w1s, scw1col), (d_w2T, HCH, w2s, scw2col)):
                wd = d_w.shape[1]
                for i in range(nch):
                    bi = i % 2
                    r_wtmp[bi].acquire(C, "sync")
                    lv = dma_in(wtmp[bi][:, 0:wd], d_w[i * 128:(i + 1) * 128, :], f"LW{bi}")
                    C.wait("vector", f"LW{bi}", lv)
                    C.wait("vector", "V", v_scales)
                    V.tensor_scalar(wtmp[bi][:, 0:wd], wtmp[bi][:, 0:wd], sc[:, :], 1.49,
                                    op0=ALU.mult, op1=ALU.min)
                    V.tensor_scalar(wtmp[bi][:, 0:wd], wtmp[bi][:, 0:wd], -1.49, RMAGIC,
                                    op0=ALU.max, op1=ALU.add)
                    vq = C.inc(V.tensor_scalar(dst[:, i * wd:(i + 1) * wd], wtmp[bi][:, 0:wd],
                                               -RMAGIC, None, op0=ALU.add), "V")
                    r_wtmp[bi].users.append(("V", vq))
            v_signs = vq

            x3 = [xtile[i][:, :].rearrange("p (c w) -> p c w", c=NCH) for i in range(2)]
            xc3 = xc_ctr.rearrange("p (c w) -> p c w", c=NCH)
            xs3 = xs.rearrange("p (c w) -> p c w", c=NCH)
            RS = lambda k: rowbuf[0:1, k * TTH:(k + 1) * TTH]

            v_rows = 0
            xtile_b = [xtile[0][:, :], xtile[1][:, :], xtile3]
            r_xtb = [r_xtile[0], r_xtile[1], Res("xt3")]
            x3b = [a.rearrange("p (c w) -> p c w", c=NCH) for a in xtile_b]
            xs_l = [xs, xs2]
            xsl3 = [a.rearrange("p (c w) -> p c w", c=NCH) for a in xs_l]
            r_xsl = [r_xs, Res("xs2")]
            for i in range(3):
                vz0 = C.inc(V.memset(xtile_b[i], 0.0), "V")
                r_xtb[i].users.append(("V", vz0))
            bst = {}

            def b_p1(it):
                t0 = it * TT
                s0 = (it // TPS) * TSEQ
                sL = (t0 - 1) >= s0
                sR = (t0 + TT + 1) <= s0 + TSEQ
                g0 = t0 - 1 if sL else t0
                g1 = t0 + TT + 1 if sR else t0 + TT
                W = g1 - g0
                a0 = 0 if sL else 1
                xb = xtile_b[it % 3]
                xc = x3b[it % 3]
                rx = r_xtb[it % 3]
                xs_ = xs_l[it % 2]
                xs3_ = xsl3[it % 2]
                rxs_ = r_xsl[it % 2]

                rx.acquire(C, "sync")
                lxs = f"LX{it % 3}"
                lx = dma_in(
                    xb.rearrange("p (c w) -> p c w", c=NCH)[:, :, a0:a0 + W],
                    d_xf[:, g0:g1].rearrange("(c p) w -> p c w", p=128), lxs)

                rpb[6].acquire(C, "tensor")
                rpb[7].acquire(C, "tensor")
                C.wait("tensor", lxs, lx)
                for c in range(NCH):
                    C.inc(Tn.matmul(pb[6][0:1, 0:TTH], onescol[:, :], xc[:, c, 0:TTH],
                                    start=(c == 0), stop=(c == NCH - 1)), "T")
                C.wait("scalar", lxs, lx)
                r_xc.acquire(C, "scalar")
                tq_slot = [0, 0]
                for c in range(NCH):
                    C.wait("scalar", "T", tq_slot[c % 2])
                    sv = C.inc(S.activation(xc3[:, c % 2, 0:TTH],
                                            xc[:, c, 0:TTH], ACT.Square), "S")
                    C.wait("tensor", "S", sv)
                    tq = C.inc(Tn.matmul(pb[7][0:1, 0:TTH], onescol[:, :],
                                         xc3[:, c % 2, 0:TTH],
                                         start=(c == 0), stop=(c == NCH - 1)), "T")
                    tq_slot[c % 2] = tq

                C.wait("scalar", "T", tq)
                sv_mean = C.inc(S.activation(RS(0), pb[6][0:1, 0:TTH], ACT.Copy, scale=1.0 / D), "S")
                rpb[6].users.append(("S", sv_mean))
                sv = C.inc(S.activation(RS(1), pb[7][0:1, 0:TTH], ACT.Copy, scale=1.0 / D), "S")
                rpb[7].users.append(("S", sv))
                sv2 = C.inc(S.activation(RS(2), RS(0), ACT.Square), "S")
                C.wait("vector", "S", sv2)
                vv = C.inc(V.tensor_sub(RS(1), RS(1), RS(2)), "V")
                vv = C.inc(V.tensor_scalar(RS(1), RS(1), EPS, None, op0=ALU.add), "V")
                C.wait("scalar", "V", vv)
                sv = C.inc(S.activation(RS(2), RS(1), ACT.Ln), "S")
                sv_rstd = C.inc(S.activation(RS(3), RS(2), ACT.Exp, scale=-0.5), "S")
                C.wait("vector", "S", sv_rstd)
                v_mr = C.inc(V.tensor_mul(RS(2), RS(0), RS(3)), "V")

                rpb[6].acquire(C, "tensor")
                C.wait("tensor", "S", sv_mean)
                t_meanB = C.inc(Tn.matmul(pb[6][:, 0:TTH], ones1[:, :], RS(0), start=True, stop=True), "T")
                C.wait("vector", "T", t_meanB)
                for c in range(NCH):
                    C.inc(V.tensor_tensor(xc3[:, c, 0:TTH], xc[:, c, 0:TTH], pb[6][:, 0:TTH],
                                          op=ALU.subtract), "V")
                    if c == 0:
                        V.tensor_copy(fmx32[:, 0:TTH], xc3[:, c, 0:TTH])
                        vv = C.inc(V.tensor_copy(fmn32[:, 0:TTH], xc3[:, c, 0:TTH]), "V")
                    else:
                        V.tensor_tensor(fmx32[:, 0:TTH], fmx32[:, 0:TTH], xc3[:, c, 0:TTH],
                                        op=ALU.max)
                        vv = C.inc(V.tensor_tensor(fmn32[:, 0:TTH], fmn32[:, 0:TTH], xc3[:, c, 0:TTH],
                                                   op=ALU.min), "V")
                V.tensor_scalar(fmn32[:, 0:TTH], fmn32[:, 0:TTH], -1.0, None, op0=ALU.mult)
                v_fold = C.inc(V.tensor_tensor(fmx32[:, 0:TTH], fmx32[:, 0:TTH], fmn32[:, 0:TTH],
                                               op=ALU.max), "V")
                rpb[6].users.append(("V", v_fold))

                rpb[7].acquire(C, "tensor")
                C.wait("tensor", "V", v_fold)
                cuts = [0, 128, TTH - 128]
                ttr = [C.inc(Tn.transpose(pb[7][:, j * 128:(j + 1) * 128], fmx32[:, cu:cu + 128],
                                          identf[:, :]), "T") for j, cu in enumerate(cuts)]
                for j, cu in enumerate(cuts):
                    C.wait("vector", "T", ttr[-1])
                    vv = C.inc(V.reduce_max(reccol[:, :], pb[7][:, j * 128:(j + 1) * 128], AX.X), "V")
                    C.wait("tensor", "V", vv)
                    tv = C.inc(Tn.matmul(pb[6][0:1, 300:428], reccol[:, :], identf[:, :],
                                         start=True, stop=True), "T")
                    C.wait("vector", "T", tv)
                    vv = C.inc(V.tensor_copy(sxrow[0:1, cu:cu + 128], pb[6][0:1, 300:428]), "V")
                rpb[6].users.append(("V", vv))
                rpb[7].users.append(("V", vv))

                vv = C.inc(V.tensor_mul(RS(4), sxrow[0:1, 0:TTH], RS(3)), "V")
                vv = C.inc(V.tensor_scalar(RS(4), RS(4), 1e-5, None, op0=ALU.max), "V")
                vv = C.inc(V.reciprocal(RS(5), RS(4)), "V")
                vv = C.inc(V.tensor_scalar(RS(5), RS(5), 127.0, None, op0=ALU.mult), "V")
                vv_a = C.inc(V.tensor_mul(RS(6), RS(3), RS(5)), "V")
                vv_f1 = C.inc(V.tensor_scalar(RS(7), RS(4), m127[0:1, 0:1], None, op0=ALU.mult), "V")
                C.inc(V.tensor_copy(rstd_rows[0:1, t0:t0 + TT], RS(3)[0:1, 1:257]), "V")
                vr = C.inc(V.tensor_copy(mr_rows[0:1, t0:t0 + TT], RS(2)[0:1, 1:257]), "V")

                rpb[7].acquire(C, "tensor")
                C.wait("tensor", "V", vv_a)
                t_aB = C.inc(Tn.matmul(pb[7][:, 0:TTH], ones1[:, :], RS(6), start=True, stop=True), "T")
                rpb[6].acquire(C, "tensor")
                C.wait("tensor", "V", vv_f1)
                t_f1B = C.inc(Tn.matmul(pb[6][:, 0:TTH], ones1[:, :], RS(7), start=True, stop=True), "T")
                C.wait("vector", "T", t_aB)
                rxs_.acquire(C, "vector")
                V.tensor_tensor(
                    xc3[:, :, :], xc3[:, :, :],
                    pb[7][:, 0:TTH].unsqueeze(1).broadcast_to((128, NCH, TTH)), op=ALU.mult)
                vv = C.inc(V.tensor_scalar(xs3_[:, :, :], xc3[:, :, :], RMAGIC, -RMAGIC,
                                           op0=ALU.add, op1=ALU.add), "V")
                rpb[7].users.append(("V", vv))
                r_xc.users.append(("V", vv))
                C.wait("vector", "T", t_f1B)
                v_xs = C.inc(V.tensor_tensor(
                    xs3_[:, :, :], xs3_[:, :, :],
                    pb[6][:, 0:TTH].unsqueeze(1).broadcast_to((128, NCH, TTH)), op=ALU.mult), "V")
                rpb[6].users.append(("V", v_xs))
                if not sL:
                    v_xs = C.inc(V.memset(xs3_[:, :, 0:1], 0.0), "V")
                if not sR:
                    v_xs = C.inc(V.memset(xs3_[:, :, TTH - 1:TTH], 0.0), "V")
                bst[it] = {"v_xs": v_xs, "v_rows": vr}

            def b_p2(it):
                st_ = bst[it]
                xs_ = xs_l[it % 2]
                rxs_ = r_xsl[it % 2]
                C.wait("tensor", "V", st_["v_xs"])
                C.wait("tensor", "V", v_signs)
                h1b = h1_l[it % 2]
                rh = r_h1b[it % 2]
                rh.acquire(C, ("vector", "scalar"))
                h1ev = []
                for cc in range(HCH):
                    bk = 2 + cc % 4
                    rpb[bk].acquire(C, "tensor")
                    for dc in range(NCH):
                        tm = C.inc(Tn.matmul(pb[bk][:, 0:TTH],
                                             w1s[:, dc * DH + cc * 128: dc * DH + cc * 128 + 128],
                                             xs_[:, dc * TTH:(dc + 1) * TTH],
                                             start=(dc == 0), stop=(dc == NCH - 1)), "T")
                    if cc % 2 == 0:
                        C.wait("vector", "T", tm)
                        se = C.inc(V.tensor_copy(h1b[:, cc * TTH:(cc + 1) * TTH], pb[bk][:, 0:TTH]), "V")
                        rpb[bk].users.append(("V", se))
                        h1ev.append(("V", se))
                    else:
                        C.wait("scalar", "T", tm)
                        se = C.inc(S.copy(h1b[:, cc * TTH:(cc + 1) * TTH], pb[bk][:, 0:TTH]), "S")
                        rpb[bk].users.append(("S", se))
                        h1ev.append(("S", se))
                rxs_.users.append(("T", tm))
                st_["h1ev"] = h1ev

            def b_p3(it):
                st_ = bst[it]
                h1ev = st_["h1ev"]
                h1b = h1_l[it % 2]
                rh = r_h1b[it % 2]
                r_h2.acquire(C, ("vector", "scalar", "gpsimd"))
                r_gsc.acquire(C, "gpsimd")
                sI = {}
                vA = {}
                gB = {}
                silu_done = [None] * HCH
                for it2 in range(HCH + 7):
                    cc = it2
                    if cc < HCH:
                        C.wait("scalar", h1ev[cc][0], h1ev[cc][1])
                        z = h2[:, cc * TT:(cc + 1) * TT]
                        sI[cc] = C.inc(S.activation(z, h1b[:, cc * TTH + 0: cc * TTH + TT],
                                                    ACT.Identity, scale=convw[:, cc:cc + 1],
                                                    bias=convb[:, cc:cc + 1]), "S")
                    c1 = it2 - 4
                    if 0 <= c1 < HCH:
                        C.wait("vector", "S", sI[c1])
                        C.wait("vector", "G", gB[c1])
                        C.wait("vector", h1ev[c1][0], h1ev[c1][1])
                        z = h2[:, c1 * TT:(c1 + 1) * TT]
                        V.scalar_tensor_tensor(z, h1b[:, c1 * TTH + 1: c1 * TTH + 1 + TT],
                                               convw[:, 32 + c1:32 + c1 + 1], z,
                                               op0=ALU.mult, op1=ALU.add)
                        g = gsc[:, (c1 % 2) * TT:(c1 % 2) * TT + TT]
                        vA[c1] = C.inc(V.tensor_tensor(z, z, g, op=ALU.add), "V")
                    c2 = it2 - 2
                    if 0 <= c2 < HCH:
                        g = gsc[:, (c2 % 2) * TT:(c2 % 2) * TT + TT]
                        C.wait("gpsimd", h1ev[c2][0], h1ev[c2][1])
                        if c2 - 2 in vA:
                            C.wait("gpsimd", "V", vA[c2 - 2])
                        gB[c2] = C.inc(G.tensor_tensor(
                            g, h1b[:, c2 * TTH + 2: c2 * TTH + 2 + TT],
                            convw[:, 64 + c2:64 + c2 + 1].broadcast_to((128, TT)),
                            op=ALU.mult), "G")
                    c3 = it2 - 7
                    if 0 <= c3 < HCH and c3 % 4 == 3:
                        C.wait("scalar", "V", vA[c3])
                        zb = h2[:, (c3 - 3) * TT:(c3 + 1) * TT]
                        sz = C.inc(S.activation(zb, zb, ACT.Silu), "S")
                        for c4 in range(c3 - 3, c3 + 1):
                            silu_done[c4] = sz
                rh.users.append(("V", vA[HCH - 1]))
                rh.users.append(("S", sI[HCH - 1]))
                r_gsc.users.append(("V", vA[HCH - 2]))
                r_gsc.users.append(("V", vA[HCH - 1]))

                fx3 = h1b[:, 0:16 * TT].rearrange("p (c t) -> p c t", c=16)
                fn3 = h1b[:, 16 * TT:32 * TT].rearrange("p (c t) -> p c t", c=16)
                h23v = h2.rearrange("p (c t) -> p c t", c=HCH)
                C.wait("vector", "S", silu_done[HCH - 1])
                V.tensor_tensor(fx3[:, :, :], h23v[:, 0:16, :], h23v[:, 16:32, :], op=ALU.max)
                V.tensor_tensor(fn3[:, :, :], h23v[:, 0:16, :], h23v[:, 16:32, :], op=ALU.min)
                for lv in (8, 4, 2, 1):
                    V.tensor_tensor(fx3[:, 0:lv, :], fx3[:, 0:lv, :], fx3[:, lv:2 * lv, :], op=ALU.max)
                    V.tensor_tensor(fn3[:, 0:lv, :], fn3[:, 0:lv, :], fn3[:, lv:2 * lv, :], op=ALU.min)
                V.tensor_scalar(fn3[:, 0, :], fn3[:, 0, :], -1.0, None, op0=ALU.mult)
                vf = C.inc(V.tensor_tensor(fx3[:, 0, :], fx3[:, 0, :], fn3[:, 0, :],
                                           op=ALU.max), "V")

                rpb[0].acquire(C, "tensor")
                C.wait("tensor", "V", vf)
                tt1 = C.inc(Tn.transpose(pbT[0][:, 0:128], h1b[:, 0:128], ident[:, :]), "T")
                tt2 = C.inc(Tn.transpose(pbT[0][:, 128:256], h1b[:, 128:256], ident[:, :]), "T")
                rh.users.append(("T", tt2))
                rh.users.append(("V", vf))
                svals = []
                for j in range(2):
                    C.wait("vector", "T", tt2)
                    vv = C.inc(V.reduce_max(reccol2[:, :], pbT[0][:, j * 128:(j + 1) * 128], AX.X), "V")
                    vv = C.inc(V.tensor_scalar(s2f2[:, 0:1], reccol2[:, :], 1e-5, None, op0=ALU.max), "V")
                    vv = C.inc(V.reciprocal(s2f2[:, 1:2], s2f2[:, 0:1]), "V")
                    V.tensor_scalar(s2f2[:, 1:2], s2f2[:, 1:2], 127.0, None, op0=ALU.mult)
                    vv = C.inc(V.tensor_scalar(s2f2[:, 0:1], s2f2[:, 0:1], mw2col[:, :], None,
                                               op0=ALU.mult), "V")
                    C.wait("tensor", "V", vv)
                    tv = C.inc(Tn.matmul(pb[0][0:1, 300:428], s2f2[:, 0:1], identf[:, :],
                                         start=True, stop=True), "T")
                    tv = C.inc(Tn.matmul(pb[0][0:1, 172:300], s2f2[:, 1:2], identf[:, :],
                                         start=True, stop=True), "T")
                    C.wait("vector", "T", tv)
                    vv = C.inc(V.tensor_copy(f2row[0:1, 0:128], pb[0][0:1, 300:428]), "V")
                    vv = C.inc(V.tensor_copy(s2row[0:1, 0:128], pb[0][0:1, 172:300]), "V")
                    C.wait("tensor", "V", vv)
                    if j == 0:
                        rpb[1].acquire(C, "tensor")
                    else:
                        C.wait("tensor", "S", svals[-1])
                    tb = C.inc(Tn.matmul(pb[1][:, j * 128:(j + 1) * 128],
                                         ones1[:, :], s2row[0:1, 0:128], start=True, stop=True), "T")
                    tb2 = C.inc(Tn.matmul(pb[1][:, 256 + j * 128:256 + (j + 1) * 128],
                                          ones1[:, :], f2row[0:1, 0:128], start=True, stop=True), "T")
                    C.wait("scalar", "T", tb2)
                    if j == 0:
                        r_s2b.acquire(C, "scalar")
                    svals.append(C.inc(S.copy(s2b[:, j * 128:(j + 1) * 128],
                                              pb[1][:, j * 128:(j + 1) * 128]), "S"))
                    svals.append(C.inc(S.copy(f2b[:, j * 128:(j + 1) * 128],
                                              pb[1][:, 256 + j * 128:256 + (j + 1) * 128]), "S"))
                rpb[0].users.append(("V", vv))
                rpb[1].users.append(("S", svals[-1]))

                C.wait("vector", "S", svals[-1])
                h23 = h2.rearrange("p (c t) -> p c t", c=HCH)
                V.tensor_tensor(h23[:, :, :], h23[:, :, :],
                                s2b[:, :].unsqueeze(1).broadcast_to((128, HCH, TT)),
                                op=ALU.mult)
                v_hq = C.inc(V.tensor_scalar(h23[:, :, :], h23[:, :, :], RMAGIC, -RMAGIC,
                                             op0=ALU.add, op1=ALU.add), "V")
                st_["v_hq"] = v_hq
                r_h2.users.append(("V", v_hq))
                rh.users.append(("T", None))
                rh.users.pop()

            def b_p45(it):
                st_ = bst.pop(it)
                t0 = it * TT
                xc = x3b[it % 3]
                rx = r_xtb[it % 3]
                C.wait("tensor", "V", st_["v_hq"])
                ob = outsb[it % 2]
                ro = r_outsb[it % 2]
                ro.acquire(C, "vector")
                for oc in (0, 2, 4, 6, 1, 3, 5, 7):
                    bk = oc // 2
                    half = (oc % 2) * 256
                    rpb[bk].acquire(C, "tensor")
                    for cc in range(HCH):
                        tm = C.inc(Tn.matmul(pb[bk][:, half:half + TT],
                                             w2s[:, cc * D + oc * 128: cc * D + oc * 128 + 128],
                                             h2[:, cc * TT:(cc + 1) * TT],
                                             start=(cc == 0), stop=(cc == HCH - 1)), "T")
                    C.wait("vector", "T", tm)
                    dst = ob[:, oc * TT:(oc + 1) * TT]
                    V.tensor_tensor(dst, pb[bk][:, half:half + TT], f2b[:, :], op=ALU.mult)
                    ve = C.inc(V.scalar_tensor_tensor(dst, xc[:, oc, 1:257], 1.0, dst,
                                                      op0=ALU.mult, op1=ALU.add), "V")
                    rpb[bk].users.append(("V", ve))
                    C.wait("scalar", "V", ve)
                    ve2 = C.inc(S.activation(dst, dst, ACT.Identity, scale=1.0,
                                             bias=boutl[:, oc:oc + 1]), "S")
                    b_p45.sevs.append(("S", ve2))
                rx.users.append(("V", ve))
                r_s2b.users.append(("V", ve))
                r_h2.users.append(("T", tm))
                for u in b_p45.sevs:
                    C.wait("gpsimd", u[0], u[1])
                b_p45.sevs = []
                stv = C.inc(G.dma_start(out=d_out[:, t0:t0 + TT].rearrange("(c p) w -> p c w", p=128),
                                        in_=ob[:, :].rearrange("p (c w) -> p c w", c=NCH)), f"ST{it % 2}", 16)
                ro.users.append((f"ST{it % 2}", stv))
                return vr if False else st_["v_rows"]

            b_p45.sevs = []
            nb_ = nb if do_b else 0
            if nb_ > 0:
                b_p1(0)
                b_p2(0)
                b_p1(1)
                for it in range(nb_):
                    if it + 2 < nb_:
                        b_p1(it + 2)
                    if it + 1 < nb_:
                        b_p2(it + 1)
                    b_p3(it)
                    v_rows = b_p45(it)
            b_rows_done = v_rows
            if bstop != "full":
                C.inc(G.dma_start(out=d_out[:, 0:TT].rearrange("(c p) w -> p c w", p=128),
                                  in_=outsb[0][:, :].rearrange("p (c w) -> p c w", c=NCH)), "ST0", 16)
                C.wait("gpsimd", "ST0", C.c.get("ST0", 0))
                return
            if not do_c:
                if not do_b:
                    C.wait("gpsimd", "V", v_signs)
                    C.inc(G.dma_start(out=d_out[:, 0:TT].rearrange("(c p) w -> p c w", p=128),
                                      in_=outsb[0][:, :].rearrange("p (c w) -> p c w", c=NCH)), "ST0", 16)
                C.wait("gpsimd", "ST0", C.c.get("ST0", 0))
                C.wait("gpsimd", "ST1", C.c.get("ST1", 0))
                return

            last_ffn_T = C.c["T"]
            C.wait("sync", "T", last_ffn_T)
            C.wait("sync", "ST0", C.c.get("ST0", 0))
            C.wait("sync", "ST1", C.c.get("ST1", 0))
            l_tab = dma_in(cosT, d_cos[:, :])
            l_tab = dma_in(sinT, d_sin[:, :])
            for (d_w, nch, dst, wd) in ((d_wqkvT, NCH, wqkv, 3 * D), (d_woutT, NCH, woutw, D)):
                for i in range(nch):
                    bi = i % 2
                    r_wtmp[bi].acquire(C, "sync")
                    lv = dma_in(wtmp[bi][:, 0:wd], d_w[i * 128:(i + 1) * 128, :], f"LW{bi}")
                    C.wait("scalar", f"LW{bi}", lv)
                    wl = C.inc(S.copy(dst[:, i * wd:(i + 1) * wd], wtmp[bi][:, 0:wd]), "S")
                    r_wtmp[bi].users.append(("S", wl))
            s_wmm = wl

            C.wait("vector", "S", s_wmm)
            C.wait("vector", "L", l_tab)
            for b in range(4):
                vb3 = vblk_l[b].rearrange("p (h d) -> p h d", h=H)
                vinit = C.inc(V.memset(vb3[:, :, HD:HD + 1], 1.0), "V")

            qsb3 = [qsb_l[i].rearrange("p (c t) -> p c t", c=NCH) for i in range(2)]
            xn3 = xnsb.rearrange("p (c t) -> p c t", c=NCH)
            state = {"rot": 0, "trot": 0, "erot": 0}
            it_info = {}

            def blk_meta(gb):
                bl = gb % 16
                if bl == 0:
                    return [0, 1], maskfirst
                if bl == 15:
                    return [-1, 0], masklast
                return [-1, 0, 1], maskint

            def em_sc(it):
                gb, h = it
                deltas, mask = blk_meta(gb)
                jq, u = gb // 2, gb % 2
                ch, ro_ = h // 2, (h % 2) * 64
                sb_ = 0 + (state["rot"] % 2)
                pv_ = 2 + (state["rot"] % 2)
                state["rot"] += 1
                it_info[it] = ii = {"sb": sb_, "pv": pv_}
                rpb[sb_].acquire(C, "tensor")
                for ci, dl in enumerate(deltas):
                    kb = (gb + dl) % 4
                    r_kblk[kb].use(C, "tensor")
                    r_qsb[jq % 2].use(C, "tensor")
                    tsc = C.inc(Tn.matmul(
                        pb[sb_][:, ci * 128:(ci + 1) * 128],
                        kblk_l[kb][ro_:ro_ + 64, ch * 128:(ch + 1) * 128],
                        qsb3[jq % 2][ro_:ro_ + 64, ch, u * 128:(u + 1) * 128],
                        start=True, stop=True), "T")
                ii["tsc"] = tsc

            def em_exp(it):
                gb, h = it
                deltas, mask = blk_meta(gb)
                nck = len(deltas)
                ii = it_info[it]
                es = state["erot"] % 2
                state["erot"] += 1
                ii["es"] = es
                r_expsb[es].acquire(C, "scalar")
                C.wait("scalar", "T", ii["tsc"])
                se = C.inc(S.activation(expsb_l[es][:, 0:nck * 128], pb[ii["sb"]][:, 0:nck * 128],
                                        ACT.Exp, scale=0.125), "S")
                rpb[ii["sb"]].users.append(("S", se))
                ii["se"] = se

            def em_mask(it):
                gb, h = it
                deltas, mask = blk_meta(gb)
                nck = len(deltas)
                ii = it_info[it]
                es = ii["es"]
                C.wait("vector", "S", ii["se"])
                vm = C.inc(V.tensor_tensor(expsb_l[es][:, 0:nck * 128], expsb_l[es][:, 0:nck * 128],
                                           mask[:, 0:nck * 128], op=ALU.mult), "V")
                ii["vm"] = vm

            def em_pv(it):
                gb, h = it
                deltas, mask = blk_meta(gb)
                nck = len(deltas)
                ii = it_info[it]
                es = ii["es"]
                pv_ = ii["pv"]
                rpb[pv_].acquire(C, "tensor")
                C.wait("tensor", "V", ii["vm"])
                for ci, dl in enumerate(deltas):
                    vb = (gb + dl) % 4
                    r_vblk[vb].use(C, "tensor")
                    tpv = C.inc(Tn.matmul(pb[pv_][:, 0:HD + 1],
                                          expsb_l[es][:, ci * 128:(ci + 1) * 128],
                                          vblk_l[vb][:, h * (HD + 1):(h + 1) * (HD + 1)],
                                          start=(ci == 0), stop=(ci == nck - 1)), "T")
                    r_vblk[vb].users.append(("T", tpv))
                r_expsb[es].users.append(("T", tpv))
                ii["tpv"] = tpv

            def em_norm(it):
                gb, h = it
                ii = it_info[it]
                pv_ = ii["pv"]
                C.wait("vector", "T", ii["tpv"])
                if h == 0:
                    r_atok[gb % 2].acquire(C, "vector")
                V.reciprocal(reccol[:, :], pb[pv_][:, HD:HD + 1])
                va = C.inc(V.tensor_scalar(atok_l[gb % 2][:, h * 64:(h + 1) * 64], pb[pv_][:, 0:HD],
                                           reccol[:, :], None, op0=ALU.mult), "V")
                rpb[pv_].users.append(("V", va))
                ii["va"] = va

            def em_tr(gb, va):
                jq, u = gb // 2, gb % 2
                for ch in range(NCH):
                    tb_ = 4 + (state["trot"] % 2)
                    state["trot"] += 1
                    rpb[tb_].acquire(C, "tensor")
                    C.wait("tensor", "V", va)
                    ttv = C.inc(Tn.transpose(pbT[tb_][:, 0:128],
                                             atok_l[gb % 2][:, ch * 128:(ch + 1) * 128],
                                             ident[:, :]), "T")
                    r_atok[gb % 2].users.append(("T", ttv))
                    C.wait("scalar", "T", ttv)
                    sev = C.inc(S.copy(attnT_l[jq % 2][:, ch * TT + u * 128: ch * TT + u * 128 + 128],
                                       pbT[tb_][:, 0:128]), "S")
                    rpb[tb_].users.append(("S", sev))
                    r_attnT[jq % 2].ready.append(("S", sev))

            def attn_tile(blocks, s_):
                iters = [(s_ * 16 + bl, h) for bl in blocks for h in range(H)]
                pend = []
                em_sc(iters[0])
                em_sc(iters[1])
                for i, it in enumerate(iters):
                    em_exp(it)
                    em_mask(it)
                    if i + 2 < len(iters):
                        em_sc(iters[i + 2])
                    em_pv(it)
                    em_norm(it)
                    gb, h = it
                    if h == H - 1:
                        pend.append((i + 2, gb, it_info[it]["va"]))
                    while pend and pend[0][0] <= i:
                        _, g_, va_ = pend.pop(0)
                        em_tr(g_, va_)
                for _, g_, va_ in pend:
                    em_tr(g_, va_)

            def wout_tile(tj):
                t0 = tj * TT
                ob = outsb[tj % 2]
                ro = r_outsb[tj % 2]
                r_attnT[tj % 2].use(C, "tensor")
                evs = []
                tm_bank = {}
                for oc in (0, 2, 4, 6, 1, 3, 5, 7):
                    bk = oc // 2
                    half = (oc % 2) * 256
                    rpb[bk].acquire(C, "tensor")
                    for dc in range(NCH):
                        tm = C.inc(Tn.matmul(pb[bk][:, half:half + TT],
                                             woutw[:, dc * D + oc * 128: dc * D + oc * 128 + 128],
                                             attnT_l[tj % 2][:, dc * TT:(dc + 1) * TT],
                                             start=(dc == 0), stop=(dc == NCH - 1)), "T")
                    tm_bank[bk] = tm
                    if oc % 2 == 1:
                        C.wait("vector", "T", tm)
                        C.wait("vector", f"ST{tj % 2}", wout_tile.rb[tj])
                        ve = C.inc(V.tensor_tensor(ob[:, bk * 512:(bk + 1) * 512],
                                                   ob[:, bk * 512:(bk + 1) * 512],
                                                   pb[bk][:, 0:512], op=ALU.add), "V")
                        rpb[bk].users.append(("V", ve))
                        evs.append(ve)
                C.wait("gpsimd", "V", evs[-1])
                stv = C.inc(G.dma_start(out=d_out[:, t0:t0 + TT].rearrange("(c p) w -> p c w", p=128),
                                        in_=ob[:, :].rearrange("p (c w) -> p c w", c=NCH)), f"ST{tj % 2}", 16)
                ro.users.append((f"ST{tj % 2}", stv))
                r_attnT[tj % 2].users.append(("T", tm))
                return stv

            wout_tile.rb = {}

            def readback(tj):
                ob = outsb[tj % 2]
                ro = r_outsb[tj % 2]
                ro.acquire(C, "gpsimd")
                wout_tile.rb[tj] = C.inc(
                    G.dma_start(out=ob[:, :].rearrange("p (c w) -> p c w", c=NCH),
                                in_=d_out[:, tj * TT:tj * TT + TT].rearrange("(c p) w -> p c w", p=128)),
                    f"ST{tj % 2}", 16)

            st_final = 0
            for j in range(NTILE):
                jl = j % TPS
                t0 = j * TT
                xb = xtile[j % 2]
                rx = r_xtile[j % 2]

                rx.acquire(C, "sync")
                lxs = f"LX{j % 2}"
                lx = dma_in(
                    xb[:, :].rearrange("p (c w) -> p c w", c=NCH)[:, :, 1:257],
                    d_xf[:, t0:t0 + TT].rearrange("(c p) w -> p c w", p=128), lxs)
                if jl >= 1:
                    readback(j - 1)
                rpb[4].acquire(C, "tensor")
                C.wait("tensor", "V", b_rows_done)
                C.inc(Tn.matmul(pb[4][:, 0:128], ones1b[:, :], rstd_rows[0:1, t0:t0 + 128],
                                start=True, stop=True), "T")
                C.inc(Tn.matmul(pb[4][:, 128:256], ones1b[:, :], rstd_rows[0:1, t0 + 128:t0 + 256],
                                start=True, stop=True), "T")
                C.inc(Tn.matmul(pb[4][:, 256:384], mones1b[:, :], mr_rows[0:1, t0:t0 + 128],
                                start=True, stop=True), "T")
                tb2 = C.inc(Tn.matmul(pb[4][:, 384:512], mones1b[:, :], mr_rows[0:1, t0 + 128:t0 + 256],
                                      start=True, stop=True), "T")
                C.wait("vector", lxs, lx)
                C.wait("vector", "T", tb2)
                r_xnsb.acquire(C, "vector")
                V.tensor_tensor(xn3[:, :, :], x3[j % 2][:, :, 1:257],
                                pb[4][:, 0:256].unsqueeze(1).broadcast_to((128, NCH, 256)),
                                op=ALU.mult)
                v_xn = C.inc(V.tensor_tensor(xn3[:, :, :], xn3[:, :, :],
                                             pb[4][:, 256:512].unsqueeze(1).broadcast_to((128, NCH, 256)),
                                             op=ALU.add), "V")
                rpb[4].users.append(("V", v_xn))
                rx.users.append(("V", v_xn))

                C.wait("tensor", "V", v_xn)
                C.wait("tensor", "S", s_wmm)
                for kind in ("q", "k"):
                    for ch in range(NCH):
                        qb = 6 + (ch % 2)
                        sw = 4 + (ch % 2)
                        woff = (0 if kind == "q" else D) + ch * 128
                        rpb[qb].acquire(C, "tensor")
                        for dc in range(NCH):
                            tm = C.inc(Tn.matmul(pb[qb][:, 0:TT],
                                                 wqkv[:, dc * 3 * D + woff: dc * 3 * D + woff + 128],
                                                 xnsb[:, dc * TT:(dc + 1) * TT],
                                                 start=(dc == 0), stop=(dc == NCH - 1)), "T")
                        r_xnsb.users.append(("T", tm))
                        qr = ch % 2
                        r_qraw[qr].acquire(C, "scalar")
                        C.wait("scalar", "T", tm)
                        se = C.inc(S.copy(qraw_l[qr][:, 0:TT], pb[qb][:, 0:TT]), "S")
                        rpb[qb].users.append(("S", se))
                        rpb[sw].acquire(C, "tensor")
                        C.wait("tensor", "S", se)
                        tsw = C.inc(Tn.matmul(pb[sw][:, 0:TT], pswap[:, :], qraw_l[qr][:, 0:TT],
                                              start=True, stop=True), "T")
                        C.wait("vector", "T", tsw)
                        V.tensor_tensor(tr1, pb[sw][:, 0:TT], sinT[:, t0:t0 + TT], op=ALU.mult)
                        V.tensor_tensor(tr2, qraw_l[qr][:, 0:TT], cosT[:, t0:t0 + TT], op=ALU.mult)
                        if kind == "q":
                            if ch == 0:
                                r_qsb[j % 2].acquire(C, "vector")
                            vr = C.inc(V.tensor_tensor(qsb3[j % 2][:, ch, :], tr1, tr2, op=ALU.add), "V")
                            r_qsb[j % 2].ready.append(("V", vr))
                        else:
                            for half in range(2):
                                kb = (2 * j + half) % 4
                                if ch == 0:
                                    r_kblk[kb].acquire(C, "vector")
                                vr = C.inc(V.tensor_tensor(
                                    kblk_l[kb][:, ch * 128:(ch + 1) * 128],
                                    tr1[:, half * 128:(half + 1) * 128],
                                    tr2[:, half * 128:(half + 1) * 128], op=ALU.add), "V")
                                r_kblk[kb].ready.append(("V", vr))
                        rpb[sw].users.append(("V", vr))
                        r_qraw[qr].users.append(("V", vr))
                for u in range(2):
                    vbi = (2 * j + u) % 4
                    r_vblk[vbi].acquire(C, "vector")
                    for hf in range(2):
                        qb = 6 + hf
                        rpb[qb].acquire(C, "tensor")
                        for dc in range(NCH):
                            tm = C.inc(Tn.matmul(
                                pb[qb][:, 0:512],
                                xn3[:, dc, u * 128:(u + 1) * 128],
                                wqkv[:, dc * 3 * D + 2 * D + hf * 512: dc * 3 * D + 2 * D + (hf + 1) * 512],
                                start=(dc == 0), stop=(dc == NCH - 1)), "T")
                        r_xnsb.users.append(("T", tm))
                        C.wait("vector", "T", tm)
                        vb3 = vblk_l[vbi].rearrange("p (h d) -> p h d", h=H)
                        vev = C.inc(V.tensor_copy(
                            vb3[:, hf * 8:(hf + 1) * 8, 0:HD],
                            pb[qb][:, 0:512].rearrange("p (h d) -> p h d", h=8)), "V")
                        rpb[qb].users.append(("V", vev))
                        r_vblk[vbi].ready.append(("V", vev))

                s_ = j // TPS
                blocks = [0] if jl == 0 else [2 * jl - 1, 2 * jl]
                if jl == TPS - 1:
                    blocks.append(15)
                attn_tile(blocks, s_)

                if jl >= 1:
                    st_final = wout_tile(j - 1)
                if jl == TPS - 1:
                    readback(j)
                    st_final = wout_tile(j)

            C.wait("gpsimd", "ST0", C.c.get("ST0", 0))
            C.wait("gpsimd", "ST1", C.c.get("ST1", 0))
    return nc


_BUILT = {}


def _get_nc():
    if "nc" not in _BUILT:
        nc = bass.Bass("TRN2", target_bir_lowering=False, debug=False)
        build_program(nc)
        _BUILT["nc"] = nc
    return _BUILT["nc"]


def _consts():
    if "c" in _BUILT:
        return _BUILT["c"]
    pos = np.arange(TSEQ, dtype=np.float64)
    inv = 1.0 / (10000.0 ** (np.arange(0, HD, 2, dtype=np.float64) / HD))
    fr = pos[:, None] * inv[None, :]
    cosv = np.cos(fr)
    sinv = np.sin(fr)
    cosT = np.zeros((128, TSEQ), np.float32)
    sinT = np.zeros((128, TSEQ), np.float32)
    for r in range(128):
        d = r % 64
        jj = d if d < 32 else d - 32
        cosT[r] = cosv[:, jj]
        sinT[r] = (-sinv[:, jj]) if d < 32 else sinv[:, jj]
    cosT = np.tile(cosT, (1, 2)).astype(bf16)
    sinT = np.tile(sinT, (1, 2)).astype(bf16)
    psw = np.zeros((128, 128), np.float32)
    for i in range(128):
        d = i % 64
        k = i + 32 if d < 32 else i - 32
        psw[k, i] = 1.0
    kk = np.arange(128)[:, None]
    qq = np.arange(128)[None, :]
    mneg = (kk > qq).astype(np.float32)
    mpos = (kk <= qq).astype(np.float32)
    one = np.ones((128, 128), np.float32)
    c = dict(
        d_cosT=cosT, d_sinT=sinT, d_pswap=psw.astype(bf16),
        d_ident=np.eye(128, dtype=np.float32).astype(bf16),
        d_maskint=np.concatenate([mneg, one, mpos], 1).astype(bf16),
        d_maskfirst=np.concatenate([one, mpos], 1).astype(bf16),
        d_masklast=np.concatenate([mneg, one], 1).astype(bf16),
        d_ones1=np.ones((1, 128), np.float32),
        d_mones1=np.full((1, 128), -1.0, np.float32),
        d_ones1b=np.ones((1, 128), bf16),
        d_mones1b=np.full((1, 128), -1.0, np.float32).astype(bf16),
        d_onescol=np.ones((128, 1), np.float32),
        d_onescolb=np.ones((128, 1), bf16),
        d_identf=np.eye(128, dtype=np.float32),
    )
    _BUILT["c"] = c
    return c


def kernel(L0_attn_x, L0_attn_w_qkv, L0_attn_w_out, L0_attn_b_out,
           L0_ffn_w1, L0_ffn_conv_w, L0_ffn_conv_b, L0_ffn_w2):
    x = np.ascontiguousarray(np.asarray(L0_attn_x, np.float32))
    wqkv = np.asarray(L0_attn_w_qkv, np.float32)
    wout = np.asarray(L0_attn_w_out, np.float32)
    bout = np.asarray(L0_attn_b_out, np.float32)
    w1 = np.asarray(L0_ffn_w1, np.float32)
    cw = np.asarray(L0_ffn_conv_w, np.float32)
    cb = np.asarray(L0_ffn_conv_b, np.float32)
    w2 = np.asarray(L0_ffn_w2, np.float32)

    shared = dict(
        wqkvT=np.ascontiguousarray(wqkv.T),
        woutT=np.ascontiguousarray(wout.T),
        d_boutl=np.ascontiguousarray(bout.reshape(NCH, 128).T),
        w1T=np.ascontiguousarray(w1.T),
        w2T=np.ascontiguousarray(w2.T),
        d_convw=np.ascontiguousarray(cw[:, 0, :].reshape(HCH, 128, 3).transpose(1, 2, 0).reshape(128, 96)),
        d_convb=np.ascontiguousarray(cb.reshape(HCH, 128).T),
        **_consts(),
    )
    nc = _get_nc()
    in_maps = []
    for ci in range(8):
        xc = x[2 * ci:2 * ci + 2]
        xf = np.concatenate([xc[0].T, xc[1].T], axis=1)
        m = dict(shared)
        m["xf"] = np.ascontiguousarray(xf)
        in_maps.append(m)
    res = run_bass_kernel_spmd(nc, in_maps, core_ids=list(range(8)))
    out = np.empty((16, TSEQ, D), np.float32)
    for ci in range(8):
        of = res.results[ci]["outf"]
        out[2 * ci] = of[:, :TSEQ].T
        out[2 * ci + 1] = of[:, TSEQ:].T
    return out


# version 10
# speedup vs baseline: 1.1419x; 1.0797x over previous
import numpy as np
import ml_dtypes
import concourse.bass as bass
import concourse.mybir as mybir
import concourse.bass_utils as _bass_utils
from concourse.bass_utils import run_bass_kernel_spmd

_orig_run_command = _bass_utils.run_command


def _run_command_ldwopt(cmd, *a, **k):
    cmd = ["--enable-ldw-opt=true" if c == "--enable-ldw-opt=false" else c for c in cmd]
    return _orig_run_command(cmd, *a, **k)


_bass_utils.run_command = _run_command_ldwopt

F32 = mybir.dt.float32
BF16 = mybir.dt.bfloat16
I8 = mybir.dt.int8
F8 = mybir.dt.float8e4
ALU = mybir.AluOpType
ACT = mybir.ActivationFunctionType
AX = mybir.AxisListType
bf16 = ml_dtypes.bfloat16

D = 1024
H = 16
HD = 64
TSEQ = 2048
T = 4096
DH = 4096
NCH = 8
HCH = 32
TT = 256
NTILE = 16
TPS = 8
TTH = TT + 2
EPS = 1e-5
RMAGIC = 12582912.0


class Ctr:
    def __init__(self, nc, sems):
        self.c = {}
        self.lastw = {}
        self.eng = {"sync": nc.sync, "gpsimd": nc.gpsimd, "tensor": nc.tensor,
                    "vector": nc.vector, "scalar": nc.scalar}
        self.sems = sems

    def inc(self, inst, sem, n=1):
        inst.then_inc(self.sems[sem], n)
        self.c[sem] = self.c.get(sem, 0) + n
        return self.c[sem]

    def wait(self, ename, sem, val):
        if val is None or val <= 0:
            return
        k = (ename, sem)
        if self.lastw.get(k, 0) >= val:
            return
        self.lastw[k] = val
        self.eng[ename].wait_ge(self.sems[sem], val)


def _ap_span(ap):
    try:
        t = ap.tensor
        name = t.name
        esz = mybir.dt.size(ap.dtype)
        pairs = list(ap.ap)
        p0 = int(getattr(t, "base_partition", 0) or 0)
        np_ = int(pairs[0][1]) if pairs else 128
        off = int(ap.offset)
        ext = 0
        for (st, ct) in pairs[1:]:
            ext += abs(int(st)) * (int(ct) - 1)
        lo = off * esz
        hi = (off + ext + 1) * esz
        return (name, p0, p0 + np_, lo, hi)
    except Exception:
        return ("?", 0, 128, 0, 1 << 40)


def _overlap(a, b):
    return (a[0] == b[0] and a[1] < b[2] and b[1] < a[2]
            and a[3] < b[4] and b[3] < a[4])


class DrainWrap:
    _OPS = ("tensor_tensor", "tensor_scalar", "tensor_add", "tensor_sub", "tensor_mul",
            "tensor_max", "tensor_copy", "tensor_single_scalar", "tensor_scalar_add",
            "tensor_scalar_sub", "tensor_scalar_mul", "tensor_scalar_max", "tensor_scalar_min",
            "tensor_reduce", "reduce_max", "reduce_sum", "scalar_tensor_tensor",
            "reciprocal", "memset", "memzero", "activation", "copy", "add", "mul",
            "square", "sqrt", "sign", "select", "transpose", "bn_stats", "bn_aggr")

    def __init__(self, eng):
        self._e = eng
        self._w = []
        self._r = []

    def __getattr__(self, attr):
        f = getattr(self._e, attr)
        if attr in DrainWrap._OPS:
            def wrapped(*a, _f=f, **k):
                aps = [x for x in list(a) + list(k.values())
                       if hasattr(x, "tensor") and hasattr(x, "ap")]
                if aps:
                    wr = [_ap_span(aps[0])]
                    ao = k.get("accum_out")
                    if ao is not None and hasattr(ao, "tensor"):
                        wr.append(_ap_span(ao))
                    rd = [_ap_span(x) for x in aps[1:]]
                    haz = (any(_overlap(x, y) for x in rd + wr for y in self._w)
                           or any(_overlap(x, y) for x in wr for y in self._r))
                    if haz:
                        self._e.drain()
                        self._w = []
                        self._r = []
                    self._w.extend(wr)
                    self._r.extend(rd)
                return _f(*a, **k)
            return wrapped
        return f


class Res:
    def __init__(self, name=""):
        self.name = name
        self.ready = []
        self.users = []

    def acquire(self, C, ename):
        if isinstance(ename, str):
            ename = (ename,)
        for e in ename:
            for (s, v) in self.users:
                C.wait(e, s, v)
        self.users = []
        self.ready = []

    def use(self, C, ename):
        for (s, v) in self.ready:
            C.wait(ename, s, v)


def build_program(nc, do_c=True, nb=NTILE, do_b=True, bstop="full"):
    DI = lambda n, sh, dt=F32: nc.dram_tensor(n, sh, dt, kind="ExternalInput").ap()
    d_xf = DI("xf", [D, T])
    d_wqkvT = DI("wqkvT", [D, 3 * D])
    d_woutT = DI("woutT", [D, D])
    d_boutl = DI("d_boutl", [128, NCH])
    d_w1T = DI("w1T", [D, DH])
    d_w2T = DI("w2T", [DH, D])
    d_convw = DI("d_convw", [128, 3 * HCH])
    d_convb = DI("d_convb", [128, HCH])
    d_cos = DI("d_cosT", [128, T], BF16)
    d_sin = DI("d_sinT", [128, T], BF16)
    d_pswap = DI("d_pswap", [128, 128], BF16)
    d_ident = DI("d_ident", [128, 128], BF16)
    d_maskint = DI("d_maskint", [128, 384], BF16)
    d_maskfirst = DI("d_maskfirst", [128, 256], BF16)
    d_masklast = DI("d_masklast", [128, 256], BF16)
    d_ones1 = DI("d_ones1", [1, 128])
    d_mones1 = DI("d_mones1", [1, 128])
    d_ones1b = DI("d_ones1b", [1, 128], BF16)
    d_mones1b = DI("d_mones1b", [1, 128], BF16)
    d_onescol = DI("d_onescol", [128, 1])
    d_onescolb = DI("d_onescolb", [128, 1], BF16)
    d_identf = DI("d_identf", [128, 128])
    d_out = nc.dram_tensor("outf", [D, T], F32, kind="ExternalOutput").ap()

    import contextlib
    with contextlib.ExitStack() as stk:
        SB = lambda n, sh, dt: stk.enter_context(nc.sbuf_tensor(n, sh, dt))
        PS = lambda n, sh, dt: stk.enter_context(nc.psum_tensor(n, sh, dt))

        xtile = [SB(f"xtile{i}", [128, NCH * TTH], F32) for i in range(2)]
        outsb = [SB(f"outsb{i}", [128, NCH * TT], F32) for i in range(2)]
        mr_rows = SB("mr_rows", [1, T], BF16)
        rstd_rows = SB("rstd_rows", [1, T], BF16)
        ones1b = SB("ones1b", [1, 128], BF16)
        mones1b = SB("mones1b", [1, 128], BF16)
        ones1 = SB("ones1", [1, 128], F32)
        mones1 = SB("mones1", [1, 128], F32)
        onescol = SB("onescol", [128, 1], F32)
        onescolb = SB("onescolb", [128, 1], BF16)
        identf = SB("identf", [128, 128], F32)
        boutl = SB("boutl", [128, NCH], F32)
        convw = SB("convw", [128, 3 * HCH], F32)
        convb = SB("convb", [128, HCH], F32)
        m127 = SB("m127", [1, 2], F32)
        mw2col = SB("mw2col", [128, 1], F32)
        scw1col = SB("scw1col", [128, 1], F32)
        scw2col = SB("scw2col", [128, 1], F32)
        ident = SB("ident", [128, 128], BF16)
        pswap = SB("pswap", [128, 128], BF16)
        maskint = SB("maskint", [128, 384], BF16)
        maskfirst = maskint[:, 128:384]
        masklast = maskint[:, 0:256]
        s2f2 = SB("s2f2", [128, 2], F32)
        fmx32 = SB("fmx32", [128, TTH], F32)
        fmn32 = SB("fmn32", [128, TTH], F32)
        gsc = SB("gsc", [128, 2 * TT], BF16)
        reccol2 = SB("reccol2", [128, 1], F32)
        f2row = SB("f2row", [1, 128], F32)
        s2row = SB("s2row", [1, 128], F32)
        sxrow = SB("sxrow", [1, TTH], F32)
        reccol = SB("reccol", [128, 1], F32)
        rowbuf = SB("rowbuf", [1, 8 * TTH], F32)
        s2b = SB("s2b", [128, TT], F32)
        f2b = SB("f2b", [128, TT], BF16)

        WA = SB("warena", [128, 81920], I8)
        w1s = WA[:, 0:NCH * DH].bitcast(F8)
        w2s = WA[:, NCH * DH:NCH * DH + HCH * D].bitcast(F8)
        wqkv = WA[:, 0:2 * NCH * 3 * D].bitcast(BF16)
        woutw = WA[:, 49152:49152 + 2 * NCH * D].bitcast(BF16)
        cosT = WA[:, 65536:65536 + 2 * T].bitcast(BF16)
        sinT = WA[:, 73728:73728 + 2 * T].bitcast(BF16)
        xs2 = WA[:, 65536:65536 + NCH * TTH * 2].bitcast(BF16)
        xtile3 = WA[:, 65536 + NCH * TTH * 2:65536 + NCH * TTH * 2 + NCH * TTH * 4].bitcast(F32)

        ARENA_SZ = 61824
        AR = SB("arena", [128, ARENA_SZ], I8)

        def carve(off, n, dt):
            size = {F32: 4, BF16: 2, I8: 1, F8: 1}[dt]
            off = (off + 3) // 4 * 4
            ap = AR[:, off:off + n * size]
            return (ap.bitcast(dt) if dt != I8 else ap), off + n * size

        o = 0
        xs, o = carve(o, NCH * TTH, BF16)
        xc_ctr, o = carve(o, NCH * TTH, F32)
        h1_l = []
        for i in range(2):
            a, o = carve(o, HCH * TTH, BF16)
            h1_l.append(a)
        h2, o = carve(o, HCH * TT, BF16)
        assert o <= ARENA_SZ, o
        wtmp = [AR[:, 0:16384].bitcast(F32), AR[:, 16384:32768].bitcast(F32)]
        wq8 = AR[:, 32768:36864]
        wabs_l = [AR[:, 36864:45056].bitcast(BF16), AR[:, 45056 - 8192 + 8192:53248].bitcast(BF16)]
        o = 0
        qsb_l = []
        for i in range(2):
            a, o = carve(o, NCH * TT, BF16)
            qsb_l.append(a)
        kblk_l = []
        for i in range(4):
            a, o = carve(o, NCH * 128, BF16)
            kblk_l.append(a)
        vblk_l = []
        for i in range(4):
            a, o = carve(o, H * (HD + 1), BF16)
            vblk_l.append(a)
        attnT_l = []
        for i in range(2):
            a, o = carve(o, NCH * TT, BF16)
            attnT_l.append(a)
        qraw_l = []
        for i in range(2):
            a, o = carve(o, TT, BF16)
            qraw_l.append(a)
        tr1, o = carve(o, TT, BF16)
        tr2, o = carve(o, TT, BF16)
        xnsb, o = carve(o, NCH * TT, BF16)
        expsb_l = []
        for i in range(2):
            a, o = carve(o, 384, BF16)
            expsb_l.append(a)
        atok_l = []
        for i in range(2):
            a, o = carve(o, H * HD, BF16)
            atok_l.append(a)
        assert o <= ARENA_SZ, o

        pb = [PS(f"pb{i}", [128, 512], F32) for i in range(8)]
        pbT = [pb[i][:, :].bitcast(BF16) for i in range(8)]
        rpb = [Res(f"pb{i}") for i in range(8)]

        sems = {}
        for s in ("L", "LX0", "LX1", "LX2", "LW0", "LW1", "T", "V", "S", "G", "ST0", "ST1"):
            sems[s] = stk.enter_context(nc.semaphore(s))

        with nc.Block() as _block:
            C = Ctr(nc, sems)
            Tn, Sy = nc.tensor, nc.sync
            G = DrainWrap(nc.gpsimd)
            V = DrainWrap(nc.vector)
            S = DrainWrap(nc.scalar)

            r_xtile = [Res("xt0"), Res("xt1")]
            r_wabs = [Res("wa0"), Res("wa1")]
            r_outsb = [Res("ou0"), Res("ou1")]
            r_wtmp = [Res("wt0"), Res("wt1")]
            r_qsb = [Res("q0"), Res("q1")]
            r_kblk = [Res(f"k{i}") for i in range(4)]
            r_vblk = [Res(f"v{i}") for i in range(4)]
            r_attnT = [Res("aT0"), Res("aT1")]
            r_qraw = [Res("qr0"), Res("qr1")]
            r_atok = [Res("atok0"), Res("atok1")]
            r_expsb = [Res("expsb0"), Res("expsb1")]
            r_xs = Res("xs")
            r_xc = Res("xc")
            r_s2b = Res("s2b")
            r_gsc = Res("gsc")
            r_h1b = [Res("h1a"), Res("h1b")]
            r_h2 = Res("h2")
            r_fold = Res("fold")
            r_xnsb = Res("xnsb")

            def dma_in(out_ap, in_ap, sem="L"):
                return C.inc(Sy.dma_start(out=out_ap, in_=in_ap), sem, 16)

            for (dst, src) in [(ones1, d_ones1), (mones1, d_mones1), (onescol, d_onescol),
                               (boutl, d_boutl), (convw, d_convw), (convb, d_convb),
                               (ident, d_ident), (identf, d_identf), (onescolb, d_onescolb),
                               (ones1b, d_ones1b), (mones1b, d_mones1b),
                               (pswap, d_pswap), (maskint, d_maskint)]:
                l_consts = dma_in(dst[:, :], src[:, :])

            C.wait("tensor", "L", l_consts)
            first = True
            tsum = 0
            for wi, (d_w, nch) in enumerate(((d_w1T, NCH), (d_w2T, HCH))):
                wd = d_w.shape[1]
                for i in range(nch):
                    bi = i % 2
                    r_wtmp[bi].acquire(C, "sync")
                    lv = dma_in(wtmp[bi][:, 0:wd], d_w[i * 128:(i + 1) * 128, :], f"LW{bi}")
                    C.wait("scalar", f"LW{bi}", lv)
                    r_wabs[bi].acquire(C, "scalar")
                    sv = C.inc(S.activation(wabs_l[bi][:, 0:wd], wtmp[bi][:, 0:wd], ACT.Abs), "S")
                    r_wtmp[bi].users.append(("S", sv))
                    C.wait("tensor", "S", sv)
                    for ss in range(wd // 256):
                        last = (i == nch - 1) and (ss == wd // 256 - 1)
                        tsum = C.inc(Tn.matmul(pb[6][0:1, wi * 256:wi * 256 + 256],
                                               onescolb[:, :], wabs_l[bi][:, ss * 256:(ss + 1) * 256],
                                               start=first, stop=last), "T")
                        first = False
                    r_wabs[bi].users.append(("T", tsum))
                first = True
            C.wait("vector", "T", tsum)
            V.reduce_sum(rowbuf[0:1, 0:1], pb[6][0:1, 0:256], AX.X)
            V.reduce_sum(rowbuf[0:1, 1:2], pb[6][0:1, 256:512], AX.X)
            V.tensor_scalar(rowbuf[0:1, 2:4], rowbuf[0:1, 0:2], 1.0 / (D * DH), 1e-5,
                            op0=ALU.mult, op1=ALU.max)
            V.tensor_scalar(rowbuf[0:1, 4:6], rowbuf[0:1, 2:4], 1.0 / 127.0, None, op0=ALU.mult)
            vv = C.inc(V.tensor_copy(m127[:, :], rowbuf[0:1, 4:6]), "V")
            C.wait("tensor", "V", vv)
            tv = C.inc(Tn.matmul(pb[7][:, 0:4], ones1[:, :], rowbuf[0:1, 2:6], start=True, stop=True), "T")
            C.wait("vector", "T", tv)
            V.reciprocal(scw1col[:, :], pb[7][:, 0:1])
            V.reciprocal(scw2col[:, :], pb[7][:, 1:2])
            v_scales = C.inc(V.tensor_copy(mw2col[:, :], pb[7][:, 3:4]), "V")
            rpb[6].users.append(("V", v_scales))
            rpb[7].users.append(("V", v_scales))

            vq = 0
            for (d_w, nch, dst, sc) in ((d_w1T, NCH, w1s, scw1col), (d_w2T, HCH, w2s, scw2col)):
                wd = d_w.shape[1]
                for i in range(nch):
                    bi = i % 2
                    r_wtmp[bi].acquire(C, "sync")
                    lv = dma_in(wtmp[bi][:, 0:wd], d_w[i * 128:(i + 1) * 128, :], f"LW{bi}")
                    C.wait("vector", f"LW{bi}", lv)
                    C.wait("vector", "V", v_scales)
                    V.tensor_scalar(wtmp[bi][:, 0:wd], wtmp[bi][:, 0:wd], sc[:, :], 1.49,
                                    op0=ALU.mult, op1=ALU.min)
                    V.tensor_scalar(wtmp[bi][:, 0:wd], wtmp[bi][:, 0:wd], -1.49, RMAGIC,
                                    op0=ALU.max, op1=ALU.add)
                    vq = C.inc(V.tensor_scalar(dst[:, i * wd:(i + 1) * wd], wtmp[bi][:, 0:wd],
                                               -RMAGIC, None, op0=ALU.add), "V")
                    r_wtmp[bi].users.append(("V", vq))
            v_signs = vq

            x3 = [xtile[i][:, :].rearrange("p (c w) -> p c w", c=NCH) for i in range(2)]
            xc3 = xc_ctr.rearrange("p (c w) -> p c w", c=NCH)
            xs3 = xs.rearrange("p (c w) -> p c w", c=NCH)
            RS = lambda k: rowbuf[0:1, k * TTH:(k + 1) * TTH]

            v_rows = 0
            xtile_b = [xtile[0][:, :], xtile[1][:, :], xtile3]
            r_xtb = [r_xtile[0], r_xtile[1], Res("xt3")]
            x3b = [a.rearrange("p (c w) -> p c w", c=NCH) for a in xtile_b]
            xs_l = [xs, xs2]
            xsl3 = [a.rearrange("p (c w) -> p c w", c=NCH) for a in xs_l]
            r_xsl = [r_xs, Res("xs2")]
            for i in range(3):
                vz0 = C.inc(V.memset(xtile_b[i], 0.0), "V")
                r_xtb[i].users.append(("V", vz0))
            bst = {}

            def b_p1(it):
                t0 = it * TT
                s0 = (it // TPS) * TSEQ
                sL = (t0 - 1) >= s0
                sR = (t0 + TT + 1) <= s0 + TSEQ
                g0 = t0 - 1 if sL else t0
                g1 = t0 + TT + 1 if sR else t0 + TT
                W = g1 - g0
                a0 = 0 if sL else 1
                xb = xtile_b[it % 3]
                xc = x3b[it % 3]
                rx = r_xtb[it % 3]
                xs_ = xs_l[it % 2]
                xs3_ = xsl3[it % 2]
                rxs_ = r_xsl[it % 2]

                rx.acquire(C, "sync")
                lxs = f"LX{it % 3}"
                lx = dma_in(
                    xb.rearrange("p (c w) -> p c w", c=NCH)[:, :, a0:a0 + W],
                    d_xf[:, g0:g1].rearrange("(c p) w -> p c w", p=128), lxs)

                rpb[6].acquire(C, "tensor")
                rpb[7].acquire(C, "tensor")
                C.wait("tensor", lxs, lx)
                for c in range(NCH):
                    C.inc(Tn.matmul(pb[6][0:1, 0:TTH], onescol[:, :], xc[:, c, 0:TTH],
                                    start=(c == 0), stop=(c == NCH - 1)), "T")
                C.wait("scalar", lxs, lx)
                r_xc.acquire(C, "scalar")
                tq_slot = [0, 0]
                for c in range(NCH):
                    C.wait("scalar", "T", tq_slot[c % 2])
                    sv = C.inc(S.activation(xc3[:, c % 2, 0:TTH],
                                            xc[:, c, 0:TTH], ACT.Square), "S")
                    C.wait("tensor", "S", sv)
                    tq = C.inc(Tn.matmul(pb[7][0:1, 0:TTH], onescol[:, :],
                                         xc3[:, c % 2, 0:TTH],
                                         start=(c == 0), stop=(c == NCH - 1)), "T")
                    tq_slot[c % 2] = tq

                C.wait("scalar", "T", tq)
                sv_mean = C.inc(S.activation(RS(0), pb[6][0:1, 0:TTH], ACT.Copy, scale=1.0 / D), "S")
                rpb[6].users.append(("S", sv_mean))
                sv = C.inc(S.activation(RS(1), pb[7][0:1, 0:TTH], ACT.Copy, scale=1.0 / D), "S")
                rpb[7].users.append(("S", sv))
                sv2 = C.inc(S.activation(RS(2), RS(0), ACT.Square), "S")
                C.wait("vector", "S", sv2)
                vv = C.inc(V.tensor_sub(RS(1), RS(1), RS(2)), "V")
                vv = C.inc(V.tensor_scalar(RS(1), RS(1), EPS, None, op0=ALU.add), "V")
                C.wait("scalar", "V", vv)
                sv = C.inc(S.activation(RS(2), RS(1), ACT.Ln), "S")
                sv_rstd = C.inc(S.activation(RS(3), RS(2), ACT.Exp, scale=-0.5), "S")
                C.wait("vector", "S", sv_rstd)
                v_mr = C.inc(V.tensor_mul(RS(2), RS(0), RS(3)), "V")

                rpb[6].acquire(C, "tensor")
                C.wait("tensor", "S", sv_mean)
                t_meanB = C.inc(Tn.matmul(pb[6][:, 0:TTH], ones1[:, :], RS(0), start=True, stop=True), "T")
                C.wait("vector", "T", t_meanB)
                for c in range(NCH):
                    C.inc(V.tensor_tensor(xc3[:, c, 0:TTH], xc[:, c, 0:TTH], pb[6][:, 0:TTH],
                                          op=ALU.subtract), "V")
                    if c == 0:
                        V.tensor_copy(fmx32[:, 0:TTH], xc3[:, c, 0:TTH])
                        vv = C.inc(V.tensor_copy(fmn32[:, 0:TTH], xc3[:, c, 0:TTH]), "V")
                    else:
                        V.tensor_tensor(fmx32[:, 0:TTH], fmx32[:, 0:TTH], xc3[:, c, 0:TTH],
                                        op=ALU.max)
                        vv = C.inc(V.tensor_tensor(fmn32[:, 0:TTH], fmn32[:, 0:TTH], xc3[:, c, 0:TTH],
                                                   op=ALU.min), "V")
                V.tensor_scalar(fmn32[:, 0:TTH], fmn32[:, 0:TTH], -1.0, None, op0=ALU.mult)
                v_fold = C.inc(V.tensor_tensor(fmx32[:, 0:TTH], fmx32[:, 0:TTH], fmn32[:, 0:TTH],
                                               op=ALU.max), "V")
                rpb[6].users.append(("V", v_fold))

                rpb[7].acquire(C, "tensor")
                C.wait("tensor", "V", v_fold)
                cuts = [0, 128, TTH - 128]
                ttr = [C.inc(Tn.transpose(pb[7][:, j * 128:(j + 1) * 128], fmx32[:, cu:cu + 128],
                                          identf[:, :]), "T") for j, cu in enumerate(cuts)]
                for j, cu in enumerate(cuts):
                    C.wait("vector", "T", ttr[-1])
                    vv = C.inc(V.reduce_max(reccol[:, :], pb[7][:, j * 128:(j + 1) * 128], AX.X), "V")
                    C.wait("tensor", "V", vv)
                    tv = C.inc(Tn.matmul(pb[6][0:1, 300:428], reccol[:, :], identf[:, :],
                                         start=True, stop=True), "T")
                    C.wait("vector", "T", tv)
                    vv = C.inc(V.tensor_copy(sxrow[0:1, cu:cu + 128], pb[6][0:1, 300:428]), "V")
                rpb[6].users.append(("V", vv))
                rpb[7].users.append(("V", vv))

                vv = C.inc(V.tensor_mul(RS(4), sxrow[0:1, 0:TTH], RS(3)), "V")
                vv = C.inc(V.tensor_scalar(RS(4), RS(4), 1e-5, None, op0=ALU.max), "V")
                vv = C.inc(V.reciprocal(RS(5), RS(4)), "V")
                vv = C.inc(V.tensor_scalar(RS(5), RS(5), 127.0, None, op0=ALU.mult), "V")
                vv_a = C.inc(V.tensor_mul(RS(6), RS(3), RS(5)), "V")
                vv_f1 = C.inc(V.tensor_scalar(RS(7), RS(4), m127[0:1, 0:1], None, op0=ALU.mult), "V")
                C.inc(V.tensor_copy(rstd_rows[0:1, t0:t0 + TT], RS(3)[0:1, 1:257]), "V")
                vr = C.inc(V.tensor_copy(mr_rows[0:1, t0:t0 + TT], RS(2)[0:1, 1:257]), "V")

                rpb[7].acquire(C, "tensor")
                C.wait("tensor", "V", vv_a)
                t_aB = C.inc(Tn.matmul(pb[7][:, 0:TTH], ones1[:, :], RS(6), start=True, stop=True), "T")
                rpb[6].acquire(C, "tensor")
                C.wait("tensor", "V", vv_f1)
                t_f1B = C.inc(Tn.matmul(pb[6][:, 0:TTH], ones1[:, :], RS(7), start=True, stop=True), "T")
                C.wait("vector", "T", t_aB)
                rxs_.acquire(C, "vector")
                V.tensor_tensor(
                    xc3[:, :, :], xc3[:, :, :],
                    pb[7][:, 0:TTH].unsqueeze(1).broadcast_to((128, NCH, TTH)), op=ALU.mult)
                vv = C.inc(V.tensor_scalar(xs3_[:, :, :], xc3[:, :, :], RMAGIC, -RMAGIC,
                                           op0=ALU.add, op1=ALU.add), "V")
                rpb[7].users.append(("V", vv))
                r_xc.users.append(("V", vv))
                C.wait("vector", "T", t_f1B)
                v_xs = C.inc(V.tensor_tensor(
                    xs3_[:, :, :], xs3_[:, :, :],
                    pb[6][:, 0:TTH].unsqueeze(1).broadcast_to((128, NCH, TTH)), op=ALU.mult), "V")
                rpb[6].users.append(("V", v_xs))
                if not sL:
                    v_xs = C.inc(V.memset(xs3_[:, :, 0:1], 0.0), "V")
                if not sR:
                    v_xs = C.inc(V.memset(xs3_[:, :, TTH - 1:TTH], 0.0), "V")
                bst[it] = {"v_xs": v_xs, "v_rows": vr}

            def b_p2(it):
                st_ = bst[it]
                xs_ = xs_l[it % 2]
                rxs_ = r_xsl[it % 2]
                C.wait("tensor", "V", st_["v_xs"])
                C.wait("tensor", "V", v_signs)
                h1b = h1_l[it % 2]
                rh = r_h1b[it % 2]
                rh.acquire(C, ("vector", "scalar"))
                h1ev = []
                for cc in range(HCH):
                    bk = 2 + cc % 4
                    rpb[bk].acquire(C, "tensor")
                    for dc in range(NCH):
                        tm = C.inc(Tn.matmul(pb[bk][:, 0:TTH],
                                             w1s[:, dc * DH + cc * 128: dc * DH + cc * 128 + 128],
                                             xs_[:, dc * TTH:(dc + 1) * TTH],
                                             start=(dc == 0), stop=(dc == NCH - 1)), "T")
                    if cc % 2 == 0:
                        C.wait("vector", "T", tm)
                        se = C.inc(V.tensor_copy(h1b[:, cc * TTH:(cc + 1) * TTH], pb[bk][:, 0:TTH]), "V")
                        rpb[bk].users.append(("V", se))
                        h1ev.append(("V", se))
                    else:
                        C.wait("scalar", "T", tm)
                        se = C.inc(S.copy(h1b[:, cc * TTH:(cc + 1) * TTH], pb[bk][:, 0:TTH]), "S")
                        rpb[bk].users.append(("S", se))
                        h1ev.append(("S", se))
                rxs_.users.append(("T", tm))
                st_["h1ev"] = h1ev

            def b_p3(it):
                st_ = bst[it]
                h1ev = st_["h1ev"]
                h1b = h1_l[it % 2]
                rh = r_h1b[it % 2]
                r_h2.acquire(C, ("vector", "scalar", "gpsimd"))
                r_gsc.acquire(C, "gpsimd")
                sI = {}
                vA = {}
                gB = {}
                silu_done = [None] * HCH
                for it2 in range(HCH + 7):
                    cc = it2
                    if cc < HCH:
                        C.wait("scalar", h1ev[cc][0], h1ev[cc][1])
                        z = h2[:, cc * TT:(cc + 1) * TT]
                        sI[cc] = C.inc(S.activation(z, h1b[:, cc * TTH + 0: cc * TTH + TT],
                                                    ACT.Identity, scale=convw[:, cc:cc + 1],
                                                    bias=convb[:, cc:cc + 1]), "S")
                    c1 = it2 - 4
                    if 0 <= c1 < HCH:
                        C.wait("vector", "S", sI[c1])
                        C.wait("vector", "G", gB[c1])
                        C.wait("vector", h1ev[c1][0], h1ev[c1][1])
                        z = h2[:, c1 * TT:(c1 + 1) * TT]
                        V.scalar_tensor_tensor(z, h1b[:, c1 * TTH + 1: c1 * TTH + 1 + TT],
                                               convw[:, 32 + c1:32 + c1 + 1], z,
                                               op0=ALU.mult, op1=ALU.add)
                        g = gsc[:, (c1 % 2) * TT:(c1 % 2) * TT + TT]
                        vA[c1] = C.inc(V.tensor_tensor(z, z, g, op=ALU.add), "V")
                    c2 = it2 - 2
                    if 0 <= c2 < HCH:
                        g = gsc[:, (c2 % 2) * TT:(c2 % 2) * TT + TT]
                        C.wait("gpsimd", h1ev[c2][0], h1ev[c2][1])
                        if c2 - 2 in vA:
                            C.wait("gpsimd", "V", vA[c2 - 2])
                        gB[c2] = C.inc(G.tensor_tensor(
                            g, h1b[:, c2 * TTH + 2: c2 * TTH + 2 + TT],
                            convw[:, 64 + c2:64 + c2 + 1].broadcast_to((128, TT)),
                            op=ALU.mult), "G")
                    c3 = it2 - 7
                    if 0 <= c3 < HCH and c3 % 4 == 3:
                        C.wait("scalar", "V", vA[c3])
                        zb = h2[:, (c3 - 3) * TT:(c3 + 1) * TT]
                        sz = C.inc(S.activation(zb, zb, ACT.Silu), "S")
                        for c4 in range(c3 - 3, c3 + 1):
                            silu_done[c4] = sz
                rh.users.append(("V", vA[HCH - 1]))
                rh.users.append(("S", sI[HCH - 1]))
                r_gsc.users.append(("V", vA[HCH - 2]))
                r_gsc.users.append(("V", vA[HCH - 1]))

                fx3 = h1b[:, 0:16 * TT].rearrange("p (c t) -> p c t", c=16)
                fn3 = h1b[:, 16 * TT:32 * TT].rearrange("p (c t) -> p c t", c=16)
                h23v = h2.rearrange("p (c t) -> p c t", c=HCH)
                C.wait("vector", "S", silu_done[HCH - 1])
                V.tensor_tensor(fx3[:, :, :], h23v[:, 0:16, :], h23v[:, 16:32, :], op=ALU.max)
                V.tensor_tensor(fn3[:, :, :], h23v[:, 0:16, :], h23v[:, 16:32, :], op=ALU.min)
                for lv in (8, 4, 2, 1):
                    V.tensor_tensor(fx3[:, 0:lv, :], fx3[:, 0:lv, :], fx3[:, lv:2 * lv, :], op=ALU.max)
                    V.tensor_tensor(fn3[:, 0:lv, :], fn3[:, 0:lv, :], fn3[:, lv:2 * lv, :], op=ALU.min)
                V.tensor_scalar(fn3[:, 0, :], fn3[:, 0, :], -1.0, None, op0=ALU.mult)
                vf = C.inc(V.tensor_tensor(fx3[:, 0, :], fx3[:, 0, :], fn3[:, 0, :],
                                           op=ALU.max), "V")

                rpb[0].acquire(C, "tensor")
                C.wait("tensor", "V", vf)
                tt1 = C.inc(Tn.transpose(pbT[0][:, 0:128], h1b[:, 0:128], ident[:, :]), "T")
                tt2 = C.inc(Tn.transpose(pbT[0][:, 128:256], h1b[:, 128:256], ident[:, :]), "T")
                rh.users.append(("T", tt2))
                rh.users.append(("V", vf))
                svals = []
                for j in range(2):
                    C.wait("vector", "T", tt2)
                    vv = C.inc(V.reduce_max(reccol2[:, :], pbT[0][:, j * 128:(j + 1) * 128], AX.X), "V")
                    vv = C.inc(V.tensor_scalar(s2f2[:, 0:1], reccol2[:, :], 1e-5, None, op0=ALU.max), "V")
                    vv = C.inc(V.reciprocal(s2f2[:, 1:2], s2f2[:, 0:1]), "V")
                    V.tensor_scalar(s2f2[:, 1:2], s2f2[:, 1:2], 127.0, None, op0=ALU.mult)
                    vv = C.inc(V.tensor_scalar(s2f2[:, 0:1], s2f2[:, 0:1], mw2col[:, :], None,
                                               op0=ALU.mult), "V")
                    C.wait("tensor", "V", vv)
                    tv = C.inc(Tn.matmul(pb[0][0:1, 300:428], s2f2[:, 0:1], identf[:, :],
                                         start=True, stop=True), "T")
                    tv = C.inc(Tn.matmul(pb[0][0:1, 172:300], s2f2[:, 1:2], identf[:, :],
                                         start=True, stop=True), "T")
                    C.wait("vector", "T", tv)
                    vv = C.inc(V.tensor_copy(f2row[0:1, 0:128], pb[0][0:1, 300:428]), "V")
                    vv = C.inc(V.tensor_copy(s2row[0:1, 0:128], pb[0][0:1, 172:300]), "V")
                    C.wait("tensor", "V", vv)
                    if j == 0:
                        rpb[1].acquire(C, "tensor")
                    else:
                        C.wait("tensor", "S", svals[-1])
                    tb = C.inc(Tn.matmul(pb[1][:, j * 128:(j + 1) * 128],
                                         ones1[:, :], s2row[0:1, 0:128], start=True, stop=True), "T")
                    tb2 = C.inc(Tn.matmul(pb[1][:, 256 + j * 128:256 + (j + 1) * 128],
                                          ones1[:, :], f2row[0:1, 0:128], start=True, stop=True), "T")
                    C.wait("scalar", "T", tb2)
                    if j == 0:
                        r_s2b.acquire(C, "scalar")
                    svals.append(C.inc(S.copy(s2b[:, j * 128:(j + 1) * 128],
                                              pb[1][:, j * 128:(j + 1) * 128]), "S"))
                    svals.append(C.inc(S.copy(f2b[:, j * 128:(j + 1) * 128],
                                              pb[1][:, 256 + j * 128:256 + (j + 1) * 128]), "S"))
                rpb[0].users.append(("V", vv))
                rpb[1].users.append(("S", svals[-1]))

                C.wait("vector", "S", svals[-1])
                h23 = h2.rearrange("p (c t) -> p c t", c=HCH)
                V.tensor_tensor(h23[:, :, :], h23[:, :, :],
                                s2b[:, :].unsqueeze(1).broadcast_to((128, HCH, TT)),
                                op=ALU.mult)
                v_hq = C.inc(V.tensor_scalar(h23[:, :, :], h23[:, :, :], RMAGIC, -RMAGIC,
                                             op0=ALU.add, op1=ALU.add), "V")
                st_["v_hq"] = v_hq
                r_h2.users.append(("V", v_hq))
                rh.users.append(("T", None))
                rh.users.pop()

            def b_p45(it):
                st_ = bst.pop(it)
                t0 = it * TT
                xc = x3b[it % 3]
                rx = r_xtb[it % 3]
                C.wait("tensor", "V", st_["v_hq"])
                ob = outsb[it % 2]
                ro = r_outsb[it % 2]
                ro.acquire(C, "vector")
                for oc in (0, 2, 4, 6, 1, 3, 5, 7):
                    bk = oc // 2
                    half = (oc % 2) * 256
                    rpb[bk].acquire(C, "tensor")
                    for cc in range(HCH):
                        tm = C.inc(Tn.matmul(pb[bk][:, half:half + TT],
                                             w2s[:, cc * D + oc * 128: cc * D + oc * 128 + 128],
                                             h2[:, cc * TT:(cc + 1) * TT],
                                             start=(cc == 0), stop=(cc == HCH - 1)), "T")
                    C.wait("vector", "T", tm)
                    dst = ob[:, oc * TT:(oc + 1) * TT]
                    V.tensor_tensor(dst, pb[bk][:, half:half + TT], f2b[:, :], op=ALU.mult)
                    ve = C.inc(V.scalar_tensor_tensor(dst, xc[:, oc, 1:257], 1.0, dst,
                                                      op0=ALU.mult, op1=ALU.add), "V")
                    rpb[bk].users.append(("V", ve))
                    C.wait("scalar", "V", ve)
                    ve2 = C.inc(S.activation(dst, dst, ACT.Identity, scale=1.0,
                                             bias=boutl[:, oc:oc + 1]), "S")
                    b_p45.sevs.append(("S", ve2))
                rx.users.append(("V", ve))
                r_s2b.users.append(("V", ve))
                r_h2.users.append(("T", tm))
                for u in b_p45.sevs:
                    C.wait("gpsimd", u[0], u[1])
                b_p45.sevs = []
                stv = C.inc(G.dma_start(out=d_out[:, t0:t0 + TT].rearrange("(c p) w -> p c w", p=128),
                                        in_=ob[:, :].rearrange("p (c w) -> p c w", c=NCH)), f"ST{it % 2}", 16)
                ro.users.append((f"ST{it % 2}", stv))
                return vr if False else st_["v_rows"]

            b_p45.sevs = []
            nb_ = nb if do_b else 0
            if nb_ > 0:
                b_p1(0)
                b_p2(0)
                b_p1(1)
                for it in range(nb_):
                    if it + 2 < nb_:
                        b_p1(it + 2)
                    if it + 1 < nb_:
                        b_p2(it + 1)
                    b_p3(it)
                    v_rows = b_p45(it)
            b_rows_done = v_rows
            if bstop != "full":
                C.inc(G.dma_start(out=d_out[:, 0:TT].rearrange("(c p) w -> p c w", p=128),
                                  in_=outsb[0][:, :].rearrange("p (c w) -> p c w", c=NCH)), "ST0", 16)
                C.wait("gpsimd", "ST0", C.c.get("ST0", 0))
                return
            if not do_c:
                if not do_b:
                    C.wait("gpsimd", "V", v_signs)
                    C.inc(G.dma_start(out=d_out[:, 0:TT].rearrange("(c p) w -> p c w", p=128),
                                      in_=outsb[0][:, :].rearrange("p (c w) -> p c w", c=NCH)), "ST0", 16)
                C.wait("gpsimd", "ST0", C.c.get("ST0", 0))
                C.wait("gpsimd", "ST1", C.c.get("ST1", 0))
                return

            last_ffn_T = C.c["T"]
            C.wait("sync", "T", last_ffn_T)
            C.wait("sync", "ST0", C.c.get("ST0", 0))
            C.wait("sync", "ST1", C.c.get("ST1", 0))
            l_tab = dma_in(cosT, d_cos[:, :])
            l_tab = dma_in(sinT, d_sin[:, :])
            for (d_w, nch, dst, wd) in ((d_wqkvT, NCH, wqkv, 3 * D), (d_woutT, NCH, woutw, D)):
                for i in range(nch):
                    bi = i % 2
                    r_wtmp[bi].acquire(C, "sync")
                    lv = dma_in(wtmp[bi][:, 0:wd], d_w[i * 128:(i + 1) * 128, :], f"LW{bi}")
                    C.wait("scalar", f"LW{bi}", lv)
                    wl = C.inc(S.copy(dst[:, i * wd:(i + 1) * wd], wtmp[bi][:, 0:wd]), "S")
                    r_wtmp[bi].users.append(("S", wl))
            s_wmm = wl

            C.wait("vector", "S", s_wmm)
            C.wait("vector", "L", l_tab)
            for b in range(4):
                vb3 = vblk_l[b].rearrange("p (h d) -> p h d", h=H)
                vinit = C.inc(V.memset(vb3[:, :, HD:HD + 1], 1.0), "V")

            qsb3 = [qsb_l[i].rearrange("p (c t) -> p c t", c=NCH) for i in range(2)]
            xn3 = xnsb.rearrange("p (c t) -> p c t", c=NCH)
            state = {"rot": 0, "trot": 0, "erot": 0}
            it_info = {}

            def blk_meta(gb):
                bl = gb % 16
                if bl == 0:
                    return [0, 1], maskfirst
                if bl == 15:
                    return [-1, 0], masklast
                return [-1, 0, 1], maskint

            def em_sc(it):
                gb, h = it
                deltas, mask = blk_meta(gb)
                jq, u = gb // 2, gb % 2
                ch, ro_ = h // 2, (h % 2) * 64
                sb_ = 0 + (state["rot"] % 2)
                pv_ = 2 + (state["rot"] % 2)
                state["rot"] += 1
                it_info[it] = ii = {"sb": sb_, "pv": pv_}
                rpb[sb_].acquire(C, "tensor")
                for ci, dl in enumerate(deltas):
                    kb = (gb + dl) % 4
                    r_kblk[kb].use(C, "tensor")
                    r_qsb[jq % 2].use(C, "tensor")
                    tsc = C.inc(Tn.matmul(
                        pb[sb_][:, ci * 128:(ci + 1) * 128],
                        kblk_l[kb][ro_:ro_ + 64, ch * 128:(ch + 1) * 128],
                        qsb3[jq % 2][ro_:ro_ + 64, ch, u * 128:(u + 1) * 128],
                        start=True, stop=True), "T")
                ii["tsc"] = tsc

            def em_exp(it):
                gb, h = it
                deltas, mask = blk_meta(gb)
                nck = len(deltas)
                ii = it_info[it]
                es = state["erot"] % 2
                state["erot"] += 1
                ii["es"] = es
                r_expsb[es].acquire(C, "scalar")
                C.wait("scalar", "T", ii["tsc"])
                se = C.inc(S.activation(expsb_l[es][:, 0:nck * 128], pb[ii["sb"]][:, 0:nck * 128],
                                        ACT.Exp, scale=0.125), "S")
                rpb[ii["sb"]].users.append(("S", se))
                ii["se"] = se

            def em_mask(it):
                gb, h = it
                deltas, mask = blk_meta(gb)
                nck = len(deltas)
                ii = it_info[it]
                es = ii["es"]
                C.wait("vector", "S", ii["se"])
                vm = C.inc(V.tensor_tensor(expsb_l[es][:, 0:nck * 128], expsb_l[es][:, 0:nck * 128],
                                           mask[:, 0:nck * 128], op=ALU.mult), "V")
                ii["vm"] = vm

            def em_pv(it):
                gb, h = it
                deltas, mask = blk_meta(gb)
                nck = len(deltas)
                ii = it_info[it]
                es = ii["es"]
                pv_ = ii["pv"]
                rpb[pv_].acquire(C, "tensor")
                C.wait("tensor", "V", ii["vm"])
                for ci, dl in enumerate(deltas):
                    vb = (gb + dl) % 4
                    r_vblk[vb].use(C, "tensor")
                    tpv = C.inc(Tn.matmul(pb[pv_][:, 0:HD + 1],
                                          expsb_l[es][:, ci * 128:(ci + 1) * 128],
                                          vblk_l[vb][:, h * (HD + 1):(h + 1) * (HD + 1)],
                                          start=(ci == 0), stop=(ci == nck - 1)), "T")
                    r_vblk[vb].users.append(("T", tpv))
                r_expsb[es].users.append(("T", tpv))
                ii["tpv"] = tpv

            def em_norm(it):
                gb, h = it
                ii = it_info[it]
                pv_ = ii["pv"]
                C.wait("vector", "T", ii["tpv"])
                if h == 0:
                    r_atok[gb % 2].acquire(C, "vector")
                V.reciprocal(reccol[:, :], pb[pv_][:, HD:HD + 1])
                va = C.inc(V.tensor_scalar(atok_l[gb % 2][:, h * 64:(h + 1) * 64], pb[pv_][:, 0:HD],
                                           reccol[:, :], None, op0=ALU.mult), "V")
                rpb[pv_].users.append(("V", va))
                ii["va"] = va

            def em_tr(gb, va):
                jq, u = gb // 2, gb % 2
                for ch in range(NCH):
                    tb_ = 4 + (state["trot"] % 2)
                    state["trot"] += 1
                    rpb[tb_].acquire(C, "tensor")
                    C.wait("tensor", "V", va)
                    ttv = C.inc(Tn.transpose(pbT[tb_][:, 0:128],
                                             atok_l[gb % 2][:, ch * 128:(ch + 1) * 128],
                                             ident[:, :]), "T")
                    r_atok[gb % 2].users.append(("T", ttv))
                    C.wait("scalar", "T", ttv)
                    sev = C.inc(S.copy(attnT_l[jq % 2][:, ch * TT + u * 128: ch * TT + u * 128 + 128],
                                       pbT[tb_][:, 0:128]), "S")
                    rpb[tb_].users.append(("S", sev))
                    r_attnT[jq % 2].ready.append(("S", sev))

            def attn_tile(blocks, s_):
                iters = [(s_ * 16 + bl, h) for bl in blocks for h in range(H)]
                pend = []
                em_sc(iters[0])
                em_sc(iters[1])
                for i, it in enumerate(iters):
                    em_exp(it)
                    em_mask(it)
                    if i + 2 < len(iters):
                        em_sc(iters[i + 2])
                    em_pv(it)
                    em_norm(it)
                    gb, h = it
                    if h == H - 1:
                        pend.append((i + 2, gb, it_info[it]["va"]))
                    while pend and pend[0][0] <= i:
                        _, g_, va_ = pend.pop(0)
                        em_tr(g_, va_)
                for _, g_, va_ in pend:
                    em_tr(g_, va_)

            def wout_tile(tj):
                t0 = tj * TT
                ob = outsb[tj % 2]
                ro = r_outsb[tj % 2]
                r_attnT[tj % 2].use(C, "tensor")
                evs = []
                tm_bank = {}
                for oc in (0, 2, 4, 6, 1, 3, 5, 7):
                    bk = oc // 2
                    half = (oc % 2) * 256
                    rpb[bk].acquire(C, "tensor")
                    for dc in range(NCH):
                        tm = C.inc(Tn.matmul(pb[bk][:, half:half + TT],
                                             woutw[:, dc * D + oc * 128: dc * D + oc * 128 + 128],
                                             attnT_l[tj % 2][:, dc * TT:(dc + 1) * TT],
                                             start=(dc == 0), stop=(dc == NCH - 1)), "T")
                    tm_bank[bk] = tm
                    if oc % 2 == 1:
                        C.wait("vector", "T", tm)
                        C.wait("vector", f"ST{tj % 2}", wout_tile.rb[tj])
                        ve = C.inc(V.tensor_tensor(ob[:, bk * 512:(bk + 1) * 512],
                                                   ob[:, bk * 512:(bk + 1) * 512],
                                                   pb[bk][:, 0:512], op=ALU.add), "V")
                        rpb[bk].users.append(("V", ve))
                        evs.append(ve)
                C.wait("gpsimd", "V", evs[-1])
                stv = C.inc(G.dma_start(out=d_out[:, t0:t0 + TT].rearrange("(c p) w -> p c w", p=128),
                                        in_=ob[:, :].rearrange("p (c w) -> p c w", c=NCH)), f"ST{tj % 2}", 16)
                ro.users.append((f"ST{tj % 2}", stv))
                r_attnT[tj % 2].users.append(("T", tm))
                return stv

            wout_tile.rb = {}

            def readback(tj):
                ob = outsb[tj % 2]
                ro = r_outsb[tj % 2]
                ro.acquire(C, "gpsimd")
                wout_tile.rb[tj] = C.inc(
                    G.dma_start(out=ob[:, :].rearrange("p (c w) -> p c w", c=NCH),
                                in_=d_out[:, tj * TT:tj * TT + TT].rearrange("(c p) w -> p c w", p=128)),
                    f"ST{tj % 2}", 16)

            st_final = 0
            for j in range(NTILE):
                jl = j % TPS
                t0 = j * TT
                xb = xtile[j % 2]
                rx = r_xtile[j % 2]

                rx.acquire(C, "sync")
                lxs = f"LX{j % 2}"
                lx = dma_in(
                    xb[:, :].rearrange("p (c w) -> p c w", c=NCH)[:, :, 1:257],
                    d_xf[:, t0:t0 + TT].rearrange("(c p) w -> p c w", p=128), lxs)
                if jl >= 1:
                    readback(j - 1)
                rpb[4].acquire(C, "tensor")
                C.wait("tensor", "V", b_rows_done)
                C.inc(Tn.matmul(pb[4][:, 0:128], ones1b[:, :], rstd_rows[0:1, t0:t0 + 128],
                                start=True, stop=True), "T")
                C.inc(Tn.matmul(pb[4][:, 128:256], ones1b[:, :], rstd_rows[0:1, t0 + 128:t0 + 256],
                                start=True, stop=True), "T")
                C.inc(Tn.matmul(pb[4][:, 256:384], mones1b[:, :], mr_rows[0:1, t0:t0 + 128],
                                start=True, stop=True), "T")
                tb2 = C.inc(Tn.matmul(pb[4][:, 384:512], mones1b[:, :], mr_rows[0:1, t0 + 128:t0 + 256],
                                      start=True, stop=True), "T")
                C.wait("vector", lxs, lx)
                C.wait("vector", "T", tb2)
                r_xnsb.acquire(C, "vector")
                V.tensor_tensor(xn3[:, :, :], x3[j % 2][:, :, 1:257],
                                pb[4][:, 0:256].unsqueeze(1).broadcast_to((128, NCH, 256)),
                                op=ALU.mult)
                v_xn = C.inc(V.tensor_tensor(xn3[:, :, :], xn3[:, :, :],
                                             pb[4][:, 256:512].unsqueeze(1).broadcast_to((128, NCH, 256)),
                                             op=ALU.add), "V")
                rpb[4].users.append(("V", v_xn))
                rx.users.append(("V", v_xn))

                C.wait("tensor", "V", v_xn)
                C.wait("tensor", "S", s_wmm)
                qk_info = {}

                def qk_mm(i):
                    kind, ch = ("q", "k")[i // 8], i % 8
                    qb = 6 + (i % 2)
                    woff = (0 if kind == "q" else D) + ch * 128
                    rpb[qb].acquire(C, "tensor")
                    for dc in range(NCH):
                        tm = C.inc(Tn.matmul(pb[qb][:, 0:TT],
                                             wqkv[:, dc * 3 * D + woff: dc * 3 * D + woff + 128],
                                             xnsb[:, dc * TT:(dc + 1) * TT],
                                             start=(dc == 0), stop=(dc == NCH - 1)), "T")
                    r_xnsb.users.append(("T", tm))
                    qk_info[i] = {"qb": qb, "tm": tm}

                def qk_evac(i):
                    ii = qk_info[i]
                    qr = i % 2
                    ii["qr"] = qr
                    r_qraw[qr].acquire(C, "scalar")
                    C.wait("scalar", "T", ii["tm"])
                    se = C.inc(S.copy(qraw_l[qr][:, 0:TT], pb[ii["qb"]][:, 0:TT]), "S")
                    rpb[ii["qb"]].users.append(("S", se))
                    ii["se"] = se

                def qk_swap(i):
                    ii = qk_info[i]
                    sw = 4 + (i % 2)
                    ii["sw"] = sw
                    rpb[sw].acquire(C, "tensor")
                    C.wait("tensor", "S", ii["se"])
                    ii["tsw"] = C.inc(Tn.matmul(pb[sw][:, 0:TT], pswap[:, :],
                                                qraw_l[ii["qr"]][:, 0:TT],
                                                start=True, stop=True), "T")

                def qk_rope(i):
                    ii = qk_info[i]
                    kind, ch = ("q", "k")[i // 8], i % 8
                    C.wait("vector", "T", ii["tsw"])
                    V.tensor_tensor(tr1, pb[ii["sw"]][:, 0:TT], sinT[:, t0:t0 + TT], op=ALU.mult)
                    V.tensor_tensor(tr2, qraw_l[ii["qr"]][:, 0:TT], cosT[:, t0:t0 + TT], op=ALU.mult)
                    if kind == "q":
                        if ch == 0:
                            r_qsb[j % 2].acquire(C, "vector")
                        vr = C.inc(V.tensor_tensor(qsb3[j % 2][:, ch, :], tr1, tr2, op=ALU.add), "V")
                        r_qsb[j % 2].ready.append(("V", vr))
                    else:
                        for half in range(2):
                            kb = (2 * j + half) % 4
                            if ch == 0:
                                r_kblk[kb].acquire(C, "vector")
                            vr = C.inc(V.tensor_tensor(
                                kblk_l[kb][:, ch * 128:(ch + 1) * 128],
                                tr1[:, half * 128:(half + 1) * 128],
                                tr2[:, half * 128:(half + 1) * 128], op=ALU.add), "V")
                            r_kblk[kb].ready.append(("V", vr))
                    rpb[ii["sw"]].users.append(("V", vr))
                    r_qraw[ii["qr"]].users.append(("V", vr))

                qk_mm(0)
                for i in range(16):
                    qk_evac(i)
                    if i + 1 < 16:
                        qk_mm(i + 1)
                    qk_swap(i)
                    qk_rope(i)
                for u in range(2):
                    vbi = (2 * j + u) % 4
                    r_vblk[vbi].acquire(C, "vector")
                    for hf in range(2):
                        qb = 6 + hf
                        rpb[qb].acquire(C, "tensor")
                        for dc in range(NCH):
                            tm = C.inc(Tn.matmul(
                                pb[qb][:, 0:512],
                                xn3[:, dc, u * 128:(u + 1) * 128],
                                wqkv[:, dc * 3 * D + 2 * D + hf * 512: dc * 3 * D + 2 * D + (hf + 1) * 512],
                                start=(dc == 0), stop=(dc == NCH - 1)), "T")
                        r_xnsb.users.append(("T", tm))
                        C.wait("vector", "T", tm)
                        vb3 = vblk_l[vbi].rearrange("p (h d) -> p h d", h=H)
                        vev = C.inc(V.tensor_copy(
                            vb3[:, hf * 8:(hf + 1) * 8, 0:HD],
                            pb[qb][:, 0:512].rearrange("p (h d) -> p h d", h=8)), "V")
                        rpb[qb].users.append(("V", vev))
                        r_vblk[vbi].ready.append(("V", vev))

                s_ = j // TPS
                blocks = [0] if jl == 0 else [2 * jl - 1, 2 * jl]
                if jl == TPS - 1:
                    blocks.append(15)
                attn_tile(blocks, s_)

                if jl >= 1:
                    st_final = wout_tile(j - 1)
                if jl == TPS - 1:
                    readback(j)
                    st_final = wout_tile(j)

            C.wait("gpsimd", "ST0", C.c.get("ST0", 0))
            C.wait("gpsimd", "ST1", C.c.get("ST1", 0))
    return nc


_BUILT = {}


def _get_nc():
    if "nc" not in _BUILT:
        nc = bass.Bass("TRN2", target_bir_lowering=False, debug=False)
        build_program(nc)
        _BUILT["nc"] = nc
    return _BUILT["nc"]


def _consts():
    if "c" in _BUILT:
        return _BUILT["c"]
    pos = np.arange(TSEQ, dtype=np.float64)
    inv = 1.0 / (10000.0 ** (np.arange(0, HD, 2, dtype=np.float64) / HD))
    fr = pos[:, None] * inv[None, :]
    cosv = np.cos(fr)
    sinv = np.sin(fr)
    cosT = np.zeros((128, TSEQ), np.float32)
    sinT = np.zeros((128, TSEQ), np.float32)
    for r in range(128):
        d = r % 64
        jj = d if d < 32 else d - 32
        cosT[r] = cosv[:, jj]
        sinT[r] = (-sinv[:, jj]) if d < 32 else sinv[:, jj]
    cosT = np.tile(cosT, (1, 2)).astype(bf16)
    sinT = np.tile(sinT, (1, 2)).astype(bf16)
    psw = np.zeros((128, 128), np.float32)
    for i in range(128):
        d = i % 64
        k = i + 32 if d < 32 else i - 32
        psw[k, i] = 1.0
    kk = np.arange(128)[:, None]
    qq = np.arange(128)[None, :]
    mneg = (kk > qq).astype(np.float32)
    mpos = (kk <= qq).astype(np.float32)
    one = np.ones((128, 128), np.float32)
    c = dict(
        d_cosT=cosT, d_sinT=sinT, d_pswap=psw.astype(bf16),
        d_ident=np.eye(128, dtype=np.float32).astype(bf16),
        d_maskint=np.concatenate([mneg, one, mpos], 1).astype(bf16),
        d_maskfirst=np.concatenate([one, mpos], 1).astype(bf16),
        d_masklast=np.concatenate([mneg, one], 1).astype(bf16),
        d_ones1=np.ones((1, 128), np.float32),
        d_mones1=np.full((1, 128), -1.0, np.float32),
        d_ones1b=np.ones((1, 128), bf16),
        d_mones1b=np.full((1, 128), -1.0, np.float32).astype(bf16),
        d_onescol=np.ones((128, 1), np.float32),
        d_onescolb=np.ones((128, 1), bf16),
        d_identf=np.eye(128, dtype=np.float32),
    )
    _BUILT["c"] = c
    return c


def kernel(L0_attn_x, L0_attn_w_qkv, L0_attn_w_out, L0_attn_b_out,
           L0_ffn_w1, L0_ffn_conv_w, L0_ffn_conv_b, L0_ffn_w2):
    x = np.ascontiguousarray(np.asarray(L0_attn_x, np.float32))
    wqkv = np.asarray(L0_attn_w_qkv, np.float32)
    wout = np.asarray(L0_attn_w_out, np.float32)
    bout = np.asarray(L0_attn_b_out, np.float32)
    w1 = np.asarray(L0_ffn_w1, np.float32)
    cw = np.asarray(L0_ffn_conv_w, np.float32)
    cb = np.asarray(L0_ffn_conv_b, np.float32)
    w2 = np.asarray(L0_ffn_w2, np.float32)

    shared = dict(
        wqkvT=np.ascontiguousarray(wqkv.T),
        woutT=np.ascontiguousarray(wout.T),
        d_boutl=np.ascontiguousarray(bout.reshape(NCH, 128).T),
        w1T=np.ascontiguousarray(w1.T),
        w2T=np.ascontiguousarray(w2.T),
        d_convw=np.ascontiguousarray(cw[:, 0, :].reshape(HCH, 128, 3).transpose(1, 2, 0).reshape(128, 96)),
        d_convb=np.ascontiguousarray(cb.reshape(HCH, 128).T),
        **_consts(),
    )
    nc = _get_nc()
    in_maps = []
    for ci in range(8):
        xc = x[2 * ci:2 * ci + 2]
        xf = np.concatenate([xc[0].T, xc[1].T], axis=1)
        m = dict(shared)
        m["xf"] = np.ascontiguousarray(xf)
        in_maps.append(m)
    res = run_bass_kernel_spmd(nc, in_maps, core_ids=list(range(8)))
    out = np.empty((16, TSEQ, D), np.float32)
    for ci in range(8):
        of = res.results[ci]["outf"]
        out[2 * ci] = of[:, :TSEQ].T
        out[2 * ci + 1] = of[:, TSEQ:].T
    return out


# version 11
# speedup vs baseline: 1.1862x; 1.0387x over previous
import numpy as np
import ml_dtypes
import concourse.bass as bass
import concourse.mybir as mybir
import concourse.bass_utils as _bass_utils
from concourse.bass_utils import run_bass_kernel_spmd

_orig_run_command = _bass_utils.run_command


def _run_command_ldwopt(cmd, *a, **k):
    cmd = ["--enable-ldw-opt=true" if c == "--enable-ldw-opt=false" else c for c in cmd]
    return _orig_run_command(cmd, *a, **k)


_bass_utils.run_command = _run_command_ldwopt

F32 = mybir.dt.float32
BF16 = mybir.dt.bfloat16
I8 = mybir.dt.int8
F8 = mybir.dt.float8e4
ALU = mybir.AluOpType
ACT = mybir.ActivationFunctionType
AX = mybir.AxisListType
bf16 = ml_dtypes.bfloat16

D = 1024
H = 16
HD = 64
TSEQ = 2048
T = 4096
DH = 4096
NCH = 8
HCH = 32
TT = 256
NTILE = 16
TPS = 8
TTH = TT + 2
EPS = 1e-5
RMAGIC = 12582912.0


class Ctr:
    def __init__(self, nc, sems):
        self.c = {}
        self.lastw = {}
        self.eng = {"sync": nc.sync, "gpsimd": nc.gpsimd, "tensor": nc.tensor,
                    "vector": nc.vector, "scalar": nc.scalar}
        self.sems = sems

    def inc(self, inst, sem, n=1):
        inst.then_inc(self.sems[sem], n)
        self.c[sem] = self.c.get(sem, 0) + n
        return self.c[sem]

    def wait(self, ename, sem, val):
        if val is None or val <= 0:
            return
        k = (ename, sem)
        if self.lastw.get(k, 0) >= val:
            return
        self.lastw[k] = val
        self.eng[ename].wait_ge(self.sems[sem], val)


def _ap_span(ap):
    try:
        t = ap.tensor
        name = t.name
        esz = mybir.dt.size(ap.dtype)
        pairs = list(ap.ap)
        p0 = int(getattr(t, "base_partition", 0) or 0)
        np_ = int(pairs[0][1]) if pairs else 128
        off = int(ap.offset)
        ext = 0
        for (st, ct) in pairs[1:]:
            ext += abs(int(st)) * (int(ct) - 1)
        lo = off * esz
        hi = (off + ext + 1) * esz
        return (name, p0, p0 + np_, lo, hi)
    except Exception:
        return ("?", 0, 128, 0, 1 << 40)


def _overlap(a, b):
    return (a[0] == b[0] and a[1] < b[2] and b[1] < a[2]
            and a[3] < b[4] and b[3] < a[4])


class DrainWrap:
    _OPS = ("tensor_tensor", "tensor_scalar", "tensor_add", "tensor_sub", "tensor_mul",
            "tensor_max", "tensor_copy", "tensor_single_scalar", "tensor_scalar_add",
            "tensor_scalar_sub", "tensor_scalar_mul", "tensor_scalar_max", "tensor_scalar_min",
            "tensor_reduce", "reduce_max", "reduce_sum", "scalar_tensor_tensor",
            "reciprocal", "memset", "memzero", "activation", "copy", "add", "mul",
            "square", "sqrt", "sign", "select", "transpose", "bn_stats", "bn_aggr")

    def __init__(self, eng):
        self._e = eng
        self._w = []
        self._r = []

    def __getattr__(self, attr):
        f = getattr(self._e, attr)
        if attr in DrainWrap._OPS:
            def wrapped(*a, _f=f, **k):
                aps = [x for x in list(a) + list(k.values())
                       if hasattr(x, "tensor") and hasattr(x, "ap")]
                if aps:
                    wr = [_ap_span(aps[0])]
                    ao = k.get("accum_out")
                    if ao is not None and hasattr(ao, "tensor"):
                        wr.append(_ap_span(ao))
                    rd = [_ap_span(x) for x in aps[1:]]
                    haz = (any(_overlap(x, y) for x in rd + wr for y in self._w)
                           or any(_overlap(x, y) for x in wr for y in self._r))
                    if haz:
                        self._e.drain()
                        self._w = []
                        self._r = []
                    self._w.extend(wr)
                    self._r.extend(rd)
                return _f(*a, **k)
            return wrapped
        return f


class Res:
    def __init__(self, name=""):
        self.name = name
        self.ready = []
        self.users = []

    def acquire(self, C, ename):
        if isinstance(ename, str):
            ename = (ename,)
        for e in ename:
            for (s, v) in self.users:
                C.wait(e, s, v)
        self.users = []
        self.ready = []

    def use(self, C, ename):
        for (s, v) in self.ready:
            C.wait(ename, s, v)


def build_program(nc, do_c=True, nb=NTILE, do_b=True, bstop="full"):
    DI = lambda n, sh, dt=F32: nc.dram_tensor(n, sh, dt, kind="ExternalInput").ap()
    d_xf = DI("xf", [D, T])
    d_wqkvT = DI("wqkvT", [D, 3 * D])
    d_woutT = DI("woutT", [D, D])
    d_boutl = DI("d_boutl", [128, NCH])
    d_w1T = DI("w1T", [D, DH])
    d_w2T = DI("w2T", [DH, D])
    d_convw = DI("d_convw", [128, 3 * HCH])
    d_convb = DI("d_convb", [128, HCH])
    d_cos = DI("d_cosT", [128, T], BF16)
    d_sin = DI("d_sinT", [128, T], BF16)
    d_pswap = DI("d_pswap", [128, 128], BF16)
    d_ident = DI("d_ident", [128, 128], BF16)
    d_maskint = DI("d_maskint", [128, 384], BF16)
    d_maskfirst = DI("d_maskfirst", [128, 256], BF16)
    d_masklast = DI("d_masklast", [128, 256], BF16)
    d_ones1 = DI("d_ones1", [1, 128])
    d_mones1 = DI("d_mones1", [1, 128])
    d_ones1b = DI("d_ones1b", [1, 128], BF16)
    d_mones1b = DI("d_mones1b", [1, 128], BF16)
    d_onescol = DI("d_onescol", [128, 1])
    d_onescolb = DI("d_onescolb", [128, 1], BF16)
    d_identf = DI("d_identf", [128, 128])
    d_out = nc.dram_tensor("outf", [D, T], F32, kind="ExternalOutput").ap()

    import contextlib
    with contextlib.ExitStack() as stk:
        SB = lambda n, sh, dt: stk.enter_context(nc.sbuf_tensor(n, sh, dt))
        PS = lambda n, sh, dt: stk.enter_context(nc.psum_tensor(n, sh, dt))

        xtile = [SB(f"xtile{i}", [128, NCH * TTH], F32) for i in range(2)]
        outsb = [SB(f"outsb{i}", [128, NCH * TT], F32) for i in range(2)]
        mr_rows = SB("mr_rows", [1, T], BF16)
        rstd_rows = SB("rstd_rows", [1, T], BF16)
        ones1b = SB("ones1b", [1, 128], BF16)
        mones1b = SB("mones1b", [1, 128], BF16)
        ones1 = SB("ones1", [1, 128], F32)
        mones1 = SB("mones1", [1, 128], F32)
        onescol = SB("onescol", [128, 1], F32)
        onescolb = SB("onescolb", [128, 1], BF16)
        identf = SB("identf", [128, 128], F32)
        boutl = SB("boutl", [128, NCH], F32)
        convw = SB("convw", [128, 3 * HCH], F32)
        convb = SB("convb", [128, HCH], F32)
        m127 = SB("m127", [1, 2], F32)
        mw2col = SB("mw2col", [128, 1], F32)
        scw1col = SB("scw1col", [128, 1], F32)
        scw2col = SB("scw2col", [128, 1], F32)
        ident = SB("ident", [128, 128], BF16)
        pswap = SB("pswap", [128, 128], BF16)
        maskint = SB("maskint", [128, 384], BF16)
        maskfirst = maskint[:, 128:384]
        masklast = maskint[:, 0:256]
        s2f2 = SB("s2f2", [128, 2], F32)
        fmx32 = SB("fmx32", [128, TTH], F32)
        fmn32 = SB("fmn32", [128, TTH], F32)
        gsc = SB("gsc", [128, 2 * TT], BF16)
        reccol2 = SB("reccol2", [128, 1], F32)
        f2row = SB("f2row", [1, 128], F32)
        s2row = SB("s2row", [1, 128], F32)
        sxrow = SB("sxrow", [1, TTH], F32)
        reccol = SB("reccol", [128, 1], F32)
        rowbuf = SB("rowbuf", [1, 8 * TTH], F32)
        s2b = SB("s2b", [128, TT], F32)
        f2b = SB("f2b", [128, TT], BF16)

        WA = SB("warena", [128, 81920], I8)
        w1s = WA[:, 0:NCH * DH].bitcast(F8)
        w2s = WA[:, NCH * DH:NCH * DH + HCH * D].bitcast(F8)
        wqkv = WA[:, 0:2 * NCH * 3 * D].bitcast(BF16)
        woutw = WA[:, 49152:49152 + 2 * NCH * D].bitcast(BF16)
        cosT = WA[:, 65536:65536 + 2 * T].bitcast(BF16)
        sinT = WA[:, 73728:73728 + 2 * T].bitcast(BF16)
        xs2 = WA[:, 65536:65536 + NCH * TTH * 2].bitcast(BF16)
        xtile3 = WA[:, 65536 + NCH * TTH * 2:65536 + NCH * TTH * 2 + NCH * TTH * 4].bitcast(F32)

        ARENA_SZ = 61824
        AR = SB("arena", [128, ARENA_SZ], I8)

        def carve(off, n, dt):
            size = {F32: 4, BF16: 2, I8: 1, F8: 1}[dt]
            off = (off + 3) // 4 * 4
            ap = AR[:, off:off + n * size]
            return (ap.bitcast(dt) if dt != I8 else ap), off + n * size

        o = 0
        xs, o = carve(o, NCH * TTH, BF16)
        xc_ctr, o = carve(o, NCH * TTH, F32)
        h1_l = []
        for i in range(2):
            a, o = carve(o, HCH * TTH, BF16)
            h1_l.append(a)
        h2, o = carve(o, HCH * TT, BF16)
        assert o <= ARENA_SZ, o
        wtmp = [AR[:, 0:16384].bitcast(F32), AR[:, 16384:32768].bitcast(F32)]
        wq8 = AR[:, 32768:36864]
        wabs_l = [AR[:, 36864:45056].bitcast(BF16), AR[:, 45056 - 8192 + 8192:53248].bitcast(BF16)]
        o = 0
        qsb_l = []
        for i in range(2):
            a, o = carve(o, NCH * TT, BF16)
            qsb_l.append(a)
        kblk_l = []
        for i in range(4):
            a, o = carve(o, NCH * 128, BF16)
            kblk_l.append(a)
        vblk_l = []
        for i in range(4):
            a, o = carve(o, H * (HD + 1), BF16)
            vblk_l.append(a)
        attnT_l = []
        for i in range(2):
            a, o = carve(o, NCH * TT, BF16)
            attnT_l.append(a)
        qraw_l = []
        for i in range(2):
            a, o = carve(o, TT, BF16)
            qraw_l.append(a)
        tr1, o = carve(o, TT, BF16)
        tr2, o = carve(o, TT, BF16)
        xnsb, o = carve(o, NCH * TT, BF16)
        expsb_l = []
        for i in range(2):
            a, o = carve(o, 384, BF16)
            expsb_l.append(a)
        atok_l = []
        for i in range(2):
            a, o = carve(o, H * HD, BF16)
            atok_l.append(a)
        assert o <= ARENA_SZ, o

        pb = [PS(f"pb{i}", [128, 512], F32) for i in range(8)]
        pbT = [pb[i][:, :].bitcast(BF16) for i in range(8)]
        rpb = [Res(f"pb{i}") for i in range(8)]

        sems = {}
        for s in ("L", "LX0", "LX1", "LX2", "LW0", "LW1", "T", "V", "S", "G", "ST0", "ST1"):
            sems[s] = stk.enter_context(nc.semaphore(s))

        with nc.Block() as _block:
            C = Ctr(nc, sems)
            Tn, Sy = nc.tensor, nc.sync
            G = DrainWrap(nc.gpsimd)
            V = DrainWrap(nc.vector)
            S = DrainWrap(nc.scalar)

            r_xtile = [Res("xt0"), Res("xt1")]
            r_wabs = [Res("wa0"), Res("wa1")]
            r_outsb = [Res("ou0"), Res("ou1")]
            r_wtmp = [Res("wt0"), Res("wt1")]
            r_qsb = [Res("q0"), Res("q1")]
            r_kblk = [Res(f"k{i}") for i in range(4)]
            r_vblk = [Res(f"v{i}") for i in range(4)]
            r_attnT = [Res("aT0"), Res("aT1")]
            r_qraw = [Res("qr0"), Res("qr1")]
            r_atok = [Res("atok0"), Res("atok1")]
            r_expsb = [Res("expsb0"), Res("expsb1")]
            r_xs = Res("xs")
            r_xc = Res("xc")
            r_s2b = Res("s2b")
            r_gsc = Res("gsc")
            r_h1b = [Res("h1a"), Res("h1b")]
            r_h2 = Res("h2")
            r_fold = Res("fold")
            r_xnsb = Res("xnsb")

            def dma_in(out_ap, in_ap, sem="L"):
                return C.inc(Sy.dma_start(out=out_ap, in_=in_ap), sem, 16)

            for (dst, src) in [(ones1, d_ones1), (mones1, d_mones1), (onescol, d_onescol),
                               (boutl, d_boutl), (convw, d_convw), (convb, d_convb),
                               (ident, d_ident), (identf, d_identf), (onescolb, d_onescolb),
                               (ones1b, d_ones1b), (mones1b, d_mones1b),
                               (pswap, d_pswap), (maskint, d_maskint)]:
                l_consts = dma_in(dst[:, :], src[:, :])

            C.wait("tensor", "L", l_consts)
            first = True
            tsum = 0
            for wi, (d_w, nch) in enumerate(((d_w1T, NCH), (d_w2T, HCH))):
                wd = d_w.shape[1]
                for i in range(nch):
                    bi = i % 2
                    r_wtmp[bi].acquire(C, "sync")
                    lv = dma_in(wtmp[bi][:, 0:wd], d_w[i * 128:(i + 1) * 128, :], f"LW{bi}")
                    C.wait("scalar", f"LW{bi}", lv)
                    r_wabs[bi].acquire(C, "scalar")
                    sv = C.inc(S.activation(wabs_l[bi][:, 0:wd], wtmp[bi][:, 0:wd], ACT.Abs), "S")
                    r_wtmp[bi].users.append(("S", sv))
                    C.wait("tensor", "S", sv)
                    for ss in range(wd // 256):
                        last = (i == nch - 1) and (ss == wd // 256 - 1)
                        tsum = C.inc(Tn.matmul(pb[6][0:1, wi * 256:wi * 256 + 256],
                                               onescolb[:, :], wabs_l[bi][:, ss * 256:(ss + 1) * 256],
                                               start=first, stop=last), "T")
                        first = False
                    r_wabs[bi].users.append(("T", tsum))
                first = True
            C.wait("vector", "T", tsum)
            V.reduce_sum(rowbuf[0:1, 0:1], pb[6][0:1, 0:256], AX.X)
            V.reduce_sum(rowbuf[0:1, 1:2], pb[6][0:1, 256:512], AX.X)
            V.tensor_scalar(rowbuf[0:1, 2:4], rowbuf[0:1, 0:2], 1.0 / (D * DH), 1e-5,
                            op0=ALU.mult, op1=ALU.max)
            V.tensor_scalar(rowbuf[0:1, 4:6], rowbuf[0:1, 2:4], 1.0 / 127.0, None, op0=ALU.mult)
            vv = C.inc(V.tensor_copy(m127[:, :], rowbuf[0:1, 4:6]), "V")
            C.wait("tensor", "V", vv)
            tv = C.inc(Tn.matmul(pb[7][:, 0:4], ones1[:, :], rowbuf[0:1, 2:6], start=True, stop=True), "T")
            C.wait("vector", "T", tv)
            V.reciprocal(scw1col[:, :], pb[7][:, 0:1])
            V.reciprocal(scw2col[:, :], pb[7][:, 1:2])
            v_scales = C.inc(V.tensor_copy(mw2col[:, :], pb[7][:, 3:4]), "V")
            rpb[6].users.append(("V", v_scales))
            rpb[7].users.append(("V", v_scales))

            vq = 0
            for (d_w, nch, dst, sc) in ((d_w1T, NCH, w1s, scw1col), (d_w2T, HCH, w2s, scw2col)):
                wd = d_w.shape[1]
                for i in range(nch):
                    bi = i % 2
                    r_wtmp[bi].acquire(C, "sync")
                    lv = dma_in(wtmp[bi][:, 0:wd], d_w[i * 128:(i + 1) * 128, :], f"LW{bi}")
                    C.wait("vector", f"LW{bi}", lv)
                    C.wait("vector", "V", v_scales)
                    V.tensor_scalar(wtmp[bi][:, 0:wd], wtmp[bi][:, 0:wd], sc[:, :], 1.49,
                                    op0=ALU.mult, op1=ALU.min)
                    V.tensor_scalar(wtmp[bi][:, 0:wd], wtmp[bi][:, 0:wd], -1.49, RMAGIC,
                                    op0=ALU.max, op1=ALU.add)
                    vq = C.inc(V.tensor_scalar(dst[:, i * wd:(i + 1) * wd], wtmp[bi][:, 0:wd],
                                               -RMAGIC, None, op0=ALU.add), "V")
                    r_wtmp[bi].users.append(("V", vq))
            v_signs = vq

            x3 = [xtile[i][:, :].rearrange("p (c w) -> p c w", c=NCH) for i in range(2)]
            xc3 = xc_ctr.rearrange("p (c w) -> p c w", c=NCH)
            xs3 = xs.rearrange("p (c w) -> p c w", c=NCH)
            RS = lambda k: rowbuf[0:1, k * TTH:(k + 1) * TTH]

            v_rows = 0
            xtile_b = [xtile[0][:, :], xtile[1][:, :], xtile3]
            r_xtb = [r_xtile[0], r_xtile[1], Res("xt3")]
            x3b = [a.rearrange("p (c w) -> p c w", c=NCH) for a in xtile_b]
            xs_l = [xs, xs2]
            xsl3 = [a.rearrange("p (c w) -> p c w", c=NCH) for a in xs_l]
            r_xsl = [r_xs, Res("xs2")]
            for i in range(3):
                vz0 = C.inc(V.memset(xtile_b[i], 0.0), "V")
                r_xtb[i].users.append(("V", vz0))
            bst = {}

            def b_p1(it):
                t0 = it * TT
                s0 = (it // TPS) * TSEQ
                sL = (t0 - 1) >= s0
                sR = (t0 + TT + 1) <= s0 + TSEQ
                g0 = t0 - 1 if sL else t0
                g1 = t0 + TT + 1 if sR else t0 + TT
                W = g1 - g0
                a0 = 0 if sL else 1
                xb = xtile_b[it % 3]
                xc = x3b[it % 3]
                rx = r_xtb[it % 3]
                xs_ = xs_l[it % 2]
                xs3_ = xsl3[it % 2]
                rxs_ = r_xsl[it % 2]

                rx.acquire(C, "sync")
                lxs = f"LX{it % 3}"
                lx = dma_in(
                    xb.rearrange("p (c w) -> p c w", c=NCH)[:, :, a0:a0 + W],
                    d_xf[:, g0:g1].rearrange("(c p) w -> p c w", p=128), lxs)

                rpb[6].acquire(C, "tensor")
                rpb[7].acquire(C, "tensor")
                C.wait("tensor", lxs, lx)
                for c in range(NCH):
                    C.inc(Tn.matmul(pb[6][0:1, 0:TTH], onescol[:, :], xc[:, c, 0:TTH],
                                    start=(c == 0), stop=(c == NCH - 1)), "T")
                C.wait("scalar", lxs, lx)
                r_xc.acquire(C, "scalar")
                tq_slot = [0, 0]
                for c in range(NCH):
                    C.wait("scalar", "T", tq_slot[c % 2])
                    sv = C.inc(S.activation(xc3[:, c % 2, 0:TTH],
                                            xc[:, c, 0:TTH], ACT.Square), "S")
                    C.wait("tensor", "S", sv)
                    tq = C.inc(Tn.matmul(pb[7][0:1, 0:TTH], onescol[:, :],
                                         xc3[:, c % 2, 0:TTH],
                                         start=(c == 0), stop=(c == NCH - 1)), "T")
                    tq_slot[c % 2] = tq

                C.wait("scalar", "T", tq)
                sv_mean = C.inc(S.activation(RS(0), pb[6][0:1, 0:TTH], ACT.Copy, scale=1.0 / D), "S")
                rpb[6].users.append(("S", sv_mean))
                sv = C.inc(S.activation(RS(1), pb[7][0:1, 0:TTH], ACT.Copy, scale=1.0 / D), "S")
                rpb[7].users.append(("S", sv))
                sv2 = C.inc(S.activation(RS(2), RS(0), ACT.Square), "S")
                C.wait("vector", "S", sv2)
                vv = C.inc(V.tensor_sub(RS(1), RS(1), RS(2)), "V")
                vv = C.inc(V.tensor_scalar(RS(1), RS(1), EPS, None, op0=ALU.add), "V")
                C.wait("scalar", "V", vv)
                sv = C.inc(S.activation(RS(2), RS(1), ACT.Ln), "S")
                sv_rstd = C.inc(S.activation(RS(3), RS(2), ACT.Exp, scale=-0.5), "S")
                C.wait("vector", "S", sv_rstd)
                v_mr = C.inc(V.tensor_mul(RS(2), RS(0), RS(3)), "V")

                rpb[6].acquire(C, "tensor")
                C.wait("tensor", "S", sv_mean)
                t_meanB = C.inc(Tn.matmul(pb[6][:, 0:TTH], ones1[:, :], RS(0), start=True, stop=True), "T")
                C.wait("vector", "T", t_meanB)
                for c in range(NCH):
                    C.inc(V.tensor_tensor(xc3[:, c, 0:TTH], xc[:, c, 0:TTH], pb[6][:, 0:TTH],
                                          op=ALU.subtract), "V")
                    if c == 0:
                        V.tensor_copy(fmx32[:, 0:TTH], xc3[:, c, 0:TTH])
                        vv = C.inc(V.tensor_copy(fmn32[:, 0:TTH], xc3[:, c, 0:TTH]), "V")
                    else:
                        V.tensor_tensor(fmx32[:, 0:TTH], fmx32[:, 0:TTH], xc3[:, c, 0:TTH],
                                        op=ALU.max)
                        vv = C.inc(V.tensor_tensor(fmn32[:, 0:TTH], fmn32[:, 0:TTH], xc3[:, c, 0:TTH],
                                                   op=ALU.min), "V")
                V.tensor_scalar(fmn32[:, 0:TTH], fmn32[:, 0:TTH], -1.0, None, op0=ALU.mult)
                v_fold = C.inc(V.tensor_tensor(fmx32[:, 0:TTH], fmx32[:, 0:TTH], fmn32[:, 0:TTH],
                                               op=ALU.max), "V")
                rpb[6].users.append(("V", v_fold))

                rpb[7].acquire(C, "tensor")
                C.wait("tensor", "V", v_fold)
                cuts = [0, 128, TTH - 128]
                ttr = [C.inc(Tn.transpose(pb[7][:, j * 128:(j + 1) * 128], fmx32[:, cu:cu + 128],
                                          identf[:, :]), "T") for j, cu in enumerate(cuts)]
                for j, cu in enumerate(cuts):
                    C.wait("vector", "T", ttr[-1])
                    vv = C.inc(V.reduce_max(reccol[:, :], pb[7][:, j * 128:(j + 1) * 128], AX.X), "V")
                    C.wait("tensor", "V", vv)
                    tv = C.inc(Tn.matmul(pb[6][0:1, 300:428], reccol[:, :], identf[:, :],
                                         start=True, stop=True), "T")
                    C.wait("vector", "T", tv)
                    vv = C.inc(V.tensor_copy(sxrow[0:1, cu:cu + 128], pb[6][0:1, 300:428]), "V")
                rpb[6].users.append(("V", vv))
                rpb[7].users.append(("V", vv))

                vv = C.inc(V.tensor_mul(RS(4), sxrow[0:1, 0:TTH], RS(3)), "V")
                vv = C.inc(V.tensor_scalar(RS(4), RS(4), 1e-5, None, op0=ALU.max), "V")
                vv = C.inc(V.reciprocal(RS(5), RS(4)), "V")
                vv = C.inc(V.tensor_scalar(RS(5), RS(5), 127.0, None, op0=ALU.mult), "V")
                vv_a = C.inc(V.tensor_mul(RS(6), RS(3), RS(5)), "V")
                vv_f1 = C.inc(V.tensor_scalar(RS(7), RS(4), m127[0:1, 0:1], None, op0=ALU.mult), "V")
                C.inc(V.tensor_copy(rstd_rows[0:1, t0:t0 + TT], RS(3)[0:1, 1:257]), "V")
                vr = C.inc(V.tensor_copy(mr_rows[0:1, t0:t0 + TT], RS(2)[0:1, 1:257]), "V")

                rpb[7].acquire(C, "tensor")
                C.wait("tensor", "V", vv_a)
                t_aB = C.inc(Tn.matmul(pb[7][:, 0:TTH], ones1[:, :], RS(6), start=True, stop=True), "T")
                rpb[6].acquire(C, "tensor")
                C.wait("tensor", "V", vv_f1)
                t_f1B = C.inc(Tn.matmul(pb[6][:, 0:TTH], ones1[:, :], RS(7), start=True, stop=True), "T")
                C.wait("vector", "T", t_aB)
                rxs_.acquire(C, "vector")
                V.tensor_tensor(
                    xc3[:, :, :], xc3[:, :, :],
                    pb[7][:, 0:TTH].unsqueeze(1).broadcast_to((128, NCH, TTH)), op=ALU.mult)
                vv = C.inc(V.tensor_scalar(xs3_[:, :, :], xc3[:, :, :], RMAGIC, -RMAGIC,
                                           op0=ALU.add, op1=ALU.add), "V")
                rpb[7].users.append(("V", vv))
                r_xc.users.append(("V", vv))
                C.wait("vector", "T", t_f1B)
                v_xs = C.inc(V.tensor_tensor(
                    xs3_[:, :, :], xs3_[:, :, :],
                    pb[6][:, 0:TTH].unsqueeze(1).broadcast_to((128, NCH, TTH)), op=ALU.mult), "V")
                rpb[6].users.append(("V", v_xs))
                if not sL:
                    v_xs = C.inc(V.memset(xs3_[:, :, 0:1], 0.0), "V")
                if not sR:
                    v_xs = C.inc(V.memset(xs3_[:, :, TTH - 1:TTH], 0.0), "V")
                bst[it] = {"v_xs": v_xs, "v_rows": vr}

            def b_p2(it):
                st_ = bst[it]
                xs_ = xs_l[it % 2]
                rxs_ = r_xsl[it % 2]
                C.wait("tensor", "V", st_["v_xs"])
                C.wait("tensor", "V", v_signs)
                h1b = h1_l[it % 2]
                rh = r_h1b[it % 2]
                rh.acquire(C, "scalar")
                h1ev = []
                for cc in range(HCH):
                    bk = 2 + cc % 4
                    rpb[bk].acquire(C, "tensor")
                    for dc in range(NCH):
                        tm = C.inc(Tn.matmul(pb[bk][:, 0:TTH],
                                             w1s[:, dc * DH + cc * 128: dc * DH + cc * 128 + 128],
                                             xs_[:, dc * TTH:(dc + 1) * TTH],
                                             start=(dc == 0), stop=(dc == NCH - 1)), "T")
                    C.wait("scalar", "T", tm)
                    se = C.inc(S.copy(h1b[:, cc * TTH:(cc + 1) * TTH], pb[bk][:, 0:TTH]), "S")
                    rpb[bk].users.append(("S", se))
                    h1ev.append(("S", se))
                rxs_.users.append(("T", tm))
                st_["h1ev"] = h1ev

            def b_p3a(it):
                st_ = bst[it]
                h1ev = st_["h1ev"]
                h1b = h1_l[it % 2]
                rh = r_h1b[it % 2]
                r_h2.acquire(C, ("vector", "scalar", "gpsimd"))
                r_gsc.acquire(C, "gpsimd")
                sI = {}
                vA = {}
                gB = {}
                silu_done = [None] * HCH
                for it2 in range(HCH + 7):
                    cc = it2
                    if cc < HCH:
                        C.wait("scalar", h1ev[cc][0], h1ev[cc][1])
                        z = h2[:, cc * TT:(cc + 1) * TT]
                        sI[cc] = C.inc(S.activation(z, h1b[:, cc * TTH + 0: cc * TTH + TT],
                                                    ACT.Identity, scale=convw[:, cc:cc + 1],
                                                    bias=convb[:, cc:cc + 1]), "S")
                    c1 = it2 - 4
                    if 0 <= c1 < HCH:
                        C.wait("vector", "S", sI[c1])
                        C.wait("vector", "G", gB[c1])
                        C.wait("vector", h1ev[c1][0], h1ev[c1][1])
                        z = h2[:, c1 * TT:(c1 + 1) * TT]
                        V.scalar_tensor_tensor(z, h1b[:, c1 * TTH + 1: c1 * TTH + 1 + TT],
                                               convw[:, 32 + c1:32 + c1 + 1], z,
                                               op0=ALU.mult, op1=ALU.add)
                        g = gsc[:, (c1 % 2) * TT:(c1 % 2) * TT + TT]
                        vA[c1] = C.inc(V.tensor_tensor(z, z, g, op=ALU.add), "V")
                    c2 = it2 - 2
                    if 0 <= c2 < HCH:
                        g = gsc[:, (c2 % 2) * TT:(c2 % 2) * TT + TT]
                        C.wait("gpsimd", h1ev[c2][0], h1ev[c2][1])
                        if c2 - 2 in vA:
                            C.wait("gpsimd", "V", vA[c2 - 2])
                        gB[c2] = C.inc(G.tensor_tensor(
                            g, h1b[:, c2 * TTH + 2: c2 * TTH + 2 + TT],
                            convw[:, 64 + c2:64 + c2 + 1].broadcast_to((128, TT)),
                            op=ALU.mult), "G")
                    c3 = it2 - 7
                    if 0 <= c3 < HCH and c3 % 4 == 3:
                        C.wait("scalar", "V", vA[c3])
                        zb = h2[:, (c3 - 3) * TT:(c3 + 1) * TT]
                        sz = C.inc(S.activation(zb, zb, ACT.Silu), "S")
                        for c4 in range(c3 - 3, c3 + 1):
                            silu_done[c4] = sz
                rh.users.append(("V", vA[HCH - 1]))
                rh.users.append(("S", sI[HCH - 1]))
                r_gsc.users.append(("V", vA[HCH - 2]))
                r_gsc.users.append(("V", vA[HCH - 1]))

                fx3 = h1b[:, 0:16 * TT].rearrange("p (c t) -> p c t", c=16)
                fn3 = h1b[:, 16 * TT:32 * TT].rearrange("p (c t) -> p c t", c=16)
                h23v = h2.rearrange("p (c t) -> p c t", c=HCH)
                C.wait("vector", "S", silu_done[HCH - 1])
                V.tensor_tensor(fx3[:, :, :], h23v[:, 0:16, :], h23v[:, 16:32, :], op=ALU.max)
                V.tensor_tensor(fn3[:, :, :], h23v[:, 0:16, :], h23v[:, 16:32, :], op=ALU.min)
                for lv in (8, 4, 2, 1):
                    V.tensor_tensor(fx3[:, 0:lv, :], fx3[:, 0:lv, :], fx3[:, lv:2 * lv, :], op=ALU.max)
                    V.tensor_tensor(fn3[:, 0:lv, :], fn3[:, 0:lv, :], fn3[:, lv:2 * lv, :], op=ALU.min)
                V.tensor_scalar(fn3[:, 0, :], fn3[:, 0, :], -1.0, None, op0=ALU.mult)
                vf = C.inc(V.tensor_tensor(fx3[:, 0, :], fx3[:, 0, :], fn3[:, 0, :],
                                           op=ALU.max), "V")
                st_["vf"] = vf

            def b_p3b(it):
                st_ = bst[it]
                h1b = h1_l[it % 2]
                rh = r_h1b[it % 2]
                vf = st_["vf"]
                rpb[0].acquire(C, "tensor")
                C.wait("tensor", "V", vf)
                tt1 = C.inc(Tn.transpose(pbT[0][:, 0:128], h1b[:, 0:128], ident[:, :]), "T")
                tt2 = C.inc(Tn.transpose(pbT[0][:, 128:256], h1b[:, 128:256], ident[:, :]), "T")
                rh.users.append(("T", tt2))
                rh.users.append(("V", vf))
                svals = []
                for j in range(2):
                    C.wait("vector", "T", tt2)
                    vv = C.inc(V.reduce_max(reccol2[:, :], pbT[0][:, j * 128:(j + 1) * 128], AX.X), "V")
                    vv = C.inc(V.tensor_scalar(s2f2[:, 0:1], reccol2[:, :], 1e-5, None, op0=ALU.max), "V")
                    vv = C.inc(V.reciprocal(s2f2[:, 1:2], s2f2[:, 0:1]), "V")
                    V.tensor_scalar(s2f2[:, 1:2], s2f2[:, 1:2], 127.0, None, op0=ALU.mult)
                    vv = C.inc(V.tensor_scalar(s2f2[:, 0:1], s2f2[:, 0:1], mw2col[:, :], None,
                                               op0=ALU.mult), "V")
                    C.wait("tensor", "V", vv)
                    tv = C.inc(Tn.matmul(pb[0][0:1, 300:428], s2f2[:, 0:1], identf[:, :],
                                         start=True, stop=True), "T")
                    tv = C.inc(Tn.matmul(pb[0][0:1, 172:300], s2f2[:, 1:2], identf[:, :],
                                         start=True, stop=True), "T")
                    C.wait("vector", "T", tv)
                    vv = C.inc(V.tensor_copy(f2row[0:1, 0:128], pb[0][0:1, 300:428]), "V")
                    vv = C.inc(V.tensor_copy(s2row[0:1, 0:128], pb[0][0:1, 172:300]), "V")
                    C.wait("tensor", "V", vv)
                    if j == 0:
                        rpb[1].acquire(C, "tensor")
                    else:
                        C.wait("tensor", "S", svals[-1])
                    tb = C.inc(Tn.matmul(pb[1][:, j * 128:(j + 1) * 128],
                                         ones1[:, :], s2row[0:1, 0:128], start=True, stop=True), "T")
                    tb2 = C.inc(Tn.matmul(pb[1][:, 256 + j * 128:256 + (j + 1) * 128],
                                          ones1[:, :], f2row[0:1, 0:128], start=True, stop=True), "T")
                    C.wait("scalar", "T", tb2)
                    if j == 0:
                        r_s2b.acquire(C, "scalar")
                    svals.append(C.inc(S.copy(s2b[:, j * 128:(j + 1) * 128],
                                              pb[1][:, j * 128:(j + 1) * 128]), "S"))
                    svals.append(C.inc(S.copy(f2b[:, j * 128:(j + 1) * 128],
                                              pb[1][:, 256 + j * 128:256 + (j + 1) * 128]), "S"))
                rpb[0].users.append(("V", vv))
                rpb[1].users.append(("S", svals[-1]))

                C.wait("vector", "S", svals[-1])
                h23 = h2.rearrange("p (c t) -> p c t", c=HCH)
                V.tensor_tensor(h23[:, :, :], h23[:, :, :],
                                s2b[:, :].unsqueeze(1).broadcast_to((128, HCH, TT)),
                                op=ALU.mult)
                v_hq = C.inc(V.tensor_scalar(h23[:, :, :], h23[:, :, :], RMAGIC, -RMAGIC,
                                             op0=ALU.add, op1=ALU.add), "V")
                st_["v_hq"] = v_hq
                r_h2.users.append(("V", v_hq))
                rh.users.append(("T", None))
                rh.users.pop()

            def b_p45(it):
                st_ = bst.pop(it)
                t0 = it * TT
                xc = x3b[it % 3]
                rx = r_xtb[it % 3]
                C.wait("tensor", "V", st_["v_hq"])
                ob = outsb[it % 2]
                ro = r_outsb[it % 2]
                ro.acquire(C, "vector")
                for oc in (0, 2, 4, 6, 1, 3, 5, 7):
                    bk = 2 + oc // 2
                    half = (oc % 2) * 256
                    rpb[bk].acquire(C, "tensor")
                    for cc in range(HCH):
                        tm = C.inc(Tn.matmul(pb[bk][:, half:half + TT],
                                             w2s[:, cc * D + oc * 128: cc * D + oc * 128 + 128],
                                             h2[:, cc * TT:(cc + 1) * TT],
                                             start=(cc == 0), stop=(cc == HCH - 1)), "T")
                    C.wait("vector", "T", tm)
                    dst = ob[:, oc * TT:(oc + 1) * TT]
                    V.tensor_tensor(dst, pb[bk][:, half:half + TT], f2b[:, :], op=ALU.mult)
                    ve = C.inc(V.scalar_tensor_tensor(dst, xc[:, oc, 1:257], 1.0, dst,
                                                      op0=ALU.mult, op1=ALU.add), "V")
                    rpb[bk].users.append(("V", ve))
                    C.wait("scalar", "V", ve)
                    ve2 = C.inc(S.activation(dst, dst, ACT.Identity, scale=1.0,
                                             bias=boutl[:, oc:oc + 1]), "S")
                    b_p45.sevs.append(("S", ve2))
                rx.users.append(("V", ve))
                r_s2b.users.append(("V", ve))
                r_h2.users.append(("T", tm))
                for u in b_p45.sevs:
                    C.wait("gpsimd", u[0], u[1])
                b_p45.sevs = []
                stv = C.inc(G.dma_start(out=d_out[:, t0:t0 + TT].rearrange("(c p) w -> p c w", p=128),
                                        in_=ob[:, :].rearrange("p (c w) -> p c w", c=NCH)), f"ST{it % 2}", 16)
                ro.users.append((f"ST{it % 2}", stv))
                return vr if False else st_["v_rows"]

            b_p45.sevs = []
            nb_ = nb if do_b else 0
            if nb_ > 0:
                b_p1(0)
                b_p2(0)
                b_p1(1)
                for it in range(nb_):
                    if it + 2 < nb_:
                        b_p1(it + 2)
                    b_p3a(it)
                    if it + 1 < nb_:
                        b_p2(it + 1)
                    b_p3b(it)
                    v_rows = b_p45(it)
            b_rows_done = v_rows
            if bstop != "full":
                C.inc(G.dma_start(out=d_out[:, 0:TT].rearrange("(c p) w -> p c w", p=128),
                                  in_=outsb[0][:, :].rearrange("p (c w) -> p c w", c=NCH)), "ST0", 16)
                C.wait("gpsimd", "ST0", C.c.get("ST0", 0))
                return
            if not do_c:
                if not do_b:
                    C.wait("gpsimd", "V", v_signs)
                    C.inc(G.dma_start(out=d_out[:, 0:TT].rearrange("(c p) w -> p c w", p=128),
                                      in_=outsb[0][:, :].rearrange("p (c w) -> p c w", c=NCH)), "ST0", 16)
                C.wait("gpsimd", "ST0", C.c.get("ST0", 0))
                C.wait("gpsimd", "ST1", C.c.get("ST1", 0))
                return

            last_ffn_T = C.c["T"]
            C.wait("sync", "T", last_ffn_T)
            C.wait("sync", "ST0", C.c.get("ST0", 0))
            C.wait("sync", "ST1", C.c.get("ST1", 0))
            l_tab = dma_in(cosT, d_cos[:, :])
            l_tab = dma_in(sinT, d_sin[:, :])
            for (d_w, nch, dst, wd) in ((d_wqkvT, NCH, wqkv, 3 * D), (d_woutT, NCH, woutw, D)):
                for i in range(nch):
                    bi = i % 2
                    r_wtmp[bi].acquire(C, "sync")
                    lv = dma_in(wtmp[bi][:, 0:wd], d_w[i * 128:(i + 1) * 128, :], f"LW{bi}")
                    C.wait("scalar", f"LW{bi}", lv)
                    wl = C.inc(S.copy(dst[:, i * wd:(i + 1) * wd], wtmp[bi][:, 0:wd]), "S")
                    r_wtmp[bi].users.append(("S", wl))
            s_wmm = wl

            C.wait("vector", "S", s_wmm)
            C.wait("vector", "L", l_tab)
            for b in range(4):
                vb3 = vblk_l[b].rearrange("p (h d) -> p h d", h=H)
                vinit = C.inc(V.memset(vb3[:, :, HD:HD + 1], 1.0), "V")

            qsb3 = [qsb_l[i].rearrange("p (c t) -> p c t", c=NCH) for i in range(2)]
            xn3 = xnsb.rearrange("p (c t) -> p c t", c=NCH)
            state = {"rot": 0, "trot": 0, "erot": 0}
            it_info = {}

            def blk_meta(gb):
                bl = gb % 16
                if bl == 0:
                    return [0, 1], maskfirst
                if bl == 15:
                    return [-1, 0], masklast
                return [-1, 0, 1], maskint

            def em_sc(it):
                gb, h = it
                deltas, mask = blk_meta(gb)
                jq, u = gb // 2, gb % 2
                ch, ro_ = h // 2, (h % 2) * 64
                sb_ = 0 + (state["rot"] % 2)
                pv_ = 2 + (state["rot"] % 2)
                state["rot"] += 1
                it_info[it] = ii = {"sb": sb_, "pv": pv_}
                rpb[sb_].acquire(C, "tensor")
                for ci, dl in enumerate(deltas):
                    kb = (gb + dl) % 4
                    r_kblk[kb].use(C, "tensor")
                    r_qsb[jq % 2].use(C, "tensor")
                    tsc = C.inc(Tn.matmul(
                        pb[sb_][:, ci * 128:(ci + 1) * 128],
                        kblk_l[kb][ro_:ro_ + 64, ch * 128:(ch + 1) * 128],
                        qsb3[jq % 2][ro_:ro_ + 64, ch, u * 128:(u + 1) * 128],
                        start=True, stop=True), "T")
                ii["tsc"] = tsc

            def em_exp(it):
                gb, h = it
                deltas, mask = blk_meta(gb)
                nck = len(deltas)
                ii = it_info[it]
                es = state["erot"] % 2
                state["erot"] += 1
                ii["es"] = es
                r_expsb[es].acquire(C, "scalar")
                C.wait("scalar", "T", ii["tsc"])
                se = C.inc(S.activation(expsb_l[es][:, 0:nck * 128], pb[ii["sb"]][:, 0:nck * 128],
                                        ACT.Exp, scale=0.125), "S")
                rpb[ii["sb"]].users.append(("S", se))
                ii["se"] = se

            def em_mask(it):
                gb, h = it
                deltas, mask = blk_meta(gb)
                nck = len(deltas)
                ii = it_info[it]
                es = ii["es"]
                C.wait("vector", "S", ii["se"])
                vm = C.inc(V.tensor_tensor(expsb_l[es][:, 0:nck * 128], expsb_l[es][:, 0:nck * 128],
                                           mask[:, 0:nck * 128], op=ALU.mult), "V")
                ii["vm"] = vm

            def em_pv(it):
                gb, h = it
                deltas, mask = blk_meta(gb)
                nck = len(deltas)
                ii = it_info[it]
                es = ii["es"]
                pv_ = ii["pv"]
                rpb[pv_].acquire(C, "tensor")
                C.wait("tensor", "V", ii["vm"])
                for ci, dl in enumerate(deltas):
                    vb = (gb + dl) % 4
                    r_vblk[vb].use(C, "tensor")
                    tpv = C.inc(Tn.matmul(pb[pv_][:, 0:HD + 1],
                                          expsb_l[es][:, ci * 128:(ci + 1) * 128],
                                          vblk_l[vb][:, h * (HD + 1):(h + 1) * (HD + 1)],
                                          start=(ci == 0), stop=(ci == nck - 1)), "T")
                    r_vblk[vb].users.append(("T", tpv))
                r_expsb[es].users.append(("T", tpv))
                ii["tpv"] = tpv

            def em_norm(it):
                gb, h = it
                ii = it_info[it]
                pv_ = ii["pv"]
                C.wait("vector", "T", ii["tpv"])
                if h == 0:
                    r_atok[gb % 2].acquire(C, "vector")
                V.reciprocal(reccol[:, :], pb[pv_][:, HD:HD + 1])
                va = C.inc(V.tensor_scalar(atok_l[gb % 2][:, h * 64:(h + 1) * 64], pb[pv_][:, 0:HD],
                                           reccol[:, :], None, op0=ALU.mult), "V")
                rpb[pv_].users.append(("V", va))
                ii["va"] = va

            def em_tr(gb, va):
                jq, u = gb // 2, gb % 2
                for ch in range(NCH):
                    tb_ = 4 + (state["trot"] % 2)
                    state["trot"] += 1
                    rpb[tb_].acquire(C, "tensor")
                    C.wait("tensor", "V", va)
                    ttv = C.inc(Tn.transpose(pbT[tb_][:, 0:128],
                                             atok_l[gb % 2][:, ch * 128:(ch + 1) * 128],
                                             ident[:, :]), "T")
                    r_atok[gb % 2].users.append(("T", ttv))
                    C.wait("scalar", "T", ttv)
                    sev = C.inc(S.copy(attnT_l[jq % 2][:, ch * TT + u * 128: ch * TT + u * 128 + 128],
                                       pbT[tb_][:, 0:128]), "S")
                    rpb[tb_].users.append(("S", sev))
                    r_attnT[jq % 2].ready.append(("S", sev))

            def attn_tile(blocks, s_):
                iters = [(s_ * 16 + bl, h) for bl in blocks for h in range(H)]
                pend = []
                em_sc(iters[0])
                em_sc(iters[1])
                for i, it in enumerate(iters):
                    em_exp(it)
                    em_mask(it)
                    if i + 2 < len(iters):
                        em_sc(iters[i + 2])
                    em_pv(it)
                    em_norm(it)
                    gb, h = it
                    if h == H - 1:
                        pend.append((i + 2, gb, it_info[it]["va"]))
                    while pend and pend[0][0] <= i:
                        _, g_, va_ = pend.pop(0)
                        em_tr(g_, va_)
                for _, g_, va_ in pend:
                    em_tr(g_, va_)

            def wout_tile(tj):
                t0 = tj * TT
                ob = outsb[tj % 2]
                ro = r_outsb[tj % 2]
                r_attnT[tj % 2].use(C, "tensor")
                evs = []
                tm_bank = {}
                for oc in (0, 2, 4, 6, 1, 3, 5, 7):
                    bk = oc // 2
                    half = (oc % 2) * 256
                    rpb[bk].acquire(C, "tensor")
                    for dc in range(NCH):
                        tm = C.inc(Tn.matmul(pb[bk][:, half:half + TT],
                                             woutw[:, dc * D + oc * 128: dc * D + oc * 128 + 128],
                                             attnT_l[tj % 2][:, dc * TT:(dc + 1) * TT],
                                             start=(dc == 0), stop=(dc == NCH - 1)), "T")
                    tm_bank[bk] = tm
                    if oc % 2 == 1:
                        C.wait("vector", "T", tm)
                        C.wait("vector", f"ST{tj % 2}", wout_tile.rb[tj])
                        ve = C.inc(V.tensor_tensor(ob[:, bk * 512:(bk + 1) * 512],
                                                   ob[:, bk * 512:(bk + 1) * 512],
                                                   pb[bk][:, 0:512], op=ALU.add), "V")
                        rpb[bk].users.append(("V", ve))
                        evs.append(ve)
                C.wait("gpsimd", "V", evs[-1])
                stv = C.inc(G.dma_start(out=d_out[:, t0:t0 + TT].rearrange("(c p) w -> p c w", p=128),
                                        in_=ob[:, :].rearrange("p (c w) -> p c w", c=NCH)), f"ST{tj % 2}", 16)
                ro.users.append((f"ST{tj % 2}", stv))
                r_attnT[tj % 2].users.append(("T", tm))
                return stv

            wout_tile.rb = {}

            def readback(tj):
                ob = outsb[tj % 2]
                ro = r_outsb[tj % 2]
                ro.acquire(C, "gpsimd")
                wout_tile.rb[tj] = C.inc(
                    G.dma_start(out=ob[:, :].rearrange("p (c w) -> p c w", c=NCH),
                                in_=d_out[:, tj * TT:tj * TT + TT].rearrange("(c p) w -> p c w", p=128)),
                    f"ST{tj % 2}", 16)

            st_final = 0
            for j in range(NTILE):
                jl = j % TPS
                t0 = j * TT
                xb = xtile[j % 2]
                rx = r_xtile[j % 2]

                rx.acquire(C, "sync")
                lxs = f"LX{j % 2}"
                lx = dma_in(
                    xb[:, :].rearrange("p (c w) -> p c w", c=NCH)[:, :, 1:257],
                    d_xf[:, t0:t0 + TT].rearrange("(c p) w -> p c w", p=128), lxs)
                if jl >= 1:
                    readback(j - 1)
                rpb[4].acquire(C, "tensor")
                C.wait("tensor", "V", b_rows_done)
                C.inc(Tn.matmul(pb[4][:, 0:128], ones1b[:, :], rstd_rows[0:1, t0:t0 + 128],
                                start=True, stop=True), "T")
                C.inc(Tn.matmul(pb[4][:, 128:256], ones1b[:, :], rstd_rows[0:1, t0 + 128:t0 + 256],
                                start=True, stop=True), "T")
                C.inc(Tn.matmul(pb[4][:, 256:384], mones1b[:, :], mr_rows[0:1, t0:t0 + 128],
                                start=True, stop=True), "T")
                tb2 = C.inc(Tn.matmul(pb[4][:, 384:512], mones1b[:, :], mr_rows[0:1, t0 + 128:t0 + 256],
                                      start=True, stop=True), "T")
                C.wait("vector", lxs, lx)
                C.wait("vector", "T", tb2)
                r_xnsb.acquire(C, "vector")
                V.tensor_tensor(xn3[:, :, :], x3[j % 2][:, :, 1:257],
                                pb[4][:, 0:256].unsqueeze(1).broadcast_to((128, NCH, 256)),
                                op=ALU.mult)
                v_xn = C.inc(V.tensor_tensor(xn3[:, :, :], xn3[:, :, :],
                                             pb[4][:, 256:512].unsqueeze(1).broadcast_to((128, NCH, 256)),
                                             op=ALU.add), "V")
                rpb[4].users.append(("V", v_xn))
                rx.users.append(("V", v_xn))

                C.wait("tensor", "V", v_xn)
                C.wait("tensor", "S", s_wmm)
                qk_info = {}

                def qk_mm(i):
                    kind, ch = ("q", "k")[i // 8], i % 8
                    qb = 6 + (i % 2)
                    woff = (0 if kind == "q" else D) + ch * 128
                    rpb[qb].acquire(C, "tensor")
                    for dc in range(NCH):
                        tm = C.inc(Tn.matmul(pb[qb][:, 0:TT],
                                             wqkv[:, dc * 3 * D + woff: dc * 3 * D + woff + 128],
                                             xnsb[:, dc * TT:(dc + 1) * TT],
                                             start=(dc == 0), stop=(dc == NCH - 1)), "T")
                    r_xnsb.users.append(("T", tm))
                    qk_info[i] = {"qb": qb, "tm": tm}

                def qk_evac(i):
                    ii = qk_info[i]
                    qr = i % 2
                    ii["qr"] = qr
                    r_qraw[qr].acquire(C, "scalar")
                    C.wait("scalar", "T", ii["tm"])
                    se = C.inc(S.copy(qraw_l[qr][:, 0:TT], pb[ii["qb"]][:, 0:TT]), "S")
                    rpb[ii["qb"]].users.append(("S", se))
                    ii["se"] = se

                def qk_swap(i):
                    ii = qk_info[i]
                    sw = 4 + (i % 2)
                    ii["sw"] = sw
                    rpb[sw].acquire(C, "tensor")
                    C.wait("tensor", "S", ii["se"])
                    ii["tsw"] = C.inc(Tn.matmul(pb[sw][:, 0:TT], pswap[:, :],
                                                qraw_l[ii["qr"]][:, 0:TT],
                                                start=True, stop=True), "T")

                def qk_rope(i):
                    ii = qk_info[i]
                    kind, ch = ("q", "k")[i // 8], i % 8
                    C.wait("vector", "T", ii["tsw"])
                    V.tensor_tensor(tr1, pb[ii["sw"]][:, 0:TT], sinT[:, t0:t0 + TT], op=ALU.mult)
                    V.tensor_tensor(tr2, qraw_l[ii["qr"]][:, 0:TT], cosT[:, t0:t0 + TT], op=ALU.mult)
                    if kind == "q":
                        if ch == 0:
                            r_qsb[j % 2].acquire(C, "vector")
                        vr = C.inc(V.tensor_tensor(qsb3[j % 2][:, ch, :], tr1, tr2, op=ALU.add), "V")
                        r_qsb[j % 2].ready.append(("V", vr))
                    else:
                        for half in range(2):
                            kb = (2 * j + half) % 4
                            if ch == 0:
                                r_kblk[kb].acquire(C, "vector")
                            vr = C.inc(V.tensor_tensor(
                                kblk_l[kb][:, ch * 128:(ch + 1) * 128],
                                tr1[:, half * 128:(half + 1) * 128],
                                tr2[:, half * 128:(half + 1) * 128], op=ALU.add), "V")
                            r_kblk[kb].ready.append(("V", vr))
                    rpb[ii["sw"]].users.append(("V", vr))
                    r_qraw[ii["qr"]].users.append(("V", vr))

                qk_mm(0)
                for i in range(16):
                    qk_evac(i)
                    if i + 1 < 16:
                        qk_mm(i + 1)
                    qk_swap(i)
                    qk_rope(i)
                for u in range(2):
                    vbi = (2 * j + u) % 4
                    r_vblk[vbi].acquire(C, "vector")
                    for hf in range(2):
                        qb = 6 + hf
                        rpb[qb].acquire(C, "tensor")
                        for dc in range(NCH):
                            tm = C.inc(Tn.matmul(
                                pb[qb][:, 0:512],
                                xn3[:, dc, u * 128:(u + 1) * 128],
                                wqkv[:, dc * 3 * D + 2 * D + hf * 512: dc * 3 * D + 2 * D + (hf + 1) * 512],
                                start=(dc == 0), stop=(dc == NCH - 1)), "T")
                        r_xnsb.users.append(("T", tm))
                        C.wait("vector", "T", tm)
                        vb3 = vblk_l[vbi].rearrange("p (h d) -> p h d", h=H)
                        vev = C.inc(V.tensor_copy(
                            vb3[:, hf * 8:(hf + 1) * 8, 0:HD],
                            pb[qb][:, 0:512].rearrange("p (h d) -> p h d", h=8)), "V")
                        rpb[qb].users.append(("V", vev))
                        r_vblk[vbi].ready.append(("V", vev))

                s_ = j // TPS
                blocks = [0] if jl == 0 else [2 * jl - 1, 2 * jl]
                if jl == TPS - 1:
                    blocks.append(15)
                attn_tile(blocks, s_)

                if jl >= 1:
                    st_final = wout_tile(j - 1)
                if jl == TPS - 1:
                    readback(j)
                    st_final = wout_tile(j)

            C.wait("gpsimd", "ST0", C.c.get("ST0", 0))
            C.wait("gpsimd", "ST1", C.c.get("ST1", 0))
    return nc


_BUILT = {}


def _get_nc():
    if "nc" not in _BUILT:
        nc = bass.Bass("TRN2", target_bir_lowering=False, debug=False)
        build_program(nc)
        _BUILT["nc"] = nc
    return _BUILT["nc"]


def _consts():
    if "c" in _BUILT:
        return _BUILT["c"]
    pos = np.arange(TSEQ, dtype=np.float64)
    inv = 1.0 / (10000.0 ** (np.arange(0, HD, 2, dtype=np.float64) / HD))
    fr = pos[:, None] * inv[None, :]
    cosv = np.cos(fr)
    sinv = np.sin(fr)
    cosT = np.zeros((128, TSEQ), np.float32)
    sinT = np.zeros((128, TSEQ), np.float32)
    for r in range(128):
        d = r % 64
        jj = d if d < 32 else d - 32
        cosT[r] = cosv[:, jj]
        sinT[r] = (-sinv[:, jj]) if d < 32 else sinv[:, jj]
    cosT = np.tile(cosT, (1, 2)).astype(bf16)
    sinT = np.tile(sinT, (1, 2)).astype(bf16)
    psw = np.zeros((128, 128), np.float32)
    for i in range(128):
        d = i % 64
        k = i + 32 if d < 32 else i - 32
        psw[k, i] = 1.0
    kk = np.arange(128)[:, None]
    qq = np.arange(128)[None, :]
    mneg = (kk > qq).astype(np.float32)
    mpos = (kk <= qq).astype(np.float32)
    one = np.ones((128, 128), np.float32)
    c = dict(
        d_cosT=cosT, d_sinT=sinT, d_pswap=psw.astype(bf16),
        d_ident=np.eye(128, dtype=np.float32).astype(bf16),
        d_maskint=np.concatenate([mneg, one, mpos], 1).astype(bf16),
        d_maskfirst=np.concatenate([one, mpos], 1).astype(bf16),
        d_masklast=np.concatenate([mneg, one], 1).astype(bf16),
        d_ones1=np.ones((1, 128), np.float32),
        d_mones1=np.full((1, 128), -1.0, np.float32),
        d_ones1b=np.ones((1, 128), bf16),
        d_mones1b=np.full((1, 128), -1.0, np.float32).astype(bf16),
        d_onescol=np.ones((128, 1), np.float32),
        d_onescolb=np.ones((128, 1), bf16),
        d_identf=np.eye(128, dtype=np.float32),
    )
    _BUILT["c"] = c
    return c


def kernel(L0_attn_x, L0_attn_w_qkv, L0_attn_w_out, L0_attn_b_out,
           L0_ffn_w1, L0_ffn_conv_w, L0_ffn_conv_b, L0_ffn_w2):
    x = np.ascontiguousarray(np.asarray(L0_attn_x, np.float32))
    wqkv = np.asarray(L0_attn_w_qkv, np.float32)
    wout = np.asarray(L0_attn_w_out, np.float32)
    bout = np.asarray(L0_attn_b_out, np.float32)
    w1 = np.asarray(L0_ffn_w1, np.float32)
    cw = np.asarray(L0_ffn_conv_w, np.float32)
    cb = np.asarray(L0_ffn_conv_b, np.float32)
    w2 = np.asarray(L0_ffn_w2, np.float32)

    shared = dict(
        wqkvT=np.ascontiguousarray(wqkv.T),
        woutT=np.ascontiguousarray(wout.T),
        d_boutl=np.ascontiguousarray(bout.reshape(NCH, 128).T),
        w1T=np.ascontiguousarray(w1.T),
        w2T=np.ascontiguousarray(w2.T),
        d_convw=np.ascontiguousarray(cw[:, 0, :].reshape(HCH, 128, 3).transpose(1, 2, 0).reshape(128, 96)),
        d_convb=np.ascontiguousarray(cb.reshape(HCH, 128).T),
        **_consts(),
    )
    nc = _get_nc()
    in_maps = []
    for ci in range(8):
        xc = x[2 * ci:2 * ci + 2]
        xf = np.concatenate([xc[0].T, xc[1].T], axis=1)
        m = dict(shared)
        m["xf"] = np.ascontiguousarray(xf)
        in_maps.append(m)
    res = run_bass_kernel_spmd(nc, in_maps, core_ids=list(range(8)))
    out = np.empty((16, TSEQ, D), np.float32)
    for ci in range(8):
        of = res.results[ci]["outf"]
        out[2 * ci] = of[:, :TSEQ].T
        out[2 * ci + 1] = of[:, TSEQ:].T
    return out


# version 12 (final)
# speedup vs baseline: 1.2279x; 1.0352x over previous
import numpy as np
import ml_dtypes
import concourse.bass as bass
import concourse.mybir as mybir
import concourse.bass_utils as _bass_utils
from concourse.bass_utils import run_bass_kernel_spmd

_orig_run_command = _bass_utils.run_command


def _run_command_ldwopt(cmd, *a, **k):
    cmd = ["--enable-ldw-opt=true" if c == "--enable-ldw-opt=false" else c for c in cmd]
    return _orig_run_command(cmd, *a, **k)


_bass_utils.run_command = _run_command_ldwopt

F32 = mybir.dt.float32
BF16 = mybir.dt.bfloat16
I8 = mybir.dt.int8
F8 = mybir.dt.float8e4
ALU = mybir.AluOpType
ACT = mybir.ActivationFunctionType
AX = mybir.AxisListType
bf16 = ml_dtypes.bfloat16

D = 1024
H = 16
HD = 64
TSEQ = 2048
T = 4096
DH = 4096
NCH = 8
HCH = 32
TT = 256
NTILE = 16
TPS = 8
TTH = TT + 2
EPS = 1e-5
RMAGIC = 12582912.0


class Ctr:
    def __init__(self, nc, sems):
        self.c = {}
        self.lastw = {}
        self.eng = {"sync": nc.sync, "gpsimd": nc.gpsimd, "tensor": nc.tensor,
                    "vector": nc.vector, "scalar": nc.scalar}
        self.sems = sems

    def inc(self, inst, sem, n=1):
        inst.then_inc(self.sems[sem], n)
        self.c[sem] = self.c.get(sem, 0) + n
        return self.c[sem]

    def wait(self, ename, sem, val):
        if val is None or val <= 0:
            return
        k = (ename, sem)
        if self.lastw.get(k, 0) >= val:
            return
        self.lastw[k] = val
        self.eng[ename].wait_ge(self.sems[sem], val)


def _ap_span(ap):
    try:
        t = ap.tensor
        name = t.name
        esz = mybir.dt.size(ap.dtype)
        pairs = list(ap.ap)
        p0 = int(getattr(t, "base_partition", 0) or 0)
        np_ = int(pairs[0][1]) if pairs else 128
        off = int(ap.offset)
        ext = 0
        for (st, ct) in pairs[1:]:
            ext += abs(int(st)) * (int(ct) - 1)
        lo = off * esz
        hi = (off + ext + 1) * esz
        return (name, p0, p0 + np_, lo, hi)
    except Exception:
        return ("?", 0, 128, 0, 1 << 40)


def _overlap(a, b):
    return (a[0] == b[0] and a[1] < b[2] and b[1] < a[2]
            and a[3] < b[4] and b[3] < a[4])


class DrainWrap:
    _OPS = ("tensor_tensor", "tensor_scalar", "tensor_add", "tensor_sub", "tensor_mul",
            "tensor_max", "tensor_copy", "tensor_single_scalar", "tensor_scalar_add",
            "tensor_scalar_sub", "tensor_scalar_mul", "tensor_scalar_max", "tensor_scalar_min",
            "tensor_reduce", "reduce_max", "reduce_sum", "scalar_tensor_tensor",
            "reciprocal", "memset", "memzero", "activation", "copy", "add", "mul",
            "square", "sqrt", "sign", "select", "transpose", "bn_stats", "bn_aggr")

    def __init__(self, eng):
        self._e = eng
        self._w = []
        self._r = []

    def __getattr__(self, attr):
        f = getattr(self._e, attr)
        if attr in DrainWrap._OPS:
            def wrapped(*a, _f=f, **k):
                aps = [x for x in list(a) + list(k.values())
                       if hasattr(x, "tensor") and hasattr(x, "ap")]
                if aps:
                    wr = [_ap_span(aps[0])]
                    ao = k.get("accum_out")
                    if ao is not None and hasattr(ao, "tensor"):
                        wr.append(_ap_span(ao))
                    rd = [_ap_span(x) for x in aps[1:]]
                    haz = (any(_overlap(x, y) for x in rd + wr for y in self._w)
                           or any(_overlap(x, y) for x in wr for y in self._r))
                    if haz:
                        self._e.drain()
                        self._w = []
                        self._r = []
                    self._w.extend(wr)
                    self._r.extend(rd)
                return _f(*a, **k)
            return wrapped
        return f


class Res:
    def __init__(self, name=""):
        self.name = name
        self.ready = []
        self.users = []

    def acquire(self, C, ename):
        if isinstance(ename, str):
            ename = (ename,)
        for e in ename:
            for (s, v) in self.users:
                C.wait(e, s, v)
        self.users = []
        self.ready = []

    def use(self, C, ename):
        for (s, v) in self.ready:
            C.wait(ename, s, v)


def build_program(nc, do_c=True, nb=NTILE, do_b=True, bstop="full"):
    DI = lambda n, sh, dt=F32: nc.dram_tensor(n, sh, dt, kind="ExternalInput").ap()
    d_xf = DI("xf", [D, T])
    d_wqkvT = DI("wqkvT", [D, 3 * D])
    d_woutT = DI("woutT", [D, D])
    d_boutl = DI("d_boutl", [128, NCH])
    d_w1T = DI("w1T", [D, DH])
    d_w2T = DI("w2T", [DH, D])
    d_convw = DI("d_convw", [128, 3 * HCH])
    d_convb = DI("d_convb", [128, HCH])
    d_cos = DI("d_cosT", [128, T], BF16)
    d_sin = DI("d_sinT", [128, T], BF16)
    d_pswap = DI("d_pswap", [128, 128], BF16)
    d_ident = DI("d_ident", [128, 128], BF16)
    d_maskint = DI("d_maskint", [128, 384], BF16)
    d_maskfirst = DI("d_maskfirst", [128, 256], BF16)
    d_masklast = DI("d_masklast", [128, 256], BF16)
    d_ones1 = DI("d_ones1", [1, 128])
    d_mones1 = DI("d_mones1", [1, 128])
    d_ones1b = DI("d_ones1b", [1, 128], BF16)
    d_mones1b = DI("d_mones1b", [1, 128], BF16)
    d_onescol = DI("d_onescol", [128, 1])
    d_onescolb = DI("d_onescolb", [128, 1], BF16)
    d_identf = DI("d_identf", [128, 128])
    d_out = nc.dram_tensor("outf", [D, T], F32, kind="ExternalOutput").ap()

    import contextlib
    with contextlib.ExitStack() as stk:
        SB = lambda n, sh, dt: stk.enter_context(nc.sbuf_tensor(n, sh, dt))
        PS = lambda n, sh, dt: stk.enter_context(nc.psum_tensor(n, sh, dt))

        xtile = [SB(f"xtile{i}", [128, NCH * TTH], F32) for i in range(2)]
        ou1x = SB("outsb1x", [128, NCH * TTH], F32)
        outsb = [SB("outsb0", [128, NCH * TT], F32), ou1x[:, 0:NCH * TT]]
        mr_rows = SB("mr_rows", [1, T], BF16)
        rstd_rows = SB("rstd_rows", [1, T], BF16)
        ones1b = SB("ones1b", [1, 128], BF16)
        mones1b = SB("mones1b", [1, 128], BF16)
        ones1 = SB("ones1", [1, 128], F32)
        mones1 = SB("mones1", [1, 128], F32)
        onescol = SB("onescol", [128, 1], F32)
        onescolb = SB("onescolb", [128, 1], BF16)
        identf = SB("identf", [128, 128], F32)
        boutl = SB("boutl", [128, NCH], F32)
        convw = SB("convw", [128, 3 * HCH], F32)
        convb = SB("convb", [128, HCH], F32)
        m127 = SB("m127", [1, 2], F32)
        mw2col = SB("mw2col", [128, 1], F32)
        scw1col = SB("scw1col", [128, 1], F32)
        scw2col = SB("scw2col", [128, 1], F32)
        ident = SB("ident", [128, 128], BF16)
        pswap = SB("pswap", [128, 128], BF16)
        maskint = SB("maskint", [128, 384], BF16)
        maskfirst = maskint[:, 128:384]
        masklast = maskint[:, 0:256]
        s2f2 = SB("s2f2", [128, 2], F32)
        fmx32 = SB("fmx32", [128, TTH], F32)
        fmn32 = SB("fmn32", [128, TTH], F32)
        gsc = SB("gsc", [128, 2 * TT], BF16)
        reccol2 = SB("reccol2", [128, 1], F32)
        f2row = SB("f2row", [1, 128], F32)
        s2row = SB("s2row", [1, 128], F32)
        sxrow = SB("sxrow", [1, TTH], F32)
        reccol = SB("reccol", [128, 1], F32)
        rowbuf = SB("rowbuf", [1, 8 * TTH], F32)
        s2b = SB("s2b", [128, TT], F32)
        f2b = SB("f2b", [128, TT], BF16)

        WA = SB("warena", [128, 81920], I8)
        w1s = WA[:, 0:NCH * DH].bitcast(F8)
        w2s = WA[:, NCH * DH:NCH * DH + HCH * D].bitcast(F8)
        wqkv = WA[:, 0:2 * NCH * 3 * D].bitcast(BF16)
        woutw = WA[:, 49152:49152 + 2 * NCH * D].bitcast(BF16)
        cosT = WA[:, 65536:65536 + 2 * T].bitcast(BF16)
        sinT = WA[:, 73728:73728 + 2 * T].bitcast(BF16)
        xs2 = WA[:, 65536:65536 + NCH * TTH * 2].bitcast(BF16)
        xtile3 = WA[:, 65536 + NCH * TTH * 2:65536 + NCH * TTH * 2 + NCH * TTH * 4].bitcast(F32)

        ARENA_SZ = 61824
        AR = SB("arena", [128, ARENA_SZ], I8)

        def carve(off, n, dt):
            size = {F32: 4, BF16: 2, I8: 1, F8: 1}[dt]
            off = (off + 3) // 4 * 4
            ap = AR[:, off:off + n * size]
            return (ap.bitcast(dt) if dt != I8 else ap), off + n * size

        o = 0
        xs, o = carve(o, NCH * TTH, BF16)
        xc_ctr, o = carve(o, NCH * TTH, F32)
        h1_l = []
        for i in range(2):
            a, o = carve(o, HCH * TTH, BF16)
            h1_l.append(a)
        h2, o = carve(o, HCH * TT, BF16)
        assert o <= ARENA_SZ, o
        wtmp = [AR[:, 0:16384].bitcast(F32), AR[:, 16384:32768].bitcast(F32)]
        wq8 = AR[:, 32768:36864]
        wabs_l = [AR[:, 36864:45056].bitcast(BF16), AR[:, 45056 - 8192 + 8192:53248].bitcast(BF16)]
        o = 0
        qsb_l = []
        for i in range(2):
            a, o = carve(o, NCH * TT, BF16)
            qsb_l.append(a)
        kblk_l = []
        for i in range(4):
            a, o = carve(o, NCH * 128, BF16)
            kblk_l.append(a)
        vblk_l = []
        for i in range(4):
            a, o = carve(o, H * (HD + 1), BF16)
            vblk_l.append(a)
        attnT_l = []
        for i in range(2):
            a, o = carve(o, NCH * TT, BF16)
            attnT_l.append(a)
        qraw_l = []
        for i in range(2):
            a, o = carve(o, TT, BF16)
            qraw_l.append(a)
        tr1, o = carve(o, TT, BF16)
        tr2, o = carve(o, TT, BF16)
        xnsb, o = carve(o, NCH * TT, BF16)
        expsb_l = []
        for i in range(2):
            a, o = carve(o, 384, BF16)
            expsb_l.append(a)
        atok_l = []
        for i in range(2):
            a, o = carve(o, H * HD, BF16)
            atok_l.append(a)
        assert o <= ARENA_SZ, o

        pb = [PS(f"pb{i}", [128, 512], F32) for i in range(8)]
        pbT = [pb[i][:, :].bitcast(BF16) for i in range(8)]
        rpb = [Res(f"pb{i}") for i in range(8)]

        sems = {}
        for s in ("L", "LX0", "LX1", "LX2", "LX3", "LW0", "LW1", "T", "V", "S", "G", "ST0", "ST1"):
            sems[s] = stk.enter_context(nc.semaphore(s))

        with nc.Block() as _block:
            C = Ctr(nc, sems)
            Tn, Sy = nc.tensor, nc.sync
            G = DrainWrap(nc.gpsimd)
            V = DrainWrap(nc.vector)
            S = DrainWrap(nc.scalar)

            r_xtile = [Res("xt0"), Res("xt1")]
            r_wabs = [Res("wa0"), Res("wa1")]
            r_outsb = [Res("ou0"), Res("ou1")]
            r_wtmp = [Res("wt0"), Res("wt1")]
            r_qsb = [Res("q0"), Res("q1")]
            r_kblk = [Res(f"k{i}") for i in range(4)]
            r_vblk = [Res(f"v{i}") for i in range(4)]
            r_attnT = [Res("aT0"), Res("aT1")]
            r_qraw = [Res("qr0"), Res("qr1")]
            r_atok = [Res("atok0"), Res("atok1")]
            r_expsb = [Res("expsb0"), Res("expsb1")]
            r_xs = Res("xs")
            r_xc = Res("xc")
            r_s2b = Res("s2b")
            r_gsc = Res("gsc")
            r_h1b = [Res("h1a"), Res("h1b")]
            r_h2 = Res("h2")
            r_fold = Res("fold")
            r_xnsb = Res("xnsb")

            def dma_in(out_ap, in_ap, sem="L"):
                return C.inc(Sy.dma_start(out=out_ap, in_=in_ap), sem, 16)

            for (dst, src) in [(ones1, d_ones1), (mones1, d_mones1), (onescol, d_onescol),
                               (boutl, d_boutl), (convw, d_convw), (convb, d_convb),
                               (ident, d_ident), (identf, d_identf), (onescolb, d_onescolb),
                               (ones1b, d_ones1b), (mones1b, d_mones1b),
                               (pswap, d_pswap), (maskint, d_maskint)]:
                l_consts = dma_in(dst[:, :], src[:, :])

            C.wait("tensor", "L", l_consts)
            first = True
            tsum = 0
            for wi, (d_w, nch) in enumerate(((d_w1T, NCH), (d_w2T, HCH))):
                wd = d_w.shape[1]
                for i in range(nch):
                    bi = i % 2
                    r_wtmp[bi].acquire(C, "sync")
                    lv = dma_in(wtmp[bi][:, 0:wd], d_w[i * 128:(i + 1) * 128, :], f"LW{bi}")
                    C.wait("scalar", f"LW{bi}", lv)
                    r_wabs[bi].acquire(C, "scalar")
                    sv = C.inc(S.activation(wabs_l[bi][:, 0:wd], wtmp[bi][:, 0:wd], ACT.Abs), "S")
                    r_wtmp[bi].users.append(("S", sv))
                    C.wait("tensor", "S", sv)
                    for ss in range(wd // 256):
                        last = (i == nch - 1) and (ss == wd // 256 - 1)
                        tsum = C.inc(Tn.matmul(pb[6][0:1, wi * 256:wi * 256 + 256],
                                               onescolb[:, :], wabs_l[bi][:, ss * 256:(ss + 1) * 256],
                                               start=first, stop=last), "T")
                        first = False
                    r_wabs[bi].users.append(("T", tsum))
                first = True
            C.wait("vector", "T", tsum)
            V.reduce_sum(rowbuf[0:1, 0:1], pb[6][0:1, 0:256], AX.X)
            V.reduce_sum(rowbuf[0:1, 1:2], pb[6][0:1, 256:512], AX.X)
            V.tensor_scalar(rowbuf[0:1, 2:4], rowbuf[0:1, 0:2], 1.0 / (D * DH), 1e-5,
                            op0=ALU.mult, op1=ALU.max)
            V.tensor_scalar(rowbuf[0:1, 4:6], rowbuf[0:1, 2:4], 1.0 / 127.0, None, op0=ALU.mult)
            vv = C.inc(V.tensor_copy(m127[:, :], rowbuf[0:1, 4:6]), "V")
            C.wait("tensor", "V", vv)
            tv = C.inc(Tn.matmul(pb[7][:, 0:4], ones1[:, :], rowbuf[0:1, 2:6], start=True, stop=True), "T")
            C.wait("vector", "T", tv)
            V.reciprocal(scw1col[:, :], pb[7][:, 0:1])
            V.reciprocal(scw2col[:, :], pb[7][:, 1:2])
            v_scales = C.inc(V.tensor_copy(mw2col[:, :], pb[7][:, 3:4]), "V")
            rpb[6].users.append(("V", v_scales))
            rpb[7].users.append(("V", v_scales))

            vq = 0
            for (d_w, nch, dst, sc) in ((d_w1T, NCH, w1s, scw1col), (d_w2T, HCH, w2s, scw2col)):
                wd = d_w.shape[1]
                for i in range(nch):
                    bi = i % 2
                    r_wtmp[bi].acquire(C, "sync")
                    lv = dma_in(wtmp[bi][:, 0:wd], d_w[i * 128:(i + 1) * 128, :], f"LW{bi}")
                    C.wait("vector", f"LW{bi}", lv)
                    C.wait("vector", "V", v_scales)
                    V.tensor_scalar(wtmp[bi][:, 0:wd], wtmp[bi][:, 0:wd], sc[:, :], 1.49,
                                    op0=ALU.mult, op1=ALU.min)
                    V.tensor_scalar(wtmp[bi][:, 0:wd], wtmp[bi][:, 0:wd], -1.49, RMAGIC,
                                    op0=ALU.max, op1=ALU.add)
                    vq = C.inc(V.tensor_scalar(dst[:, i * wd:(i + 1) * wd], wtmp[bi][:, 0:wd],
                                               -RMAGIC, None, op0=ALU.add), "V")
                    r_wtmp[bi].users.append(("V", vq))
            v_signs = vq

            x3 = [xtile[i][:, :].rearrange("p (c w) -> p c w", c=NCH) for i in range(2)]
            xc3 = xc_ctr.rearrange("p (c w) -> p c w", c=NCH)
            xs3 = xs.rearrange("p (c w) -> p c w", c=NCH)
            RS = lambda k: rowbuf[0:1, k * TTH:(k + 1) * TTH]

            v_rows = 0
            xtile_b = [xtile[0][:, :], xtile[1][:, :], xtile3, ou1x[:, :]]
            r_xtb = [r_xtile[0], r_xtile[1], Res("xt3"), Res("xt4")]
            x3b = [a.rearrange("p (c w) -> p c w", c=NCH) for a in xtile_b]
            xs_l = [xs, xs2]
            xsl3 = [a.rearrange("p (c w) -> p c w", c=NCH) for a in xs_l]
            r_xsl = [r_xs, Res("xs2")]
            for i in range(4):
                vz0 = C.inc(V.memset(xtile_b[i], 0.0), "V")
                r_xtb[i].users.append(("V", vz0))
            bst = {}

            def b_p1(it):
                t0 = it * TT
                s0 = (it // TPS) * TSEQ
                sL = (t0 - 1) >= s0
                sR = (t0 + TT + 1) <= s0 + TSEQ
                g0 = t0 - 1 if sL else t0
                g1 = t0 + TT + 1 if sR else t0 + TT
                W = g1 - g0
                a0 = 0 if sL else 1
                xb = xtile_b[it % 4]
                xc = x3b[it % 4]
                rx = r_xtb[it % 4]
                xs_ = xs_l[it % 2]
                xs3_ = xsl3[it % 2]
                rxs_ = r_xsl[it % 2]

                rx.acquire(C, "sync")
                lxs = f"LX{it % 4}"
                lx = dma_in(
                    xb.rearrange("p (c w) -> p c w", c=NCH)[:, :, a0:a0 + W],
                    d_xf[:, g0:g1].rearrange("(c p) w -> p c w", p=128), lxs)

                rpb[6].acquire(C, "tensor")
                rpb[7].acquire(C, "tensor")
                C.wait("tensor", lxs, lx)
                for c in range(NCH):
                    C.inc(Tn.matmul(pb[6][0:1, 0:TTH], onescol[:, :], xc[:, c, 0:TTH],
                                    start=(c == 0), stop=(c == NCH - 1)), "T")
                C.wait("scalar", lxs, lx)
                r_xc.acquire(C, "scalar")
                tq_slot = [0, 0]
                for c in range(NCH):
                    C.wait("scalar", "T", tq_slot[c % 2])
                    sv = C.inc(S.activation(xc3[:, c % 2, 0:TTH],
                                            xc[:, c, 0:TTH], ACT.Square), "S")
                    C.wait("tensor", "S", sv)
                    tq = C.inc(Tn.matmul(pb[7][0:1, 0:TTH], onescol[:, :],
                                         xc3[:, c % 2, 0:TTH],
                                         start=(c == 0), stop=(c == NCH - 1)), "T")
                    tq_slot[c % 2] = tq

                C.wait("scalar", "T", tq)
                sv_mean = C.inc(S.activation(RS(0), pb[6][0:1, 0:TTH], ACT.Copy, scale=1.0 / D), "S")
                rpb[6].users.append(("S", sv_mean))
                sv = C.inc(S.activation(RS(1), pb[7][0:1, 0:TTH], ACT.Copy, scale=1.0 / D), "S")
                rpb[7].users.append(("S", sv))
                sv2 = C.inc(S.activation(RS(2), RS(0), ACT.Square), "S")
                C.wait("vector", "S", sv2)
                vv = C.inc(V.tensor_sub(RS(1), RS(1), RS(2)), "V")
                vv = C.inc(V.tensor_scalar(RS(1), RS(1), EPS, None, op0=ALU.add), "V")
                C.wait("scalar", "V", vv)
                sv = C.inc(S.activation(RS(2), RS(1), ACT.Ln), "S")
                sv_rstd = C.inc(S.activation(RS(3), RS(2), ACT.Exp, scale=-0.5), "S")
                C.wait("vector", "S", sv_rstd)
                v_mr = C.inc(V.tensor_mul(RS(2), RS(0), RS(3)), "V")

                rpb[6].acquire(C, "tensor")
                C.wait("tensor", "S", sv_mean)
                t_meanB = C.inc(Tn.matmul(pb[6][:, 0:TTH], ones1[:, :], RS(0), start=True, stop=True), "T")
                C.wait("vector", "T", t_meanB)
                for c in range(NCH):
                    C.inc(V.tensor_tensor(xc3[:, c, 0:TTH], xc[:, c, 0:TTH], pb[6][:, 0:TTH],
                                          op=ALU.subtract), "V")
                    if c == 0:
                        V.tensor_copy(fmx32[:, 0:TTH], xc3[:, c, 0:TTH])
                        vv = C.inc(V.tensor_copy(fmn32[:, 0:TTH], xc3[:, c, 0:TTH]), "V")
                    else:
                        V.tensor_tensor(fmx32[:, 0:TTH], fmx32[:, 0:TTH], xc3[:, c, 0:TTH],
                                        op=ALU.max)
                        vv = C.inc(V.tensor_tensor(fmn32[:, 0:TTH], fmn32[:, 0:TTH], xc3[:, c, 0:TTH],
                                                   op=ALU.min), "V")
                V.tensor_scalar(fmn32[:, 0:TTH], fmn32[:, 0:TTH], -1.0, None, op0=ALU.mult)
                v_fold = C.inc(V.tensor_tensor(fmx32[:, 0:TTH], fmx32[:, 0:TTH], fmn32[:, 0:TTH],
                                               op=ALU.max), "V")
                rpb[6].users.append(("V", v_fold))

                rpb[7].acquire(C, "tensor")
                C.wait("tensor", "V", v_fold)
                cuts = [0, 128, TTH - 128]
                ttr = [C.inc(Tn.transpose(pb[7][:, j * 128:(j + 1) * 128], fmx32[:, cu:cu + 128],
                                          identf[:, :]), "T") for j, cu in enumerate(cuts)]
                for j, cu in enumerate(cuts):
                    C.wait("vector", "T", ttr[-1])
                    vv = C.inc(V.reduce_max(reccol[:, :], pb[7][:, j * 128:(j + 1) * 128], AX.X), "V")
                    C.wait("tensor", "V", vv)
                    tv = C.inc(Tn.matmul(pb[6][0:1, 300:428], reccol[:, :], identf[:, :],
                                         start=True, stop=True), "T")
                    C.wait("vector", "T", tv)
                    vv = C.inc(V.tensor_copy(sxrow[0:1, cu:cu + 128], pb[6][0:1, 300:428]), "V")
                rpb[6].users.append(("V", vv))
                rpb[7].users.append(("V", vv))

                vv = C.inc(V.tensor_mul(RS(4), sxrow[0:1, 0:TTH], RS(3)), "V")
                vv = C.inc(V.tensor_scalar(RS(4), RS(4), 1e-5, None, op0=ALU.max), "V")
                vv = C.inc(V.reciprocal(RS(5), RS(4)), "V")
                vv = C.inc(V.tensor_scalar(RS(5), RS(5), 127.0, None, op0=ALU.mult), "V")
                vv_a = C.inc(V.tensor_mul(RS(6), RS(3), RS(5)), "V")
                vv_f1 = C.inc(V.tensor_scalar(RS(7), RS(4), m127[0:1, 0:1], None, op0=ALU.mult), "V")
                C.inc(V.tensor_copy(rstd_rows[0:1, t0:t0 + TT], RS(3)[0:1, 1:257]), "V")
                vr = C.inc(V.tensor_copy(mr_rows[0:1, t0:t0 + TT], RS(2)[0:1, 1:257]), "V")

                rpb[7].acquire(C, "tensor")
                C.wait("tensor", "V", vv_a)
                t_aB = C.inc(Tn.matmul(pb[7][:, 0:TTH], ones1[:, :], RS(6), start=True, stop=True), "T")
                rpb[6].acquire(C, "tensor")
                C.wait("tensor", "V", vv_f1)
                t_f1B = C.inc(Tn.matmul(pb[6][:, 0:TTH], ones1[:, :], RS(7), start=True, stop=True), "T")
                C.wait("vector", "T", t_aB)
                rxs_.acquire(C, "vector")
                V.tensor_tensor(
                    xc3[:, :, :], xc3[:, :, :],
                    pb[7][:, 0:TTH].unsqueeze(1).broadcast_to((128, NCH, TTH)), op=ALU.mult)
                vv = C.inc(V.tensor_scalar(xs3_[:, :, :], xc3[:, :, :], RMAGIC, -RMAGIC,
                                           op0=ALU.add, op1=ALU.add), "V")
                rpb[7].users.append(("V", vv))
                r_xc.users.append(("V", vv))
                C.wait("vector", "T", t_f1B)
                v_xs = C.inc(V.tensor_tensor(
                    xs3_[:, :, :], xs3_[:, :, :],
                    pb[6][:, 0:TTH].unsqueeze(1).broadcast_to((128, NCH, TTH)), op=ALU.mult), "V")
                rpb[6].users.append(("V", v_xs))
                if not sL:
                    v_xs = C.inc(V.memset(xs3_[:, :, 0:1], 0.0), "V")
                if not sR:
                    v_xs = C.inc(V.memset(xs3_[:, :, TTH - 1:TTH], 0.0), "V")
                bst[it] = {"v_xs": v_xs, "v_rows": vr}

            def b_p2(it):
                st_ = bst[it]
                xs_ = xs_l[it % 2]
                rxs_ = r_xsl[it % 2]
                C.wait("tensor", "V", st_["v_xs"])
                C.wait("tensor", "V", v_signs)
                h1b = h1_l[it % 2]
                rh = r_h1b[it % 2]
                rh.acquire(C, "scalar")
                h1ev = []
                for cc in range(HCH):
                    bk = 2 + cc % 4
                    rpb[bk].acquire(C, "tensor")
                    for dc in range(NCH):
                        tm = C.inc(Tn.matmul(pb[bk][:, 0:TTH],
                                             w1s[:, dc * DH + cc * 128: dc * DH + cc * 128 + 128],
                                             xs_[:, dc * TTH:(dc + 1) * TTH],
                                             start=(dc == 0), stop=(dc == NCH - 1)), "T")
                    C.wait("scalar", "T", tm)
                    se = C.inc(S.copy(h1b[:, cc * TTH:(cc + 1) * TTH], pb[bk][:, 0:TTH]), "S")
                    rpb[bk].users.append(("S", se))
                    h1ev.append(("S", se))
                rxs_.users.append(("T", tm))
                st_["h1ev"] = h1ev

            def b_p3a(it):
                st_ = bst[it]
                h1ev = st_["h1ev"]
                h1b = h1_l[it % 2]
                rh = r_h1b[it % 2]
                r_h2.acquire(C, ("vector", "scalar", "gpsimd"))
                r_gsc.acquire(C, "gpsimd")
                sI = {}
                vA = {}
                gB = {}
                silu_done = [None] * HCH
                for it2 in range(HCH + 7):
                    cc = it2
                    if cc < HCH:
                        C.wait("scalar", h1ev[cc][0], h1ev[cc][1])
                        z = h2[:, cc * TT:(cc + 1) * TT]
                        sI[cc] = C.inc(S.activation(z, h1b[:, cc * TTH + 0: cc * TTH + TT],
                                                    ACT.Identity, scale=convw[:, cc:cc + 1],
                                                    bias=convb[:, cc:cc + 1]), "S")
                    c1 = it2 - 4
                    if 0 <= c1 < HCH:
                        C.wait("vector", "S", sI[c1])
                        C.wait("vector", "G", gB[c1])
                        C.wait("vector", h1ev[c1][0], h1ev[c1][1])
                        z = h2[:, c1 * TT:(c1 + 1) * TT]
                        V.scalar_tensor_tensor(z, h1b[:, c1 * TTH + 1: c1 * TTH + 1 + TT],
                                               convw[:, 32 + c1:32 + c1 + 1], z,
                                               op0=ALU.mult, op1=ALU.add)
                        g = gsc[:, (c1 % 2) * TT:(c1 % 2) * TT + TT]
                        vA[c1] = C.inc(V.tensor_tensor(z, z, g, op=ALU.add), "V")
                    c2 = it2 - 2
                    if 0 <= c2 < HCH:
                        g = gsc[:, (c2 % 2) * TT:(c2 % 2) * TT + TT]
                        C.wait("gpsimd", h1ev[c2][0], h1ev[c2][1])
                        if c2 - 2 in vA:
                            C.wait("gpsimd", "V", vA[c2 - 2])
                        gB[c2] = C.inc(G.tensor_tensor(
                            g, h1b[:, c2 * TTH + 2: c2 * TTH + 2 + TT],
                            convw[:, 64 + c2:64 + c2 + 1].broadcast_to((128, TT)),
                            op=ALU.mult), "G")
                    c3 = it2 - 7
                    if 0 <= c3 < HCH and c3 % 4 == 3:
                        C.wait("scalar", "V", vA[c3])
                        zb = h2[:, (c3 - 3) * TT:(c3 + 1) * TT]
                        sz = C.inc(S.activation(zb, zb, ACT.Silu), "S")
                        for c4 in range(c3 - 3, c3 + 1):
                            silu_done[c4] = sz
                rh.users.append(("V", vA[HCH - 1]))
                rh.users.append(("S", sI[HCH - 1]))
                r_gsc.users.append(("V", vA[HCH - 2]))
                r_gsc.users.append(("V", vA[HCH - 1]))

                fx3 = h1b[:, 0:16 * TT].rearrange("p (c t) -> p c t", c=16)
                fn3 = h1b[:, 16 * TT:32 * TT].rearrange("p (c t) -> p c t", c=16)
                h23v = h2.rearrange("p (c t) -> p c t", c=HCH)
                C.wait("vector", "S", silu_done[HCH - 1])
                V.tensor_tensor(fx3[:, :, :], h23v[:, 0:16, :], h23v[:, 16:32, :], op=ALU.max)
                V.tensor_tensor(fn3[:, :, :], h23v[:, 0:16, :], h23v[:, 16:32, :], op=ALU.min)
                for lv in (8, 4, 2, 1):
                    V.tensor_tensor(fx3[:, 0:lv, :], fx3[:, 0:lv, :], fx3[:, lv:2 * lv, :], op=ALU.max)
                    V.tensor_tensor(fn3[:, 0:lv, :], fn3[:, 0:lv, :], fn3[:, lv:2 * lv, :], op=ALU.min)
                V.tensor_scalar(fn3[:, 0, :], fn3[:, 0, :], -1.0, None, op0=ALU.mult)
                vf = C.inc(V.tensor_tensor(fx3[:, 0, :], fx3[:, 0, :], fn3[:, 0, :],
                                           op=ALU.max), "V")
                st_["vf"] = vf

            def b_p3b(it):
                st_ = bst[it]
                h1b = h1_l[it % 2]
                rh = r_h1b[it % 2]
                vf = st_["vf"]
                rpb[0].acquire(C, "tensor")
                C.wait("tensor", "V", vf)
                tt1 = C.inc(Tn.transpose(pbT[0][:, 0:128], h1b[:, 0:128], ident[:, :]), "T")
                tt2 = C.inc(Tn.transpose(pbT[0][:, 128:256], h1b[:, 128:256], ident[:, :]), "T")
                rh.users.append(("T", tt2))
                rh.users.append(("V", vf))
                svals = []
                for j in range(2):
                    C.wait("vector", "T", tt2)
                    vv = C.inc(V.reduce_max(reccol2[:, :], pbT[0][:, j * 128:(j + 1) * 128], AX.X), "V")
                    vv = C.inc(V.tensor_scalar(s2f2[:, 0:1], reccol2[:, :], 1e-5, None, op0=ALU.max), "V")
                    vv = C.inc(V.reciprocal(s2f2[:, 1:2], s2f2[:, 0:1]), "V")
                    V.tensor_scalar(s2f2[:, 1:2], s2f2[:, 1:2], 127.0, None, op0=ALU.mult)
                    vv = C.inc(V.tensor_scalar(s2f2[:, 0:1], s2f2[:, 0:1], mw2col[:, :], None,
                                               op0=ALU.mult), "V")
                    C.wait("tensor", "V", vv)
                    tv = C.inc(Tn.matmul(pb[0][0:1, 300:428], s2f2[:, 0:1], identf[:, :],
                                         start=True, stop=True), "T")
                    tv = C.inc(Tn.matmul(pb[0][0:1, 172:300], s2f2[:, 1:2], identf[:, :],
                                         start=True, stop=True), "T")
                    C.wait("vector", "T", tv)
                    vv = C.inc(V.tensor_copy(f2row[0:1, 0:128], pb[0][0:1, 300:428]), "V")
                    vv = C.inc(V.tensor_copy(s2row[0:1, 0:128], pb[0][0:1, 172:300]), "V")
                    C.wait("tensor", "V", vv)
                    if j == 0:
                        rpb[1].acquire(C, "tensor")
                    else:
                        C.wait("tensor", "S", svals[-1])
                    tb = C.inc(Tn.matmul(pb[1][:, j * 128:(j + 1) * 128],
                                         ones1[:, :], s2row[0:1, 0:128], start=True, stop=True), "T")
                    tb2 = C.inc(Tn.matmul(pb[1][:, 256 + j * 128:256 + (j + 1) * 128],
                                          ones1[:, :], f2row[0:1, 0:128], start=True, stop=True), "T")
                    C.wait("scalar", "T", tb2)
                    if j == 0:
                        r_s2b.acquire(C, "scalar")
                    svals.append(C.inc(S.copy(s2b[:, j * 128:(j + 1) * 128],
                                              pb[1][:, j * 128:(j + 1) * 128]), "S"))
                    svals.append(C.inc(S.copy(f2b[:, j * 128:(j + 1) * 128],
                                              pb[1][:, 256 + j * 128:256 + (j + 1) * 128]), "S"))
                rpb[0].users.append(("V", vv))
                rpb[1].users.append(("S", svals[-1]))

                C.wait("vector", "S", svals[-1])
                h23 = h2.rearrange("p (c t) -> p c t", c=HCH)
                V.tensor_tensor(h23[:, :, :], h23[:, :, :],
                                s2b[:, :].unsqueeze(1).broadcast_to((128, HCH, TT)),
                                op=ALU.mult)
                v_hq = C.inc(V.tensor_scalar(h23[:, :, :], h23[:, :, :], RMAGIC, -RMAGIC,
                                             op0=ALU.add, op1=ALU.add), "V")
                st_["v_hq"] = v_hq
                r_h2.users.append(("V", v_hq))
                rh.users.append(("T", None))
                rh.users.pop()

            def b_p45(it):
                st_ = bst.pop(it)
                t0 = it * TT
                xc = x3b[it % 4]
                rx = r_xtb[it % 4]
                C.wait("tensor", "V", st_["v_hq"])
                ob = outsb[0]
                ro = r_outsb[0]
                ro.acquire(C, "vector")
                for oc in (0, 2, 4, 6, 1, 3, 5, 7):
                    bk = 2 + oc // 2
                    half = (oc % 2) * 256
                    rpb[bk].acquire(C, "tensor")
                    for cc in range(HCH):
                        tm = C.inc(Tn.matmul(pb[bk][:, half:half + TT],
                                             w2s[:, cc * D + oc * 128: cc * D + oc * 128 + 128],
                                             h2[:, cc * TT:(cc + 1) * TT],
                                             start=(cc == 0), stop=(cc == HCH - 1)), "T")
                    C.wait("vector", "T", tm)
                    dst = ob[:, oc * TT:(oc + 1) * TT]
                    V.tensor_tensor(dst, pb[bk][:, half:half + TT], f2b[:, :], op=ALU.mult)
                    ve = C.inc(V.scalar_tensor_tensor(dst, xc[:, oc, 1:257], 1.0, dst,
                                                      op0=ALU.mult, op1=ALU.add), "V")
                    rpb[bk].users.append(("V", ve))
                    C.wait("scalar", "V", ve)
                    ve2 = C.inc(S.activation(dst, dst, ACT.Identity, scale=1.0,
                                             bias=boutl[:, oc:oc + 1]), "S")
                    b_p45.sevs.append(("S", ve2))
                rx.users.append(("V", ve))
                r_s2b.users.append(("V", ve))
                r_h2.users.append(("T", tm))
                for u in b_p45.sevs:
                    C.wait("gpsimd", u[0], u[1])
                b_p45.sevs = []
                stv = C.inc(G.dma_start(out=d_out[:, t0:t0 + TT].rearrange("(c p) w -> p c w", p=128),
                                        in_=ob[:, :].rearrange("p (c w) -> p c w", c=NCH)), f"ST{it % 2}", 16)
                ro.users.append((f"ST{it % 2}", stv))
                return vr if False else st_["v_rows"]

            b_p45.sevs = []
            nb_ = nb if do_b else 0
            if nb_ > 0:
                b_p1(0)
                b_p2(0)
                b_p1(1)
                for it in range(nb_):
                    if it + 2 < nb_:
                        b_p1(it + 2)
                    b_p3a(it)
                    if it + 1 < nb_:
                        b_p2(it + 1)
                    b_p3b(it)
                    v_rows = b_p45(it)
            b_rows_done = v_rows
            if bstop != "full":
                C.inc(G.dma_start(out=d_out[:, 0:TT].rearrange("(c p) w -> p c w", p=128),
                                  in_=outsb[0][:, :].rearrange("p (c w) -> p c w", c=NCH)), "ST0", 16)
                C.wait("gpsimd", "ST0", C.c.get("ST0", 0))
                return
            if not do_c:
                if not do_b:
                    C.wait("gpsimd", "V", v_signs)
                    C.inc(G.dma_start(out=d_out[:, 0:TT].rearrange("(c p) w -> p c w", p=128),
                                      in_=outsb[0][:, :].rearrange("p (c w) -> p c w", c=NCH)), "ST0", 16)
                C.wait("gpsimd", "ST0", C.c.get("ST0", 0))
                C.wait("gpsimd", "ST1", C.c.get("ST1", 0))
                return

            last_ffn_T = C.c["T"]
            C.wait("sync", "T", last_ffn_T)
            C.wait("sync", "ST0", C.c.get("ST0", 0))
            C.wait("sync", "ST1", C.c.get("ST1", 0))
            l_tab = dma_in(cosT, d_cos[:, :])
            l_tab = dma_in(sinT, d_sin[:, :])
            for (d_w, nch, dst, wd) in ((d_wqkvT, NCH, wqkv, 3 * D), (d_woutT, NCH, woutw, D)):
                for i in range(nch):
                    bi = i % 2
                    r_wtmp[bi].acquire(C, "sync")
                    lv = dma_in(wtmp[bi][:, 0:wd], d_w[i * 128:(i + 1) * 128, :], f"LW{bi}")
                    C.wait("scalar", f"LW{bi}", lv)
                    wl = C.inc(S.copy(dst[:, i * wd:(i + 1) * wd], wtmp[bi][:, 0:wd]), "S")
                    r_wtmp[bi].users.append(("S", wl))
            s_wmm = wl

            C.wait("vector", "S", s_wmm)
            C.wait("vector", "L", l_tab)
            for b in range(4):
                vb3 = vblk_l[b].rearrange("p (h d) -> p h d", h=H)
                vinit = C.inc(V.memset(vb3[:, :, HD:HD + 1], 1.0), "V")

            qsb3 = [qsb_l[i].rearrange("p (c t) -> p c t", c=NCH) for i in range(2)]
            xn3 = xnsb.rearrange("p (c t) -> p c t", c=NCH)
            state = {"rot": 0, "trot": 0, "erot": 0}
            it_info = {}

            def blk_meta(gb):
                bl = gb % 16
                if bl == 0:
                    return [0, 1], maskfirst
                if bl == 15:
                    return [-1, 0], masklast
                return [-1, 0, 1], maskint

            def em_sc(it):
                gb, h = it
                deltas, mask = blk_meta(gb)
                jq, u = gb // 2, gb % 2
                ch, ro_ = h // 2, (h % 2) * 64
                sb_ = 0 + (state["rot"] % 2)
                pv_ = 2 + (state["rot"] % 2)
                state["rot"] += 1
                it_info[it] = ii = {"sb": sb_, "pv": pv_}
                rpb[sb_].acquire(C, "tensor")
                for ci, dl in enumerate(deltas):
                    kb = (gb + dl) % 4
                    r_kblk[kb].use(C, "tensor")
                    r_qsb[jq % 2].use(C, "tensor")
                    tsc = C.inc(Tn.matmul(
                        pb[sb_][:, ci * 128:(ci + 1) * 128],
                        kblk_l[kb][ro_:ro_ + 64, ch * 128:(ch + 1) * 128],
                        qsb3[jq % 2][ro_:ro_ + 64, ch, u * 128:(u + 1) * 128],
                        start=True, stop=True), "T")
                ii["tsc"] = tsc

            def em_exp(it):
                gb, h = it
                deltas, mask = blk_meta(gb)
                nck = len(deltas)
                ii = it_info[it]
                es = state["erot"] % 2
                state["erot"] += 1
                ii["es"] = es
                r_expsb[es].acquire(C, "scalar")
                C.wait("scalar", "T", ii["tsc"])
                se = C.inc(S.activation(expsb_l[es][:, 0:nck * 128], pb[ii["sb"]][:, 0:nck * 128],
                                        ACT.Exp, scale=0.125), "S")
                rpb[ii["sb"]].users.append(("S", se))
                ii["se"] = se

            def em_mask(it):
                gb, h = it
                deltas, mask = blk_meta(gb)
                nck = len(deltas)
                ii = it_info[it]
                es = ii["es"]
                C.wait("vector", "S", ii["se"])
                vm = C.inc(V.tensor_tensor(expsb_l[es][:, 0:nck * 128], expsb_l[es][:, 0:nck * 128],
                                           mask[:, 0:nck * 128], op=ALU.mult), "V")
                ii["vm"] = vm

            def em_pv(it):
                gb, h = it
                deltas, mask = blk_meta(gb)
                nck = len(deltas)
                ii = it_info[it]
                es = ii["es"]
                pv_ = ii["pv"]
                rpb[pv_].acquire(C, "tensor")
                C.wait("tensor", "V", ii["vm"])
                for ci, dl in enumerate(deltas):
                    vb = (gb + dl) % 4
                    r_vblk[vb].use(C, "tensor")
                    tpv = C.inc(Tn.matmul(pb[pv_][:, 0:HD + 1],
                                          expsb_l[es][:, ci * 128:(ci + 1) * 128],
                                          vblk_l[vb][:, h * (HD + 1):(h + 1) * (HD + 1)],
                                          start=(ci == 0), stop=(ci == nck - 1)), "T")
                    r_vblk[vb].users.append(("T", tpv))
                r_expsb[es].users.append(("T", tpv))
                ii["tpv"] = tpv

            def em_norm(it):
                gb, h = it
                ii = it_info[it]
                pv_ = ii["pv"]
                C.wait("vector", "T", ii["tpv"])
                if h == 0:
                    r_atok[gb % 2].acquire(C, "vector")
                V.reciprocal(reccol[:, :], pb[pv_][:, HD:HD + 1])
                va = C.inc(V.tensor_scalar(atok_l[gb % 2][:, h * 64:(h + 1) * 64], pb[pv_][:, 0:HD],
                                           reccol[:, :], None, op0=ALU.mult), "V")
                rpb[pv_].users.append(("V", va))
                ii["va"] = va

            def em_tr(gb, va):
                jq, u = gb // 2, gb % 2
                for ch in range(NCH):
                    tb_ = 4 + (state["trot"] % 2)
                    state["trot"] += 1
                    rpb[tb_].acquire(C, "tensor")
                    C.wait("tensor", "V", va)
                    ttv = C.inc(Tn.transpose(pbT[tb_][:, 0:128],
                                             atok_l[gb % 2][:, ch * 128:(ch + 1) * 128],
                                             ident[:, :]), "T")
                    r_atok[gb % 2].users.append(("T", ttv))
                    C.wait("scalar", "T", ttv)
                    sev = C.inc(S.copy(attnT_l[jq % 2][:, ch * TT + u * 128: ch * TT + u * 128 + 128],
                                       pbT[tb_][:, 0:128]), "S")
                    rpb[tb_].users.append(("S", sev))
                    r_attnT[jq % 2].ready.append(("S", sev))

            def attn_tile(blocks, s_):
                iters = [(s_ * 16 + bl, h) for bl in blocks for h in range(H)]
                pend = []
                em_sc(iters[0])
                em_sc(iters[1])
                for i, it in enumerate(iters):
                    em_exp(it)
                    em_mask(it)
                    if i + 2 < len(iters):
                        em_sc(iters[i + 2])
                    em_pv(it)
                    em_norm(it)
                    gb, h = it
                    if h == H - 1:
                        pend.append((i + 2, gb, it_info[it]["va"]))
                    while pend and pend[0][0] <= i:
                        _, g_, va_ = pend.pop(0)
                        em_tr(g_, va_)
                for _, g_, va_ in pend:
                    em_tr(g_, va_)

            def wout_tile(tj):
                t0 = tj * TT
                ob = outsb[tj % 2]
                ro = r_outsb[tj % 2]
                r_attnT[tj % 2].use(C, "tensor")
                evs = []
                tm_bank = {}
                for oc in (0, 2, 4, 6, 1, 3, 5, 7):
                    bk = oc // 2
                    half = (oc % 2) * 256
                    rpb[bk].acquire(C, "tensor")
                    for dc in range(NCH):
                        tm = C.inc(Tn.matmul(pb[bk][:, half:half + TT],
                                             woutw[:, dc * D + oc * 128: dc * D + oc * 128 + 128],
                                             attnT_l[tj % 2][:, dc * TT:(dc + 1) * TT],
                                             start=(dc == 0), stop=(dc == NCH - 1)), "T")
                    tm_bank[bk] = tm
                    if oc % 2 == 1:
                        C.wait("vector", "T", tm)
                        C.wait("vector", f"ST{tj % 2}", wout_tile.rb[tj])
                        ve = C.inc(V.tensor_tensor(ob[:, bk * 512:(bk + 1) * 512],
                                                   ob[:, bk * 512:(bk + 1) * 512],
                                                   pb[bk][:, 0:512], op=ALU.add), "V")
                        rpb[bk].users.append(("V", ve))
                        evs.append(ve)
                C.wait("gpsimd", "V", evs[-1])
                stv = C.inc(G.dma_start(out=d_out[:, t0:t0 + TT].rearrange("(c p) w -> p c w", p=128),
                                        in_=ob[:, :].rearrange("p (c w) -> p c w", c=NCH)), f"ST{tj % 2}", 16)
                ro.users.append((f"ST{tj % 2}", stv))
                r_attnT[tj % 2].users.append(("T", tm))
                return stv

            wout_tile.rb = {}

            def readback(tj):
                ob = outsb[tj % 2]
                ro = r_outsb[tj % 2]
                ro.acquire(C, "gpsimd")
                wout_tile.rb[tj] = C.inc(
                    G.dma_start(out=ob[:, :].rearrange("p (c w) -> p c w", c=NCH),
                                in_=d_out[:, tj * TT:tj * TT + TT].rearrange("(c p) w -> p c w", p=128)),
                    f"ST{tj % 2}", 16)

            st_final = 0
            for j in range(NTILE):
                jl = j % TPS
                t0 = j * TT
                xb = xtile[j % 2]
                rx = r_xtile[j % 2]

                rx.acquire(C, "sync")
                lxs = f"LX{j % 2}"
                lx = dma_in(
                    xb[:, :].rearrange("p (c w) -> p c w", c=NCH)[:, :, 1:257],
                    d_xf[:, t0:t0 + TT].rearrange("(c p) w -> p c w", p=128), lxs)
                if jl >= 1:
                    readback(j - 1)
                rpb[4].acquire(C, "tensor")
                C.wait("tensor", "V", b_rows_done)
                C.inc(Tn.matmul(pb[4][:, 0:128], ones1b[:, :], rstd_rows[0:1, t0:t0 + 128],
                                start=True, stop=True), "T")
                C.inc(Tn.matmul(pb[4][:, 128:256], ones1b[:, :], rstd_rows[0:1, t0 + 128:t0 + 256],
                                start=True, stop=True), "T")
                C.inc(Tn.matmul(pb[4][:, 256:384], mones1b[:, :], mr_rows[0:1, t0:t0 + 128],
                                start=True, stop=True), "T")
                tb2 = C.inc(Tn.matmul(pb[4][:, 384:512], mones1b[:, :], mr_rows[0:1, t0 + 128:t0 + 256],
                                      start=True, stop=True), "T")
                C.wait("vector", lxs, lx)
                C.wait("vector", "T", tb2)
                r_xnsb.acquire(C, "vector")
                V.tensor_tensor(xn3[:, :, :], x3[j % 2][:, :, 1:257],
                                pb[4][:, 0:256].unsqueeze(1).broadcast_to((128, NCH, 256)),
                                op=ALU.mult)
                v_xn = C.inc(V.tensor_tensor(xn3[:, :, :], xn3[:, :, :],
                                             pb[4][:, 256:512].unsqueeze(1).broadcast_to((128, NCH, 256)),
                                             op=ALU.add), "V")
                rpb[4].users.append(("V", v_xn))
                rx.users.append(("V", v_xn))

                C.wait("tensor", "V", v_xn)
                C.wait("tensor", "S", s_wmm)
                qk_info = {}

                def qk_mm(i):
                    kind, ch = ("q", "k")[i // 8], i % 8
                    qb = 6 + (i % 2)
                    woff = (0 if kind == "q" else D) + ch * 128
                    rpb[qb].acquire(C, "tensor")
                    for dc in range(NCH):
                        tm = C.inc(Tn.matmul(pb[qb][:, 0:TT],
                                             wqkv[:, dc * 3 * D + woff: dc * 3 * D + woff + 128],
                                             xnsb[:, dc * TT:(dc + 1) * TT],
                                             start=(dc == 0), stop=(dc == NCH - 1)), "T")
                    r_xnsb.users.append(("T", tm))
                    qk_info[i] = {"qb": qb, "tm": tm}

                def qk_evac(i):
                    ii = qk_info[i]
                    qr = i % 2
                    ii["qr"] = qr
                    r_qraw[qr].acquire(C, "scalar")
                    C.wait("scalar", "T", ii["tm"])
                    se = C.inc(S.copy(qraw_l[qr][:, 0:TT], pb[ii["qb"]][:, 0:TT]), "S")
                    rpb[ii["qb"]].users.append(("S", se))
                    ii["se"] = se

                def qk_swap(i):
                    ii = qk_info[i]
                    sw = 4 + (i % 2)
                    ii["sw"] = sw
                    rpb[sw].acquire(C, "tensor")
                    C.wait("tensor", "S", ii["se"])
                    ii["tsw"] = C.inc(Tn.matmul(pb[sw][:, 0:TT], pswap[:, :],
                                                qraw_l[ii["qr"]][:, 0:TT],
                                                start=True, stop=True), "T")

                def qk_rope(i):
                    ii = qk_info[i]
                    kind, ch = ("q", "k")[i // 8], i % 8
                    C.wait("vector", "T", ii["tsw"])
                    V.tensor_tensor(tr1, pb[ii["sw"]][:, 0:TT], sinT[:, t0:t0 + TT], op=ALU.mult)
                    V.tensor_tensor(tr2, qraw_l[ii["qr"]][:, 0:TT], cosT[:, t0:t0 + TT], op=ALU.mult)
                    if kind == "q":
                        if ch == 0:
                            r_qsb[j % 2].acquire(C, "vector")
                        vr = C.inc(V.tensor_tensor(qsb3[j % 2][:, ch, :], tr1, tr2, op=ALU.add), "V")
                        r_qsb[j % 2].ready.append(("V", vr))
                    else:
                        for half in range(2):
                            kb = (2 * j + half) % 4
                            if ch == 0:
                                r_kblk[kb].acquire(C, "vector")
                            vr = C.inc(V.tensor_tensor(
                                kblk_l[kb][:, ch * 128:(ch + 1) * 128],
                                tr1[:, half * 128:(half + 1) * 128],
                                tr2[:, half * 128:(half + 1) * 128], op=ALU.add), "V")
                            r_kblk[kb].ready.append(("V", vr))
                    rpb[ii["sw"]].users.append(("V", vr))
                    r_qraw[ii["qr"]].users.append(("V", vr))

                qk_mm(0)
                for i in range(16):
                    qk_evac(i)
                    if i + 1 < 16:
                        qk_mm(i + 1)
                    qk_swap(i)
                    qk_rope(i)
                for u in range(2):
                    vbi = (2 * j + u) % 4
                    r_vblk[vbi].acquire(C, "vector")
                    for hf in range(2):
                        qb = 6 + hf
                        rpb[qb].acquire(C, "tensor")
                        for dc in range(NCH):
                            tm = C.inc(Tn.matmul(
                                pb[qb][:, 0:512],
                                xn3[:, dc, u * 128:(u + 1) * 128],
                                wqkv[:, dc * 3 * D + 2 * D + hf * 512: dc * 3 * D + 2 * D + (hf + 1) * 512],
                                start=(dc == 0), stop=(dc == NCH - 1)), "T")
                        r_xnsb.users.append(("T", tm))
                        C.wait("vector", "T", tm)
                        vb3 = vblk_l[vbi].rearrange("p (h d) -> p h d", h=H)
                        vev = C.inc(V.tensor_copy(
                            vb3[:, hf * 8:(hf + 1) * 8, 0:HD],
                            pb[qb][:, 0:512].rearrange("p (h d) -> p h d", h=8)), "V")
                        rpb[qb].users.append(("V", vev))
                        r_vblk[vbi].ready.append(("V", vev))

                s_ = j // TPS
                blocks = [0] if jl == 0 else [2 * jl - 1, 2 * jl]
                if jl == TPS - 1:
                    blocks.append(15)
                attn_tile(blocks, s_)

                if jl >= 1:
                    st_final = wout_tile(j - 1)
                if jl == TPS - 1:
                    readback(j)
                    st_final = wout_tile(j)

            C.wait("gpsimd", "ST0", C.c.get("ST0", 0))
            C.wait("gpsimd", "ST1", C.c.get("ST1", 0))
    return nc


_BUILT = {}


def _get_nc():
    if "nc" not in _BUILT:
        nc = bass.Bass("TRN2", target_bir_lowering=False, debug=False)
        build_program(nc)
        _BUILT["nc"] = nc
    return _BUILT["nc"]


def _consts():
    if "c" in _BUILT:
        return _BUILT["c"]
    pos = np.arange(TSEQ, dtype=np.float64)
    inv = 1.0 / (10000.0 ** (np.arange(0, HD, 2, dtype=np.float64) / HD))
    fr = pos[:, None] * inv[None, :]
    cosv = np.cos(fr)
    sinv = np.sin(fr)
    cosT = np.zeros((128, TSEQ), np.float32)
    sinT = np.zeros((128, TSEQ), np.float32)
    for r in range(128):
        d = r % 64
        jj = d if d < 32 else d - 32
        cosT[r] = cosv[:, jj]
        sinT[r] = (-sinv[:, jj]) if d < 32 else sinv[:, jj]
    cosT = np.tile(cosT, (1, 2)).astype(bf16)
    sinT = np.tile(sinT, (1, 2)).astype(bf16)
    psw = np.zeros((128, 128), np.float32)
    for i in range(128):
        d = i % 64
        k = i + 32 if d < 32 else i - 32
        psw[k, i] = 1.0
    kk = np.arange(128)[:, None]
    qq = np.arange(128)[None, :]
    mneg = (kk > qq).astype(np.float32)
    mpos = (kk <= qq).astype(np.float32)
    one = np.ones((128, 128), np.float32)
    c = dict(
        d_cosT=cosT, d_sinT=sinT, d_pswap=psw.astype(bf16),
        d_ident=np.eye(128, dtype=np.float32).astype(bf16),
        d_maskint=np.concatenate([mneg, one, mpos], 1).astype(bf16),
        d_maskfirst=np.concatenate([one, mpos], 1).astype(bf16),
        d_masklast=np.concatenate([mneg, one], 1).astype(bf16),
        d_ones1=np.ones((1, 128), np.float32),
        d_mones1=np.full((1, 128), -1.0, np.float32),
        d_ones1b=np.ones((1, 128), bf16),
        d_mones1b=np.full((1, 128), -1.0, np.float32).astype(bf16),
        d_onescol=np.ones((128, 1), np.float32),
        d_onescolb=np.ones((128, 1), bf16),
        d_identf=np.eye(128, dtype=np.float32),
    )
    _BUILT["c"] = c
    return c


def kernel(L0_attn_x, L0_attn_w_qkv, L0_attn_w_out, L0_attn_b_out,
           L0_ffn_w1, L0_ffn_conv_w, L0_ffn_conv_b, L0_ffn_w2):
    x = np.ascontiguousarray(np.asarray(L0_attn_x, np.float32))
    wqkv = np.asarray(L0_attn_w_qkv, np.float32)
    wout = np.asarray(L0_attn_w_out, np.float32)
    bout = np.asarray(L0_attn_b_out, np.float32)
    w1 = np.asarray(L0_ffn_w1, np.float32)
    cw = np.asarray(L0_ffn_conv_w, np.float32)
    cb = np.asarray(L0_ffn_conv_b, np.float32)
    w2 = np.asarray(L0_ffn_w2, np.float32)

    shared = dict(
        wqkvT=np.ascontiguousarray(wqkv.T),
        woutT=np.ascontiguousarray(wout.T),
        d_boutl=np.ascontiguousarray(bout.reshape(NCH, 128).T),
        w1T=np.ascontiguousarray(w1.T),
        w2T=np.ascontiguousarray(w2.T),
        d_convw=np.ascontiguousarray(cw[:, 0, :].reshape(HCH, 128, 3).transpose(1, 2, 0).reshape(128, 96)),
        d_convb=np.ascontiguousarray(cb.reshape(HCH, 128).T),
        **_consts(),
    )
    nc = _get_nc()
    in_maps = []
    for ci in range(8):
        xc = x[2 * ci:2 * ci + 2]
        xf = np.concatenate([xc[0].T, xc[1].T], axis=1)
        m = dict(shared)
        m["xf"] = np.ascontiguousarray(xf)
        in_maps.append(m)
    res = run_bass_kernel_spmd(nc, in_maps, core_ids=list(range(8)))
    out = np.empty((16, TSEQ, D), np.float32)
    for ci in range(8):
        of = res.results[ci]["outf"]
        out[2 * ci] = of[:, :TSEQ].T
        out[2 * ci + 1] = of[:, TSEQ:].T
    return out
